# version 07
# baseline (speedup 1.0000x reference)
import numpy as np
import ml_dtypes
from contextlib import ExitStack

import concourse.bass as bass
import concourse.mybir as mybir
from concourse.ap import AP
from concourse.bass_utils import run_bass_kernel_spmd

F32 = mybir.dt.float32
BF16 = mybir.dt.bfloat16
AF = mybir.ActivationFunctionType
ALU = mybir.AluOpType

P = 128
L = 2048
CH = 512
NCH = L // CH
NET = 16
NCT = 8
NST = 16
DTR = 64
DM = 1024
DI = 2048
NG = 4
GN = 4

BF = ml_dtypes.bfloat16


class Prog:

    ENGINES = ("sync", "vector", "scalar", "tensor", "gpsimd")

    def __init__(self):
        self.items = {e: [] for e in self.ENGINES}
        self.counts = {}

    def emit(self, eng, fn, waits=(), incs=()):
        waits = [(s, v) for (s, v) in waits if v > 0]
        self.items[eng].append((fn, waits, list(incs)))
        post = None
        for (sem, amt) in incs:
            self.counts[id(sem)] = self.counts.get(id(sem), 0) + amt
            post = self.counts[id(sem)]
        return post

    def count(self, sem):
        return self.counts.get(id(sem), 0)

    def replay(self, eng, handle):
        for fn, waits, incs in self.items[eng]:
            for sem, val in waits:
                handle.wait_ge(sem, val)
            inst = fn()
            for sem, amt in incs:
                inst = inst.then_inc(sem, amt)


def build_program():
    nc = bass.Bass("TRN2", target_bir_lowering=False)

    xT = nc.dram_tensor("xT", [DM, L], F32, kind="ExternalInput").ap()
    inprojT = nc.dram_tensor("inprojT", [DM, 2 * DI], F32, kind="ExternalInput").ap()
    outprojT = nc.dram_tensor("outprojT", [DI, DM], F32, kind="ExternalInput").ap()
    xprojT = nc.dram_tensor("xprojT", [DI, 96], F32, kind="ExternalInput").ap()
    dt_wT = nc.dram_tensor("dt_wT", [DTR, DI], F32, kind="ExternalInput").ap()
    w1T = nc.dram_tensor("w1T", [DM, DM], F32, kind="ExternalInput").ap()
    wa1T = nc.dram_tensor("wa1T", [DM, DM], F32, kind="ExternalInput").ap()
    wa2 = nc.dram_tensor("wa2", [DM, DM], F32, kind="ExternalInput").ap()
    convw_r = nc.dram_tensor("convw_r", [P, 4 * NET], F32, kind="ExternalInput").ap()
    convb_r = nc.dram_tensor("convb_r", [P, NET], F32, kind="ExternalInput").ap()
    dtb_r = nc.dram_tensor("dtb_r", [P, NET], F32, kind="ExternalInput").ap()
    alog_r = nc.dram_tensor("alog_r", [P, NST * NET], F32, kind="ExternalInput").ap()
    d_r = nc.dram_tensor("d_r", [P, NET], F32, kind="ExternalInput").ap()
    fbias_r = nc.dram_tensor("fbias_r", [P, NCT], F32, kind="ExternalInput").ap()
    eye_in = nc.dram_tensor("eye_in", [P, P], BF16, kind="ExternalInput").ap()
    ones_in = nc.dram_tensor("ones_in", [P, 1], F32, kind="ExternalInput").ap()

    outT = nc.dram_tensor("outT", [DM, L], F32, kind="ExternalOutput").ap()

    scr_inproj_h = nc.dram_tensor("scr_inproj", [32, P, NCT * P], BF16, kind="Internal")
    scr_outproj_h = nc.dram_tensor("scr_outproj", [NCT, P, NET * P], BF16, kind="Internal")
    scr_inproj = scr_inproj_h.ap()
    scr_outproj = scr_outproj_h.ap()
    scr_wp = nc.dram_tensor("scr_wp", [NCT, P, NCT * P], BF16, kind="Internal").ap()
    scr_bc = nc.dram_tensor("scr_bc", [32, CH], BF16, kind="Internal").ap()
    scr_stat = nc.dram_tensor("scr_stat", [2, CH], BF16, kind="Internal").ap()
    scr_wa2 = nc.dram_tensor("scr_wa2", [NCT, P, DM], BF16, kind="Internal").ap()

    prog = Prog()

    with ExitStack() as ctx:
        sbt = lambda name, shape, dt: ctx.enter_context(nc.sbuf_tensor(name, shape, dt))
        psb = lambda name, shape, dt: ctx.enter_context(nc.psum_tensor(name, shape, dt))
        sem = lambda name: ctx.enter_context(nc.semaphore(name))

        x_ch = sbt("x_ch", [P, 2 * NCT * CH], F32)
        xn_bf = sbt("xn_bf", [P, NCT * CH], BF16)
        xcry = sbt("xcry", [P, NET * 520], BF16)
        xc_bf = sbt("xc_bf", [P, NET * CH], BF16)
        z_all = sbt("z_all", [P, NET * CH], BF16)
        bcb = sbt("bcb", [P, 32 * CH], BF16)
        wstream = sbt("wstream", [P, 48 * P], BF16)
        u_roll = sbt("u_roll", [P, GN * CH], BF16)
        hs_roll = sbt("hs_roll", [P, 2 * GN * CH], BF16)
        tmp_roll = sbt("tmp_roll", [P, 2 * GN * CH], BF16)
        delta2 = sbt("delta2", [P, 2 * CH], F32)
        v2 = sbt("v2", [P, 2 * CH], BF16)
        dA = sbt("dA", [P, 2 * CH], F32)
        dxc = sbt("dxc", [P, 2 * CH], BF16)
        stage = sbt("stage", [P, 2 * CH], F32)
        spst = sbt("spst", [P, CH], F32)
        outstg = sbt("outstg", [P, 2 * CH], F32)
        states = sbt("states", [P, NET * NST], F32)
        aB = sbt("aB", [P, CH], BF16)
        bB = sbt("bB", [P, CH], BF16)
        diag = sbt("diag", [P, 2 * 4 * P], BF16)
        drawT = sbt("drawT", [DTR, 2 * CH], BF16)
        bc_sb = sbt("bc_sb", [96, CH], BF16)
        halo = sbt("halo", [P, NET * 4], BF16)
        negA = sbt("negA", [P, NST * NET], F32)
        convw = sbt("convw", [P, 4 * NET], F32)
        convb = sbt("convb", [P, NET], F32)
        dtb = sbt("dtb", [P, NET], F32)
        dparm = sbt("dparm", [P, NET], F32)
        fbias = sbt("fbias", [P, NCT], F32)
        eye_bf = sbt("eye_bf", [P, P], BF16)
        ones = sbt("ones", [P, 1], F32)
        xprojT_bf = sbt("xprojT_bf", [P, NET * 96], BF16)
        dt_wT_bf = sbt("dt_wT_bf", [DTR, DI], BF16)
        r_sx = sbt("r_sx", [1, CH], F32)
        r_sx2 = sbt("r_sx2", [1, CH], F32)
        r_a = sbt("r_a", [1, CH], BF16)
        r_b = sbt("r_b", [1, CH], BF16)
        r_t = sbt("r_t", [1, CH], F32)

        IN_A, IN_B, OPJ, FF_A, FF_B = 0, 8 * P, 16 * P, 32 * P, 40 * P

        mm_ps = psb("mm_ps", [P, 2 * CH], F32)
        conv_ps = psb("conv_ps", [P, 2 * CH], F32)
        y_ps = psb("y_ps", [P, 2 * CH], F32)
        st_ps = psb("st_ps", [P, NET * NST], F32)
        stat_ps = psb("stat_ps", [1, CH], F32)

        sL = sem("sL")
        sG = sem("sG")
        sR = sem("sR")
        sWA = sem("sWA")
        sWB = sem("sWB")
        sWO = sem("sWO")
        sFA = sem("sFA")
        sFB = sem("sFB")
        sV = sem("sV")
        sS = sem("sS")
        sT = sem("sT")
        sO = sem("sO")

        T = nc.tensor
        V = nc.vector
        S = nc.scalar

        def sy_dma(dst, src, waits=()):
            return prog.emit("sync", lambda d=dst, s=src: nc.sync.dma_start(out=d, in_=s),
                             waits=waits, incs=[(sL, 16)])

        def g_dma(dst, src, waits=()):
            return prog.emit("gpsimd", lambda d=dst, s=src: nc.gpsimd.dma_start(out=d, in_=s),
                             waits=waits, incs=[(sG, 16)])

        def v_op(fn, waits=()):
            return prog.emit("vector", fn, waits=waits, incs=[(sV, 1)])

        def s_op(fn, waits=()):
            return prog.emit("scalar", fn, waits=waits, incs=[(sS, 1)])

        def t_op(fn, waits=()):
            return prog.emit("tensor", fn, waits=waits, incs=[(sT, 1)])

        SEMS = {"S": sS, "V": sV, "T": sT}

        def w(tag):
            eng, cnt = tag
            return (SEMS[eng], cnt)

        sy_dma(eye_bf[:, :], eye_in)
        sy_dma(ones[:, :], ones_in)
        sy_dma(convw[:, :], convw_r)
        sy_dma(convb[:, :], convb_r)
        sy_dma(dtb[:, :], dtb_r)
        sy_dma(dparm[:, :], d_r)
        sy_dma(fbias[:, :], fbias_r)
        sy_dma(x_ch[:, 0:NST * NET], alog_r)
        sy_dma(x_ch[0:DTR, 256:256 + DI], dt_wT)
        w_xp = 0
        for kt in range(NET):
            w_xp = sy_dma(x_ch[:, 2304 + kt * 96:2304 + (kt + 1) * 96],
                          xprojT[kt * P:(kt + 1) * P, :])

        e1 = s_op(lambda: S.activation(stage[:, 0:NST * NET], x_ch[:, 0:NST * NET], AF.Exp),
                  waits=[(sL, w_xp)])
        nega_done = v_op(lambda: V.tensor_scalar(negA[:, :], stage[:, 0:NST * NET],
                                                 -1.0, None, ALU.mult),
                         waits=[(sS, e1)])
        dtw_cast = v_op(lambda: V.tensor_copy(dt_wT_bf[:, :], x_ch[0:DTR, 256:256 + DI]),
                        waits=[(sL, w_xp)])
        xproj_cast = v_op(lambda: V.tensor_copy(xprojT_bf[:, :],
                                                x_ch[:, 2304:2304 + NET * 96]),
                          waits=[(sL, w_xp)])
        v_op(lambda: V.memset(halo[:, :], 0.0))

        last_w_inproj = 0
        for kt in range(NCT):
            wts = [(sL, last_w_inproj)] if kt else [(sS, e1), (sV, dtw_cast),
                                                    (sV, xproj_cast)]
            wld = sy_dma(x_ch[:, 0:2 * DI], inprojT[kt * P:(kt + 1) * P, :], waits=wts)
            cst = v_op(lambda: V.tensor_copy(xcry[:, 0:2 * DI], x_ch[:, 0:2 * DI]),
                       waits=[(sL, wld)])
            for q in range(4):
                dst = AP(scr_inproj_h, q * 8 * P * NCT * P + kt * P,
                         [[NCT * P, P], [P * NCT * P, 8], [1, P]])
                src = AP(xcry, q * 8 * P, [[NET * 520, P], [P, 8], [1, P]])
                last_w_inproj = sy_dma(dst, src, waits=[(sV, cst)] if q == 0 else ())

        last_w_outproj = 0
        for kt in range(NET):
            wld = sy_dma(x_ch[:, 0:DM], outprojT[kt * P:(kt + 1) * P, :],
                         waits=[(sL, last_w_inproj if kt == 0 else last_w_outproj)])
            cst = v_op(lambda: V.tensor_copy(xcry[:, 0:DM], x_ch[:, 0:DM]),
                       waits=[(sL, wld)])
            for q in range(4):
                dst = AP(scr_outproj_h, q * 2 * P * NET * P + kt * P,
                         [[NET * P, P], [P * NET * P, 2], [1, P]])
                src = AP(xcry, q * 2 * P, [[NET * 520, P], [P, 2], [1, P]])
                last_w_outproj = sy_dma(dst, src, waits=[(sV, cst)] if q == 0 else ())

        lastv = prog.count(sV)
        for i in range(NCT):
            wld = sy_dma(x_ch[:, 0:DM], wa1T[i * P:(i + 1) * P, :],
                         waits=[(sV, lastv), (sL, last_w_outproj if i == 0 else 0)])
            lastv = v_op(lambda i=i: V.tensor_copy(xc_bf[:, i * DM:(i + 1) * DM],
                                                   x_ch[:, 0:DM]),
                         waits=[(sL, wld)])
        last_wa2_w = 0
        for i in range(NCT):
            wld = sy_dma(x_ch[:, 0:DM], wa2[i * P:(i + 1) * P, :], waits=[(sV, lastv)])
            lastv = v_op(lambda i=i: V.tensor_copy(z_all[:, (i % 2) * CH:(i % 2) * CH + DM // 2],
                                                   x_ch[:, 0:DM // 2]),
                         waits=[(sL, wld)])
            lastv = v_op(lambda i=i: V.tensor_copy(z_all[:, (2 + i % 2) * CH:(2 + i % 2) * CH + DM // 2],
                                                   x_ch[:, DM // 2:DM]))
            last_wa2_w = sy_dma(scr_wa2[i, :, 0:DM // 2], z_all[:, (i % 2) * CH:(i % 2) * CH + DM // 2],
                                waits=[(sV, lastv)])
            last_wa2_w = sy_dma(scr_wa2[i, :, DM // 2:DM], z_all[:, (2 + i % 2) * CH:(2 + i % 2) * CH + DM // 2])
        for i in range(NCT):
            wld = sy_dma(x_ch[:, 0:DM], w1T[i * P:(i + 1) * P, :], waits=[(sV, lastv)])
            lastv = v_op(lambda i=i: V.tensor_copy(xcry[:, i * DM:(i + 1) * DM],
                                                   x_ch[:, 0:DM]),
                         waits=[(sL, wld)])
        cast_all = lastv

        wa2_ld = {}
        for kt in range(NCT):
            wa2_ld[kt] = g_dma(bcb[:, kt * DM:(kt + 1) * DM], scr_wa2[kt, :, :],
                               waits=[(sL, last_wa2_w)] if kt == 0 else ())
        T2B = 8 * DM
        for (lhs_buf, rhs_buf, rhs_base, out_buf, out_base, first_wait) in (
                (xc_bf, bcb, 0, bcb, T2B, cast_all),
                (xcry, bcb, T2B, xc_bf, 0, None)):
            for mt in range(NCT):
                for half in range(2):
                    mmv = 0
                    for kt in range(NCT):
                        if kt == 0:
                            if first_wait is not None and mt == 0 and half == 0:
                                wts = [(sV, first_wait), (sG, wa2_ld[NCT - 1])]
                            else:
                                wts = [(sV, prog.count(sV))]
                        else:
                            wts = []
                        mmv = t_op(lambda lb=lhs_buf, rb=rhs_buf, rb0=rhs_base, mt=mt, half=half, kt=kt:
                                   T.matmul(
                                       mm_ps[:, (mt % 2) * CH:(mt % 2) * CH + CH],
                                       lb[:, kt * DM + mt * P:kt * DM + (mt + 1) * P],
                                       rb[:, rb0 + kt * DM + half * CH:rb0 + kt * DM + half * CH + CH],
                                       start=(kt == 0), stop=(kt == NCT - 1)), waits=wts)
                    v_op(lambda ob=out_buf, ob0=out_base, mt=mt, half=half: V.tensor_copy(
                        ob[:, ob0 + mt * DM + half * CH:ob0 + mt * DM + half * CH + CH],
                        mm_ps[:, (mt % 2) * CH:(mt % 2) * CH + CH]),
                        waits=[(sT, mmv)])
        wp_evac_all = prog.count(sV)
        last_wp_w = 0
        for ct in range(NCT):
            src = AP(xc_bf, ct * P, [[NET * CH, P], [DM, NCT], [1, P]])
            last_wp_w = sy_dma(scr_wp[ct, :, :], src,
                               waits=[(sV, wp_evac_all)] if ct == 0 else ())
        prep_mm_last = prog.count(sT)
        PREP_SL = prog.count(sL)

        scan_hist = {}
        stmm_hist = {}
        foldl_hist = {}
        gate_hist = {}
        f17_hist = {}
        v2_hist = {}
        dxc_hist = {}
        silu_hist = {}
        zdone_hist = {}
        convgrp = {}
        rep_done = [0] * NCH
        bcw_k = [0] * NCH
        xload_k = [0] * NCH
        statrep_k = [0] * NCH
        xn_done_k = [0] * NCH
        draw_k = [0] * NCH
        dtmm_last_k = [0] * NCH
        cp_sx2_k = [0] * NCH
        sx2mm_last_k = [0] * NCH
        tmp_last = [0] * NCH
        mmgrp_k = [dict() for _ in range(NCH)]
        ffnmm_k = [dict() for _ in range(NCH)]
        ffn_add_k = [dict() for _ in range(NCH)]
        ffn_th_k = [dict() for _ in range(NCH)]
        opmm_k = [dict() for _ in range(NCH)]
        outstage_k = [dict() for _ in range(NCH)]
        st_vals = {}
        states_copy_k = [0] * NCH

        mmfree = {0: ("T", 0), 1: ("T", 0)}

        def front_steps(k):
            steps = []
            min_its = []
            stepsZ = []
            xb = (k % 2) * NCT * CH

            def s_xload():
                xl = 0
                for ct in range(NCT):
                    if ct == 0:
                        if k >= 2:
                            xw = [(sV, outstage_k[k - 2].get(NCT - 1, 0))]
                        else:
                            xw = [(sV, cast_all)] if k == 0 else []
                    else:
                        xw = []
                    xl = sy_dma(x_ch[:, xb + ct * CH:xb + (ct + 1) * CH],
                                xT[ct * P:(ct + 1) * P, k * CH:k * CH + CH],
                                waits=xw)
                xload_k[k] = xl
            steps.append(s_xload); min_its.append(0)

            def s_lnsum():
                sxmm = 0
                for ct in range(NCT):
                    wts = []
                    if ct == 0:
                        wts = [(sL, xload_k[k])]
                        if k >= 1:
                            wts += [(sV, cp_sx2_k[k - 1])]
                    sxmm = t_op(lambda ct=ct: T.matmul(
                        stat_ps[0:1, :], ones[:, :],
                        x_ch[:, xb + ct * CH:xb + (ct + 1) * CH],
                        start=(ct == 0), stop=(ct == NCT - 1)), waits=wts)
                v_op(lambda: V.tensor_copy(r_sx[:, :], stat_ps[0:1, :]),
                     waits=[(sT, sxmm)])
            steps.append(s_lnsum); min_its.append(0)

            def s_lnsq():
                cp_sx = prog.count(sV)
                sx2mm = 0
                for ct in range(NCT):
                    wts = []
                    if ct == 0:
                        wts = [(sL, xload_k[k])]
                        if k >= 1:
                            wts += [(sV, ffn_add_k[k - 1].get(NCT - 1, 0)),
                                    (sT, sx2mm_last_k[k - 1])]
                    elif ct >= 2:
                        wts = [(sT, sx2mm)]
                    sq = s_op(lambda ct=ct: S.activation(
                        stage[:, (ct % 2) * CH:(ct % 2) * CH + CH],
                        x_ch[:, xb + ct * CH:xb + (ct + 1) * CH], AF.Square),
                        waits=wts)
                    sx2mm = t_op(lambda ct=ct: T.matmul(
                        stat_ps[0:1, :], ones[:, :],
                        stage[:, (ct % 2) * CH:(ct % 2) * CH + CH],
                        start=(ct == 0), stop=(ct == NCT - 1)),
                        waits=[(sS, sq)] + ([(sV, cp_sx)] if ct == 0 else []))
                sx2mm_last_k[k] = sx2mm
                cp_sx2_k[k] = v_op(lambda: V.tensor_copy(r_sx2[:, :], stat_ps[0:1, :]),
                                   waits=[(sT, sx2mm)])
            steps.append(s_lnsq); min_its.append(1)

            def s_lnfin():
                v_op(lambda: V.tensor_scalar(r_sx[:, :], r_sx[:, :], 1.0 / DM, None, ALU.mult))
                v_op(lambda: V.tensor_scalar(r_sx2[:, :], r_sx2[:, :], 1.0 / DM, None, ALU.mult))
                v_op(lambda: V.scalar_tensor_tensor(r_t[:, :], r_sx[:, :], -1.0,
                                                    r_sx[:, :], ALU.mult, ALU.mult))
                v_op(lambda: V.tensor_tensor(r_sx2[:, :], r_sx2[:, :], r_t[:, :], ALU.add))
                vvar = v_op(lambda: V.tensor_scalar(r_sx2[:, :], r_sx2[:, :], 1e-5, None, ALU.add))
                s_op(lambda: S.activation(r_t[:, :], r_sx2[:, :], AF.Ln),
                     waits=[(sV, vvar)])
                rstd = s_op(lambda: S.activation(r_a[:, :], r_t[:, :], AF.Exp, scale=-0.5))
                rb = v_op(lambda: V.scalar_tensor_tensor(r_b[:, :], r_sx[:, :], -1.0,
                                                         r_a[:, :], ALU.mult, ALU.mult),
                          waits=[(sS, rstd)])
                wts = [(sV, rb)]
                if k >= 1:
                    wts += [(sL, statrep_k[k - 1]), (sV, xn_done_k[k - 1])]
                wst = sy_dma(scr_stat[0:1, :], r_a[:, :], waits=wts)
                wst = sy_dma(scr_stat[1:2, :], r_b[:, :])
                sy_dma(aB[:, :], scr_stat[0:1, :].broadcast_to([P, CH]), waits=[(sL, wst)])
                statrep_k[k] = sy_dma(bB[:, :], scr_stat[1:2, :].broadcast_to([P, CH]))
            steps.append(s_lnfin); min_its.append(1)

            def s_xn():
                xnd = 0
                for ct in range(NCT):
                    wts = []
                    if ct == 0:
                        wts = [(sL, statrep_k[k]), (sO, prog.count(sO))]
                        if k >= 1:
                            wts += [(sT, max(mmgrp_k[k - 1].get(31, 0),
                                             ffnmm_k[k - 1].get(NCT - 1, 0)))]
                    v_op(lambda ct=ct: V.tensor_tensor(
                        outstg[:, (ct % 2) * CH:(ct % 2) * CH + CH],
                        x_ch[:, xb + ct * CH:xb + (ct + 1) * CH], aB[:, :], ALU.mult),
                        waits=wts)
                    xnd = v_op(lambda ct=ct: V.tensor_tensor(
                        xn_bf[:, ct * CH:(ct + 1) * CH],
                        outstg[:, (ct % 2) * CH:(ct % 2) * CH + CH], bB[:, :], ALU.add))
                xn_done_k[k] = xnd
            steps.append(s_xn); min_its.append(2)

            for ct in range(NCT):
                def s_ffn(ct=ct):
                    slot = FF_A if ct % 2 == 0 else FF_B
                    wts = []
                    if ct >= 2:
                        wts = [(sT, ffnmm_k[k][ct - 2])]
                    elif k >= 1:
                        wts = [(sT, ffnmm_k[k - 1].get(NCT - 2 + ct, 0))]
                    if k == 0 and ct < 2:
                        wts += [(sL, last_wp_w)]
                    fsem = sFA if ct % 2 == 0 else sFB
                    wload = prog.emit("gpsimd",
                                      lambda slot=slot, ct=ct: nc.gpsimd.dma_start(
                                          out=wstream[:, slot:slot + NCT * P],
                                          in_=scr_wp[ct, :, :]),
                                      waits=wts, incs=[(fsem, 16)])
                    mmv = 0
                    for kt in range(NCT):
                        wts = []
                        if kt == 0:
                            wts = [(fsem, wload), (sV, xn_done_k[k]), w(mmfree[ct % 2])]
                        mmv = t_op(lambda ct=ct, kt=kt, slot=slot: T.matmul(
                            mm_ps[:, (ct % 2) * CH:(ct % 2) * CH + CH],
                            wstream[:, slot + kt * P:slot + (kt + 1) * P],
                            xn_bf[:, kt * CH:(kt + 1) * CH],
                            start=(kt == 0), stop=(kt == NCT - 1)), waits=wts)
                    ffnmm_k[k][ct] = mmv
                    prev_add = (ffn_add_k[k][ct - 2] if ct >= 2 else
                                (ffn_add_k[k - 1].get(NCT - 2 + ct, 0) if k >= 1 else 0))
                    th = s_op(lambda ct=ct: S.activation(
                        stage[:, (ct % 2) * CH:(ct % 2) * CH + CH],
                        mm_ps[:, (ct % 2) * CH:(ct % 2) * CH + CH],
                        AF.Tanh, bias=fbias[:, ct:ct + 1]),
                        waits=[(sT, mmv), (sV, prev_add)])
                    mmfree[ct % 2] = ("S", th)
                    wts = [(sS, th)]
                    if ct == 0:
                        wts += [(sL, xload_k[k])]
                    ffn_add_k[k][ct] = prog.emit("gpsimd", lambda ct=ct:
                        nc.gpsimd.tensor_tensor(
                            out=x_ch[:, xb + ct * CH:xb + (ct + 1) * CH],
                            in0=x_ch[:, xb + ct * CH:xb + (ct + 1) * CH],
                            in1=stage[:, (ct % 2) * CH:(ct % 2) * CH + CH],
                            op=ALU.add),
                        waits=wts, incs=[(sV, 1)])
                steps.append(s_ffn); min_its.append(2)

            for mt in list(range(NET)) + list(range(NET, 32)):
                def s_inproj(mt=mt):
                    slot = IN_A if mt % 2 == 0 else IN_B
                    wts = []
                    if mt >= 2:
                        wts = [(sT, mmgrp_k[k][mt - 2])]
                    elif k >= 1:
                        wts = [(sT, mmgrp_k[k - 1].get(30 + mt, 0))]
                    if k == 0 and mt < 2:
                        wts += [(sL, PREP_SL)]
                    wsem = sWA if mt % 2 == 0 else sWB
                    wload = prog.emit("gpsimd",
                                      lambda slot=slot, mt=mt: nc.gpsimd.dma_start(
                                          out=wstream[:, slot:slot + NCT * P],
                                          in_=scr_inproj[mt, :, :]),
                                      waits=wts, incs=[(wsem, 16)])
                    mmv = 0
                    for kt in range(NCT):
                        wts = []
                        if kt == 0:
                            wts = [(wsem, wload), (sV, xn_done_k[k]), w(mmfree[mt % 2])]
                        mmv = t_op(lambda mt=mt, kt=kt, slot=slot: T.matmul(
                            mm_ps[:, (mt % 2) * CH:(mt % 2) * CH + CH],
                            wstream[:, slot + kt * P:slot + (kt + 1) * P],
                            xn_bf[:, kt * CH:(kt + 1) * CH],
                            start=(kt == 0), stop=(kt == NCT - 1)), waits=wts)
                    mmgrp_k[k][mt] = mmv
                    if mt < NET:
                        et = mt
                        etg = k * NET + et
                        base = et * 520
                        s_op(lambda et=et, base=base: S.activation(
                            xcry[:, base + 4:base + 8], halo[:, et * 4:(et + 1) * 4],
                            AF.Copy))
                        s_op(lambda base=base, mt=mt: S.activation(
                            xcry[:, base + 8:base + 520],
                            mm_ps[:, (mt % 2) * CH:(mt % 2) * CH + CH], AF.Copy),
                            waits=[(sT, mmv)])
                        cp3 = s_op(lambda et=et, mt=mt: S.activation(
                            halo[:, et * 4:(et + 1) * 4],
                            mm_ps[:, (mt % 2) * CH + 508:(mt % 2) * CH + 512], AF.Copy))
                        mmfree[mt % 2] = ("S", cp3)
                        dgv = 0
                        for kk in range(4):
                            dgv = s_op(lambda et=et, kk=kk: S.activation(
                                diag[:, (et % 2) * 4 * P + kk * P:(et % 2) * 4 * P + (kk + 1) * P],
                                eye_bf[:, :], AF.Copy,
                                scale=convw[:, et * 4 + kk:et * 4 + kk + 1]),
                                waits=([(sT, convgrp.get(etg - 2, 0))] if kk == 0 else []))
                        cnv = 0
                        for kk in range(4):
                            wts = []
                            if kk == 0:
                                wts = [(sS, dgv), (sS, silu_hist.get(etg - 2, 0))]
                                if et == 0 and k >= 1:
                                    wts += [(sS, draw_k[k - 1])]
                            cnv = t_op(lambda et=et, kk=kk, base=base: T.matmul(
                                conv_ps[:, (et % 2) * CH:(et % 2) * CH + CH],
                                diag[:, (et % 2) * 4 * P + kk * P:(et % 2) * 4 * P + (kk + 1) * P],
                                xcry[:, base + 5 + kk:base + 5 + kk + CH],
                                start=(kk == 0), stop=(kk == 3)), waits=wts)
                        convgrp[etg] = cnv
                        silu_hist[etg] = s_op(lambda et=et: S.activation(
                            xc_bf[:, et * CH:(et + 1) * CH],
                            conv_ps[:, (et % 2) * CH:(et % 2) * CH + CH],
                            AF.Silu, bias=convb[:, et:et + 1]),
                            waits=[(sT, cnv), (sV, dxc_hist.get(etg - NET, 0))])
                    else:
                        et = mt - NET
                        etg = k * NET + et
                        zdone_hist[etg] = s_op(lambda et=et, mt=mt: S.activation(
                            z_all[:, et * CH:(et + 1) * CH],
                            mm_ps[:, (mt % 2) * CH:(mt % 2) * CH + CH],
                            AF.Silu),
                            waits=[(sT, mmv)])
                        mmfree[mt % 2] = ("S", zdone_hist[etg])
                if mt < NET:
                    steps.append(s_inproj); min_its.append(mt)
                else:
                    stepsZ.append(s_inproj)

            def s_xproj():
                xpv = 0
                for kt in range(NET):
                    wts = [(sS, silu_hist[k * NET + kt])]
                    if kt == 0:
                        wts += [(sS, silu_hist[k * NET + NET - 1])]
                    xpv = t_op(lambda kt=kt: T.matmul(
                        conv_ps[0:96, 0:CH],
                        xprojT_bf[:, kt * 96:(kt + 1) * 96],
                        xc_bf[:, kt * CH:(kt + 1) * CH],
                        start=(kt == 0), stop=(kt == NET - 1)), waits=wts)
                wts = [(sT, xpv)]
                if k >= 2:
                    wts += [(sT, dtmm_last_k[k - 2])]
                draw_k[k] = s_op(lambda xs=k % 2: S.activation(
                    drawT[:, xs * CH:xs * CH + CH], conv_ps[0:DTR, 0:CH], AF.Copy),
                    waits=wts)
                bcd = v_op(lambda: V.tensor_copy(bc_sb[DTR:96, :], conv_ps[DTR:96, 0:CH]),
                           waits=[(sT, xpv)])
                wts = [(sV, bcd)]
                if k >= 1:
                    wts += [(sR, rep_done[k - 1])]
                bcw_k[k] = sy_dma(scr_bc[:, :], bc_sb[DTR:96, :], waits=wts)
            steps.append(s_xproj); min_its.append(15)
            return list(zip(min_its, steps)), stepsZ

        def emit_reps(k):
            for n in range(NST):
                wts = []
                if n == 0:
                    wts = [(sL, bcw_k[k])]
                    if k >= 1:
                        wts += [(sV, tmp_last[k - 1])]
                    else:
                        wts += [(sT, prep_mm_last)]
                prog.emit("sync",
                          lambda n=n: nc.sync.dma_start(
                              out=bcb[:, n * CH:(n + 1) * CH],
                              in_=scr_bc[n:n + 1, :].broadcast_to([P, CH])),
                          waits=wts, incs=[(sR, 16)])
                prog.emit("gpsimd",
                          lambda n=n: nc.gpsimd.dma_start(
                              out=bcb[:, (16 + n) * CH:(17 + n) * CH],
                              in_=scr_bc[16 + n:17 + n, :].broadcast_to([P, CH])),
                          waits=wts, incs=[(sR, 16)])
            rep_done[k] = prog.count(sR)

        expd_hist = {}

        def emit_dtchain(k, et):
            etg = k * NET + et
            par = et % 2
            xs = k % 2
            dtmm = t_op(lambda et=et, par=par, xs=xs: T.matmul(
                mm_ps[:, par * CH:par * CH + CH],
                dt_wT_bf[:, et * P:(et + 1) * P],
                drawT[:, xs * CH:xs * CH + CH], start=True, stop=True),
                waits=[w(mmfree[par]), (sS, draw_k[k])])
            dtmm_last_k[k] = dtmm
            s_op(lambda par=par, et=et: S.activation(
                spst[:, :],
                mm_ps[:, par * CH:par * CH + CH],
                AF.Exp, bias=dtb[:, et:et + 1]),
                waits=[(sT, dtmm)])
            sp = s_op(lambda par=par: S.activation(
                delta2[:, par * CH:par * CH + CH],
                spst[:, :],
                AF.Ln, bias=1.0),
                waits=[(sV, v2_hist.get(etg - 2, 0))])
            mmfree[par] = ("S", sp)
            v2d = v_op(lambda par=par, et=et: V.tensor_tensor(
                v2[:, par * CH:par * CH + CH],
                delta2[:, par * CH:par * CH + CH],
                xc_bf[:, et * CH:(et + 1) * CH], ALU.mult),
                waits=[(sS, sp)])
            v2_hist[etg] = v2d

        def emit_scan_et(k, et):
            etg = k * NET + et
            par = et % 2

            for g in range(NG):
                gg = etg * NG + g
                gp = gg % 2
                uw = []
                if et == 0 and g == 0:
                    uw = [(sR, rep_done[k])]
                v_op(lambda par=par, g=g: V.tensor_tensor(
                    out=AP(u_roll, 0, [[GN * CH, P], [CH, GN], [1, CH]]),
                    in0=AP(v2, par * CH, [[2 * CH, P], [0, GN], [1, CH]]),
                    in1=AP(bcb, g * GN * CH, [[32 * CH, P], [CH, GN], [1, CH]]),
                    op=ALU.mult), waits=uw)
                for n4 in range(GN):
                    n = g * GN + n4
                    i = et * NST + n
                    gi = k * 256 + i
                    ew = [(sV, scan_hist.get(gi - 2, 0))]
                    if gi == 0:
                        ew += [(sV, nega_done)]
                    expd_hist[gi] = s_op(lambda par=par, i=i: S.activation(
                        dA[:, (i % 2) * CH:(i % 2) * CH + CH],
                        delta2[:, par * CH:par * CH + CH],
                        AF.Exp, scale=negA[:, i:i + 1]), waits=ew)
                    init = 0.0 if k == 0 else states[:, i:i + 1]
                    sw = [(sS, expd_hist[gi])]
                    if n4 == 0:
                        sw += [(sT, stmm_hist.get(gg - 2, 0))]
                    scan_hist[gi] = v_op(lambda gp=gp, n4=n4, i=i, init=init:
                                         V.tensor_tensor_scan(
                        hs_roll[:, gp * GN * CH + n4 * CH:gp * GN * CH + (n4 + 1) * CH],
                        dA[:, (i % 2) * CH:(i % 2) * CH + CH],
                        u_roll[:, n4 * CH:(n4 + 1) * CH],
                        init, ALU.mult, ALU.add), waits=sw)
                stw = [(sV, scan_hist[k * 256 + et * NST + g * GN + GN - 1])]
                if et == 0 and g == 0 and k >= 1:
                    stw += [(sV, states_copy_k[k - 1])]
                stmm_hist[gg] = t_op(lambda et=et, g=g, gp=gp: T.matmul(
                    st_ps[:, et * NST + g * GN:et * NST + (g + 1) * GN],
                    eye_bf[:, :],
                    AP(hs_roll, gp * GN * CH + CH - 1, [[2 * GN * CH, P], [CH, GN]]),
                    start=True, stop=True), waits=stw)
                tw = [(sT, foldl_hist.get(gg - 2, 0))]
                if et == 0 and g == 0:
                    tw += [(sR, rep_done[k])]
                tmpd = v_op(lambda g=g, gp=gp: V.tensor_tensor(
                    out=AP(tmp_roll, gp * GN * CH, [[2 * GN * CH, P], [CH, GN], [1, CH]]),
                    in0=AP(hs_roll, gp * GN * CH, [[2 * GN * CH, P], [CH, GN], [1, CH]]),
                    in1=AP(bcb, (16 + g * GN) * CH, [[32 * CH, P], [CH, GN], [1, CH]]),
                    op=ALU.mult), waits=tw)
                if et == NET - 1 and g == NG - 1:
                    tmp_last[k] = tmpd
                fold = 0
                for j in range(GN):
                    fw = []
                    if j == 0:
                        fw += [(sV, tmpd)]
                        if g == 0:
                            fw += [(sV, gate_hist.get(etg - 2, 0))]
                    fold = t_op(lambda par=par, gp=gp, j=j, g=g: T.matmul(
                        y_ps[:, par * CH:par * CH + CH],
                        eye_bf[:, :],
                        tmp_roll[:, gp * GN * CH + j * CH:gp * GN * CH + (j + 1) * CH],
                        start=(g == 0 and j == 0), stop=False), waits=fw)
                foldl_hist[gg] = fold

            dxcv = v_op(lambda et=et, par=par: V.tensor_scalar(
                dxc[:, par * CH:par * CH + CH],
                xc_bf[:, et * CH:(et + 1) * CH],
                dparm[:, et:et + 1], None, ALU.mult),
                waits=[(sT, f17_hist.get(etg - 2, 0))])
            dxc_hist[etg] = dxcv
            f17 = t_op(lambda par=par: T.matmul(
                y_ps[:, par * CH:par * CH + CH],
                eye_bf[:, :],
                dxc[:, par * CH:par * CH + CH],
                start=False, stop=True),
                waits=[(sV, dxcv)])
            f17_hist[etg] = f17
            gate_hist[etg] = v_op(lambda et=et, par=par: V.tensor_tensor(
                z_all[:, et * CH:(et + 1) * CH],
                y_ps[:, par * CH:par * CH + CH],
                z_all[:, et * CH:(et + 1) * CH], ALU.mult),
                waits=[(sT, f17), (sS, zdone_hist[etg])])

        def emit_outproj(k):
            xb = (k % 2) * NCT * CH
            c0 = k * CH
            for mt in range(NCT):
                if mt >= 1:
                    wts = [(sT, opmm_k[k][mt - 1])]
                elif k >= 1:
                    wts = [(sT, opmm_k[k - 1].get(NCT - 1, 0))]
                else:
                    wts = [(sL, last_w_outproj)]
                wload = prog.emit("gpsimd",
                                  lambda mt=mt: nc.gpsimd.dma_start(
                                      out=wstream[:, OPJ:OPJ + NET * P],
                                      in_=scr_outproj[mt, :, :]),
                                  waits=wts, incs=[(sWO, 16)])
                mmv = 0
                for kt in range(NET):
                    wts = []
                    if kt == 0:
                        wts = [(sWO, wload), (sV, gate_hist[k * NET + NET - 1]),
                               w(mmfree[mt % 2])]
                        if mt >= 2:
                            wts += [(sV, outstage_k[k].get(mt - 2, 0))]
                    mmv = t_op(lambda mt=mt, kt=kt: T.matmul(
                        mm_ps[:, (mt % 2) * CH:(mt % 2) * CH + CH],
                        wstream[:, OPJ + kt * P:OPJ + (kt + 1) * P],
                        z_all[:, kt * CH:(kt + 1) * CH],
                        start=(kt == 0), stop=(kt == NET - 1)), waits=wts)
                opmm_k[k][mt] = mmv
                ost = v_op(lambda mt=mt, xb=xb: V.tensor_tensor(
                    outstg[:, (mt % 2) * CH:(mt % 2) * CH + CH],
                    mm_ps[:, (mt % 2) * CH:(mt % 2) * CH + CH],
                    x_ch[:, xb + mt * CH:xb + (mt + 1) * CH], ALU.add),
                    waits=[(sT, mmv), (sV, ffn_add_k[k][mt]),
                           (sO, st_vals.get((k, mt - 2), st_vals.get((k - 1, 6 + mt), 0)))])
                outstage_k[k][mt] = ost
                mmfree[mt % 2] = ("V", ost)
                st_vals[(k, mt)] = prog.emit("sync",
                          lambda mt=mt, c0=c0: nc.sync.dma_start(
                              out=outT[mt * P:(mt + 1) * P, c0:c0 + CH],
                              in_=outstg[:, (mt % 2) * CH:(mt % 2) * CH + CH]),
                          waits=[(sV, outstage_k[k][mt])], incs=[(sO, 16)])

        bootA, bootZ = front_steps(0)
        for _, st in bootA:
            st()
        for st in bootZ:
            st()
        emit_reps(0)
        emit_dtchain(0, 0)

        for k in range(NCH):
            if k + 1 < NCH:
                stepsA, stepsZ = front_steps(k + 1)
            else:
                stepsA, stepsZ = [], []
            si = 0
            for et in range(NET):
                emit_scan_et(k, et)
                if et + 1 < NET:
                    emit_dtchain(k, et + 1)
                budget = 3
                while si < len(stepsA) and budget > 0 and (
                        stepsA[si][0] <= et or et == NET - 1):
                    stepsA[si][1]()
                    si += 1
                    budget -= 1
                if et == NET - 1:
                    while si < len(stepsA):
                        stepsA[si][1]()
                        si += 1
            states_copy_k[k] = v_op(lambda: V.tensor_copy(states[:, :], st_ps[:, :]),
                                    waits=[(sT, stmm_hist[(k * NET + NET - 1) * NG + NG - 1])])
            emit_outproj(k)
            for st in stepsZ:
                st()
            if k + 1 < NCH:
                emit_reps(k + 1)
                emit_dtchain(k + 1, 0)

        total_stores = prog.count(sO)
        prog.emit("sync", lambda: nc.sync.nop(), waits=[(sO, total_stores)])

        with nc.Block() as block:
            @block.sync
            def _(eng):
                prog.replay("sync", eng)

            @block.vector
            def _(eng):
                prog.replay("vector", eng)

            @block.scalar
            def _(eng):
                prog.replay("scalar", eng)

            @block.tensor
            def _(eng):
                prog.replay("tensor", eng)

            @block.gpsimd
            def _(eng):
                prog.replay("gpsimd", eng)

    return nc


_NC_CACHE = None


def _get_nc():
    global _NC_CACHE
    if _NC_CACHE is None:
        _NC_CACHE = build_program()
    return _NC_CACHE


def kernel(L0_attn_x, L0_attn_in_proj_w, L0_attn_conv_w, L0_attn_conv_b,
           L0_attn_x_proj_w, L0_attn_dt_w, L0_attn_dt_b, L0_attn_A_log,
           L0_attn_D_param, L0_attn_out_proj_w, L0_ffn_weight,
           L0_ffn_aux_weight1, L0_ffn_aux_weight2, L0_ffn_bias):
    x = np.asarray(L0_attn_x, dtype=np.float32)
    B = x.shape[0]
    f32 = lambda a: np.ascontiguousarray(np.asarray(a, dtype=np.float32))

    inprojT = f32(np.asarray(L0_attn_in_proj_w).T)
    outprojT = f32(np.asarray(L0_attn_out_proj_w).T)
    xprojT = f32(np.asarray(L0_attn_x_proj_w).T)
    dt_wT = f32(np.asarray(L0_attn_dt_w).T)
    w1T = f32(np.asarray(L0_ffn_weight).T)
    wa1T = f32(np.asarray(L0_ffn_aux_weight1).T)
    wa2 = f32(L0_ffn_aux_weight2)

    def chan_r(vec, nt):
        return f32(np.asarray(vec).reshape(nt, P).T)

    convw_r = f32(np.asarray(L0_attn_conv_w).reshape(NET, P, 4)
                  .transpose(1, 0, 2).reshape(P, 4 * NET))
    convb_r = chan_r(L0_attn_conv_b, NET)
    dtb_r = chan_r(L0_attn_dt_b, NET)
    alog_r = f32(np.asarray(L0_attn_A_log).reshape(NET, P, NST)
                 .transpose(1, 0, 2).reshape(P, NST * NET))
    d_r = chan_r(L0_attn_D_param, NET)
    fbias_r = chan_r(L0_ffn_bias, NCT)
    eye = np.eye(P, dtype=BF)
    ones = np.ones((P, 1), dtype=np.float32)

    shared = dict(inprojT=inprojT, outprojT=outprojT, xprojT=xprojT,
                  dt_wT=dt_wT, w1T=w1T, wa1T=wa1T, wa2=wa2,
                  convw_r=convw_r, convb_r=convb_r, dtb_r=dtb_r,
                  alog_r=alog_r, d_r=d_r, fbias_r=fbias_r,
                  eye_in=eye, ones_in=ones)

    in_maps = []
    for b in range(B):
        m = dict(shared)
        m["xT"] = np.ascontiguousarray(x[b].T)
        in_maps.append(m)

    nc = _get_nc()
    res = run_bass_kernel_spmd(nc, in_maps, core_ids=list(range(B)))
    out = np.stack([np.asarray(r["outT"]).T for r in res.results], axis=0)
    return out.astype(np.float32)


# version 08
# speedup vs baseline: 1.0634x; 1.0634x over previous
import numpy as np
import ml_dtypes
from contextlib import ExitStack

import concourse.bass as bass
import concourse.mybir as mybir
from concourse.ap import AP
from concourse.bass_utils import run_bass_kernel_spmd

F32 = mybir.dt.float32
BF16 = mybir.dt.bfloat16
AF = mybir.ActivationFunctionType
ALU = mybir.AluOpType

P = 128
L = 2048
CH = 512
NCH = L // CH
NET = 16
NCT = 8
NST = 16
DTR = 64
DM = 1024
DI = 2048
NG = 4
GN = 4

BF = ml_dtypes.bfloat16


class Prog:

    ENGINES = ("sync", "vector", "scalar", "tensor", "gpsimd")

    def __init__(self):
        self.items = {e: [] for e in self.ENGINES}
        self.counts = {}

    def emit(self, eng, fn, waits=(), incs=()):
        waits = [(s, v) for (s, v) in waits if v > 0]
        self.items[eng].append((fn, waits, list(incs)))
        post = None
        for (sem, amt) in incs:
            self.counts[id(sem)] = self.counts.get(id(sem), 0) + amt
            post = self.counts[id(sem)]
        return post

    def count(self, sem):
        return self.counts.get(id(sem), 0)

    def replay(self, eng, handle):
        for fn, waits, incs in self.items[eng]:
            for sem, val in waits:
                handle.wait_ge(sem, val)
            inst = fn()
            for sem, amt in incs:
                inst = inst.then_inc(sem, amt)


def build_program(debug=False, enddump=False):
    nc = bass.Bass("TRN2", target_bir_lowering=False)

    xT = nc.dram_tensor("xT", [DM, L], F32, kind="ExternalInput").ap()
    inprojT = nc.dram_tensor("inprojT", [DM, 2 * DI], F32, kind="ExternalInput").ap()
    outprojT = nc.dram_tensor("outprojT", [DI, DM], F32, kind="ExternalInput").ap()
    xprojT = nc.dram_tensor("xprojT", [DI, 96], F32, kind="ExternalInput").ap()
    dt_wT = nc.dram_tensor("dt_wT", [DTR, DI], F32, kind="ExternalInput").ap()
    w1T = nc.dram_tensor("w1T", [DM, DM], F32, kind="ExternalInput").ap()
    wa1T = nc.dram_tensor("wa1T", [DM, DM], F32, kind="ExternalInput").ap()
    wa2 = nc.dram_tensor("wa2", [DM, DM], F32, kind="ExternalInput").ap()
    convw_r = nc.dram_tensor("convw_r", [P, 4 * NET], F32, kind="ExternalInput").ap()
    convb_r = nc.dram_tensor("convb_r", [P, NET], F32, kind="ExternalInput").ap()
    dtb_r = nc.dram_tensor("dtb_r", [P, NET], F32, kind="ExternalInput").ap()
    alog_r = nc.dram_tensor("alog_r", [P, NST * NET], F32, kind="ExternalInput").ap()
    d_r = nc.dram_tensor("d_r", [P, NET], F32, kind="ExternalInput").ap()
    fbias_r = nc.dram_tensor("fbias_r", [P, NCT], F32, kind="ExternalInput").ap()
    eye_in = nc.dram_tensor("eye_in", [P, P], BF16, kind="ExternalInput").ap()
    ones_in = nc.dram_tensor("ones_in", [P, 1], F32, kind="ExternalInput").ap()

    outT = nc.dram_tensor("outT", [DM, L], F32, kind="ExternalOutput").ap()
    if enddump:
        ed = {}
        for nm, shp, dt in (("xch", [P, NCT * CH], F32),
                            ("negA", [P, NST * NET], F32), ("dtw", [DTR, DI], BF16),
                            ("xproj", [P, NET * 96], BF16), ("wp", [P, NCT * DM], BF16),
                            ("bcb", [P, 32 * CH], BF16), ("xn", [P, NCT * CH], BF16),
                            ("xc", [P, NET * CH], BF16), ("zall", [P, NET * CH], BF16),
                            ("ygate", [P, NET * CH], BF16), ("states", [P, NET * NST], F32),
                            ("convw", [P, 4 * NET], F32), ("dtb", [P, NET], F32),
                            ("aB", [P, CH], BF16), ("bB", [P, CH], BF16),
                            ("drawT", [DTR, CH], BF16), ("eye", [P, P], BF16)):
            ed[nm] = nc.dram_tensor("ed_" + nm, shp, dt, kind="ExternalOutput").ap()
    if debug:
        dbg_negA = nc.dram_tensor("dbg_negA", [P, NST * NET], F32, kind="ExternalOutput").ap()
        dbg_wp = nc.dram_tensor("dbg_wp", [P, NCT * DM], BF16, kind="ExternalOutput").ap()
        dbg_dtw = nc.dram_tensor("dbg_dtw", [DTR, DI], BF16, kind="ExternalOutput").ap()
        dbg_xproj = nc.dram_tensor("dbg_xproj", [P, NET * 96], BF16, kind="ExternalOutput").ap()
        dbg_xn = nc.dram_tensor("dbg_xn", [NCH, P, NCT * CH], BF16, kind="ExternalOutput").ap()
        dbg_xc = nc.dram_tensor("dbg_xc", [NCH, P, NET * CH], BF16, kind="ExternalOutput").ap()
        dbg_bcb = nc.dram_tensor("dbg_bcb", [NCH, P, 32 * CH], BF16, kind="ExternalOutput").ap()
        dbg_states = nc.dram_tensor("dbg_states", [NCH, P, NET * NST], F32, kind="ExternalOutput").ap()
        dbg_ygate = nc.dram_tensor("dbg_ygate", [NCH, P, NET * CH], BF16, kind="ExternalOutput").ap()
        dbg_z = nc.dram_tensor("dbg_z", [NCH, P, 2 * CH], BF16, kind="ExternalOutput").ap()

    scr_inproj_h = nc.dram_tensor("scr_inproj", [32, P, NCT * P], BF16, kind="Internal")
    scr_outproj_h = nc.dram_tensor("scr_outproj", [NCT, P, NET * P], BF16, kind="Internal")
    scr_inproj = scr_inproj_h.ap()
    scr_outproj = scr_outproj_h.ap()
    scr_bc = nc.dram_tensor("scr_bc", [32, CH], BF16, kind="Internal").ap()
    scr_stat = nc.dram_tensor("scr_stat", [2, CH], BF16, kind="Internal").ap()
    scr_wa2 = nc.dram_tensor("scr_wa2", [NCT, P, DM], BF16, kind="Internal").ap()

    prog = Prog()

    with ExitStack() as ctx:
        sbt = lambda name, shape, dt: ctx.enter_context(nc.sbuf_tensor(name, shape, dt))
        psb = lambda name, shape, dt: ctx.enter_context(nc.psum_tensor(name, shape, dt))
        sem = lambda name: ctx.enter_context(nc.semaphore(name))

        x_ch = sbt("x_ch", [P, NCT * CH], F32)
        xn_bf = sbt("xn_bf", [P, NCT * CH], BF16)
        xcry = sbt("xcry", [P, NET * 520], BF16)
        xc_bf = sbt("xc_bf", [P, NET * CH], BF16)
        z_all = sbt("z_all", [P, NET * CH], BF16)
        bcb = sbt("bcb", [P, 32 * CH], BF16)
        wp_bf = sbt("wp_bf", [P, NCT * DM], BF16)
        wstream = sbt("wstream", [P, 32 * P], BF16)
        u_roll = sbt("u_roll", [P, 2 * GN * CH], BF16)
        hs_roll = sbt("hs_roll", [P, 2 * GN * CH], BF16)
        tmp_roll = sbt("tmp_roll", [P, 2 * GN * CH], BF16)
        delta2 = sbt("delta2", [P, 2 * CH], F32)
        v2 = sbt("v2", [P, 2 * CH], BF16)
        dA = sbt("dA", [P, 4 * CH], F32)
        dxc = sbt("dxc", [P, 2 * CH], BF16)
        stage = sbt("stage", [P, 2 * CH], F32)
        outstg = sbt("outstg", [P, 2 * CH], F32)
        states = sbt("states", [P, NET * NST], F32)
        aB = sbt("aB", [P, CH], BF16)
        bB = sbt("bB", [P, CH], BF16)
        diag = sbt("diag", [P, 2 * 4 * P], BF16)
        drawT = sbt("drawT", [DTR, CH], BF16)
        bc_sb = sbt("bc_sb", [96, CH], BF16)
        halo = sbt("halo", [P, NET * 4], BF16)
        negA = sbt("negA", [P, NST * NET], F32)
        convw = sbt("convw", [P, 4 * NET], F32)
        convb = sbt("convb", [P, NET], F32)
        dtb = sbt("dtb", [P, NET], F32)
        dparm = sbt("dparm", [P, NET], F32)
        fbias = sbt("fbias", [P, NCT], F32)
        eye_bf = sbt("eye_bf", [P, P], BF16)
        ones = sbt("ones", [P, 1], F32)
        xprojT_bf = sbt("xprojT_bf", [P, NET * 96], BF16)
        dt_wT_bf = sbt("dt_wT_bf", [DTR, DI], BF16)
        r_sx = sbt("r_sx", [1, CH], F32)
        r_sx2 = sbt("r_sx2", [1, CH], F32)
        r_a = sbt("r_a", [1, CH], BF16)
        r_b = sbt("r_b", [1, CH], BF16)
        r_t = sbt("r_t", [1, CH], F32)

        mm_ps = psb("mm_ps", [P, 2 * CH], F32)
        conv_ps = psb("conv_ps", [P, 2 * CH], F32)
        y_ps = psb("y_ps", [P, 2 * CH], F32)
        st_ps = psb("st_ps", [P, NET * NST], F32)
        stat_ps = psb("stat_ps", [1, CH], F32)

        sL = sem("sL")
        sG = sem("sG")
        sV = sem("sV")
        sS = sem("sS")
        sT = sem("sT")
        sO = sem("sO")

        T = nc.tensor
        V = nc.vector
        S = nc.scalar

        def sy_dma(dst, src, waits=()):
            return prog.emit("sync", lambda d=dst, s=src: nc.sync.dma_start(out=d, in_=s),
                             waits=waits, incs=[(sL, 16)])

        def g_dma(dst, src, waits=()):
            return prog.emit("gpsimd", lambda d=dst, s=src: nc.gpsimd.dma_start(out=d, in_=s),
                             waits=waits, incs=[(sG, 16)])

        def v_op(fn, waits=()):
            return prog.emit("vector", fn, waits=waits, incs=[(sV, 1)])

        def s_op(fn, waits=()):
            return prog.emit("scalar", fn, waits=waits, incs=[(sS, 1)])

        def t_op(fn, waits=()):
            return prog.emit("tensor", fn, waits=waits, incs=[(sT, 1)])

        SEMS = {"S": sS, "V": sV, "T": sT}

        def w(tag):
            eng, cnt = tag
            return (SEMS[eng], cnt)

        sy_dma(eye_bf[:, :], eye_in)
        sy_dma(ones[:, :], ones_in)
        sy_dma(convw[:, :], convw_r)
        sy_dma(convb[:, :], convb_r)
        sy_dma(dtb[:, :], dtb_r)
        sy_dma(dparm[:, :], d_r)
        sy_dma(fbias[:, :], fbias_r)
        w_alog = sy_dma(x_ch[:, 0:NST * NET], alog_r)
        w_dtw = sy_dma(x_ch[0:DTR, 256:256 + DI], dt_wT)
        w_xp = 0
        for kt in range(NET):
            w_xp = sy_dma(x_ch[:, 2304 + kt * 96:2304 + (kt + 1) * 96],
                          xprojT[kt * P:(kt + 1) * P, :])

        e1 = s_op(lambda: S.activation(stage[:, 0:NST * NET], x_ch[:, 0:NST * NET], AF.Exp),
                  waits=[(sL, w_xp)])
        nega_done = v_op(lambda: V.tensor_scalar(negA[:, :], stage[:, 0:NST * NET],
                                                 -1.0, None, ALU.mult),
                         waits=[(sS, e1)])
        dtw_cast = v_op(lambda: V.tensor_copy(dt_wT_bf[:, :], x_ch[0:DTR, 256:256 + DI]),
                        waits=[(sL, w_xp)])
        xproj_cast = v_op(lambda: V.tensor_copy(xprojT_bf[:, :],
                                                x_ch[:, 2304:2304 + NET * 96]),
                          waits=[(sL, w_xp)])
        v_op(lambda: V.memset(halo[:, :], 0.0))

        last_w_inproj = 0
        for kt in range(NCT):
            wts = [(sL, last_w_inproj)] if kt else [(sS, e1), (sV, dtw_cast),
                                                    (sV, xproj_cast)]
            wld = sy_dma(x_ch[:, 0:2 * DI], inprojT[kt * P:(kt + 1) * P, :], waits=wts)
            cst = v_op(lambda: V.tensor_copy(xcry[:, 0:2 * DI], x_ch[:, 0:2 * DI]),
                       waits=[(sL, wld)])
            for q in range(4):
                dst = AP(scr_inproj_h, q * 8 * P * NCT * P + kt * P,
                         [[NCT * P, P], [P * NCT * P, 8], [1, P]])
                src = AP(xcry, q * 8 * P, [[NET * 520, P], [P, 8], [1, P]])
                last_w_inproj = sy_dma(dst, src, waits=[(sV, cst)] if q == 0 else ())

        last_w_outproj = 0
        for kt in range(NET):
            wld = sy_dma(x_ch[:, 0:DM], outprojT[kt * P:(kt + 1) * P, :],
                         waits=[(sL, last_w_inproj if kt == 0 else last_w_outproj)])
            cst = v_op(lambda: V.tensor_copy(xcry[:, 0:DM], x_ch[:, 0:DM]),
                       waits=[(sL, wld)])
            for q in range(4):
                dst = AP(scr_outproj_h, q * 2 * P * NET * P + kt * P,
                         [[NET * P, P], [P * NET * P, 2], [1, P]])
                src = AP(xcry, q * 2 * P, [[NET * 520, P], [P, 2], [1, P]])
                last_w_outproj = sy_dma(dst, src, waits=[(sV, cst)] if q == 0 else ())

        lastv = prog.count(sV)
        for i in range(NCT):
            wld = sy_dma(x_ch[:, 0:DM], wa1T[i * P:(i + 1) * P, :],
                         waits=[(sV, lastv), (sL, last_w_outproj if i == 0 else 0)])
            lastv = v_op(lambda i=i: V.tensor_copy(xc_bf[:, i * DM:(i + 1) * DM],
                                                   x_ch[:, 0:DM]),
                         waits=[(sL, wld)])
        last_wa2_w = 0
        for i in range(NCT):
            wld = sy_dma(x_ch[:, 0:DM], wa2[i * P:(i + 1) * P, :], waits=[(sV, lastv)])
            lastv = v_op(lambda i=i: V.tensor_copy(z_all[:, (i % 2) * CH:(i % 2) * CH + DM // 2],
                                                   x_ch[:, 0:DM // 2]),
                         waits=[(sL, wld)])
            lastv = v_op(lambda i=i: V.tensor_copy(z_all[:, (2 + i % 2) * CH:(2 + i % 2) * CH + DM // 2],
                                                   x_ch[:, DM // 2:DM]))
            last_wa2_w = sy_dma(scr_wa2[i, :, 0:DM // 2], z_all[:, (i % 2) * CH:(i % 2) * CH + DM // 2],
                                waits=[(sV, lastv)])
            last_wa2_w = sy_dma(scr_wa2[i, :, DM // 2:DM], z_all[:, (2 + i % 2) * CH:(2 + i % 2) * CH + DM // 2])
        for i in range(NCT):
            wld = sy_dma(x_ch[:, 0:DM], w1T[i * P:(i + 1) * P, :], waits=[(sV, lastv)])
            lastv = v_op(lambda i=i: V.tensor_copy(xcry[:, i * DM:(i + 1) * DM],
                                                   x_ch[:, 0:DM]),
                         waits=[(sL, wld)])
        cast_all = lastv

        wa2_ld = {}
        for kt in range(NCT):
            wa2_ld[kt] = g_dma(bcb[:, kt * DM:(kt + 1) * DM], scr_wa2[kt, :, :],
                               waits=[(sL, last_wa2_w)] if kt == 0 else ())
        T2B = 8 * DM
        for (lhs_buf, rhs_buf, rhs_base, out_buf, out_base, first_wait) in (
                (xc_bf, bcb, 0, bcb, T2B, cast_all),
                (xcry, bcb, T2B, wp_bf, 0, None)):
            for mt in range(NCT):
                for half in range(2):
                    mmv = 0
                    for kt in range(NCT):
                        if kt == 0:
                            if first_wait is not None and mt == 0 and half == 0:
                                wts = [(sV, first_wait), (sG, wa2_ld[NCT - 1])]
                            else:
                                wts = [(sV, prog.count(sV))]
                        else:
                            wts = []
                        mmv = t_op(lambda lb=lhs_buf, rb=rhs_buf, rb0=rhs_base, mt=mt, half=half, kt=kt:
                                   T.matmul(
                                       mm_ps[:, (mt % 2) * CH:(mt % 2) * CH + CH],
                                       lb[:, kt * DM + mt * P:kt * DM + (mt + 1) * P],
                                       rb[:, rb0 + kt * DM + half * CH:rb0 + kt * DM + half * CH + CH],
                                       start=(kt == 0), stop=(kt == NCT - 1)), waits=wts)
                    v_op(lambda ob=out_buf, ob0=out_base, mt=mt, half=half: V.tensor_copy(
                        ob[:, ob0 + mt * DM + half * CH:ob0 + mt * DM + half * CH + CH],
                        mm_ps[:, (mt % 2) * CH:(mt % 2) * CH + CH]),
                        waits=[(sT, mmv)])
        prep_mm_last = prog.count(sT)
        if debug:
            prog.emit("sync", lambda: nc.sync.dma_start(out=dbg_negA, in_=negA[:, :]),
                      waits=[(sV, nega_done)], incs=[(sO, 16)])
            prog.emit("sync", lambda: nc.sync.dma_start(out=dbg_dtw, in_=dt_wT_bf[:, :]),
                      waits=[(sV, dtw_cast)], incs=[(sO, 16)])
            prog.emit("sync", lambda: nc.sync.dma_start(out=dbg_xproj, in_=xprojT_bf[:, :]),
                      waits=[(sV, xproj_cast)], incs=[(sO, 16)])
            prog.emit("sync", lambda: nc.sync.dma_start(out=dbg_wp, in_=wp_bf[:, :]),
                      waits=[(sT, prep_mm_last), (sV, prog.count(sV))], incs=[(sO, 16)])

        prev = dict(outstage_done=0, states_copy=0,
                    gate_done={}, opmm={},
                    rep_sy=0, rep_g=0, stat_rep=0, ffn_add_last=0,
                    bcb_free=("T", prep_mm_last), dtmm_last=0,
                    dxc_hist={})

        pending_gate = [None]
        scan_hist = {}
        stmm_hist = {}
        foldl_hist = {}
        gate_hist = {}
        f17_hist = {}
        v2_hist = {}
        dxc_hist = {}

        for k in range(NCH):
            c0 = k * CH

            xload = 0
            for ct in range(NCT):
                if ct == 0:
                    xw = [(sV, prev["outstage_done"] if k > 0 else cast_all)]
                else:
                    xw = []
                xload = sy_dma(x_ch[:, ct * CH:(ct + 1) * CH],
                               xT[ct * P:(ct + 1) * P, c0:c0 + CH],
                               waits=xw)
            sxmm = 0
            for ct in range(NCT):
                sxmm = t_op(lambda ct=ct: T.matmul(
                    stat_ps[0:1, :], ones[:, :], x_ch[:, ct * CH:(ct + 1) * CH],
                    start=(ct == 0), stop=(ct == NCT - 1)),
                    waits=([(sL, xload)] if ct == 0 else []))
            cp_sx = v_op(lambda: V.tensor_copy(r_sx[:, :], stat_ps[0:1, :]),
                         waits=[(sT, sxmm)])
            sx2mm = 0
            for ct in range(NCT):
                sq = s_op(lambda ct=ct: S.activation(
                    stage[:, (ct % 2) * CH:(ct % 2) * CH + CH],
                    x_ch[:, ct * CH:(ct + 1) * CH], AF.Square),
                    waits=([(sL, xload), (sV, prev["ffn_add_last"])] if ct == 0 else []) +
                          ([(sT, sx2mm)] if ct >= 2 else []))
                sx2mm = t_op(lambda ct=ct: T.matmul(
                    stat_ps[0:1, :], ones[:, :], stage[:, (ct % 2) * CH:(ct % 2) * CH + CH],
                    start=(ct == 0), stop=(ct == NCT - 1)),
                    waits=[(sS, sq)] + ([(sV, cp_sx)] if ct == 0 else []))
            cp_sx2 = v_op(lambda: V.tensor_copy(r_sx2[:, :], stat_ps[0:1, :]),
                          waits=[(sT, sx2mm)])
            v_op(lambda: V.tensor_scalar(r_sx[:, :], r_sx[:, :], 1.0 / DM, None, ALU.mult))
            v_op(lambda: V.tensor_scalar(r_sx2[:, :], r_sx2[:, :], 1.0 / DM, None, ALU.mult))
            v_op(lambda: V.scalar_tensor_tensor(r_t[:, :], r_sx[:, :], -1.0,
                                                r_sx[:, :], ALU.mult, ALU.mult))
            v_op(lambda: V.tensor_tensor(r_sx2[:, :], r_sx2[:, :], r_t[:, :], ALU.add))
            vvar = v_op(lambda: V.tensor_scalar(r_sx2[:, :], r_sx2[:, :], 1e-5, None, ALU.add))
            s_op(lambda: S.activation(r_t[:, :], r_sx2[:, :], AF.Ln),
                 waits=[(sV, vvar)])
            rstd = s_op(lambda: S.activation(r_a[:, :], r_t[:, :], AF.Exp, scale=-0.5))
            rb = v_op(lambda: V.scalar_tensor_tensor(r_b[:, :], r_sx[:, :], -1.0,
                                                     r_a[:, :], ALU.mult, ALU.mult),
                      waits=[(sS, rstd)])
            wst = sy_dma(scr_stat[0:1, :], r_a[:, :],
                         waits=[(sV, rb), (sL, prev["stat_rep"])])
            wst = sy_dma(scr_stat[1:2, :], r_b[:, :])
            sy_dma(aB[:, :], scr_stat[0:1, :].broadcast_to([P, CH]), waits=[(sL, wst)])
            stat_rep = sy_dma(bB[:, :], scr_stat[1:2, :].broadcast_to([P, CH]))
            xn_done = 0
            for ct in range(NCT):
                v_op(lambda ct=ct: V.tensor_tensor(
                    outstg[:, (ct % 2) * CH:(ct % 2) * CH + CH],
                    x_ch[:, ct * CH:(ct + 1) * CH], aB[:, :], ALU.mult),
                    waits=([(sL, stat_rep), (sO, prog.count(sO))] if ct == 0 else []))
                xn_done = v_op(lambda ct=ct: V.tensor_tensor(
                    xn_bf[:, ct * CH:(ct + 1) * CH],
                    outstg[:, (ct % 2) * CH:(ct % 2) * CH + CH], bB[:, :], ALU.add))

            silu_done = {}
            z_done = {}
            convgrp_done = {}
            mmgrp_done = {}
            for mt in range(32):
                wts = [(sT, mmgrp_done.get(mt - 2, 0))]
                if k == 0 and mt < 2:
                    wts += [(sL, last_w_outproj)]
                if k > 0 and mt < 2:
                    wts += [(sT, prev["opmm"].get(NCT - 1, 0))]
                wload = g_dma(wstream[:, (mt % 2) * NET * P:(mt % 2) * NET * P + NCT * P],
                              scr_inproj[mt, :, :], waits=wts)
                mmv = 0
                for kt in range(NCT):
                    wts = []
                    if kt == 0:
                        wts += [(sG, wload)]
                        if mt == 0:
                            wts += [(sV, xn_done)]
                        if mt - 2 >= 0:
                            if mt - 2 < NET:
                                wts += [(sV, prog.count(sV))]
                            else:
                                wts += [(sS, z_done.get(mt - 2 - NET, 0))]
                    mmv = t_op(lambda mt=mt, kt=kt: T.matmul(
                        mm_ps[:, (mt % 2) * CH:(mt % 2) * CH + CH],
                        wstream[:, (mt % 2) * NET * P + kt * P:(mt % 2) * NET * P + (kt + 1) * P],
                        xn_bf[:, kt * CH:(kt + 1) * CH],
                        start=(kt == 0), stop=(kt == NCT - 1)), waits=wts)
                mmgrp_done[mt] = mmv
                if mt < NET:
                    et = mt
                    base = et * 520
                    v_op(lambda et=et, base=base: V.tensor_copy(
                        xcry[:, base + 4:base + 8], halo[:, et * 4:(et + 1) * 4]),
                        waits=[(sV, prev["gate_done"].get(et, 0))])
                    v_op(lambda base=base, mt=mt: V.tensor_copy(
                        xcry[:, base + 8:base + 520],
                        mm_ps[:, (mt % 2) * CH:(mt % 2) * CH + CH]),
                        waits=[(sT, mmv)])
                    v_op(lambda et=et, mt=mt: V.tensor_copy(
                        halo[:, et * 4:(et + 1) * 4],
                        mm_ps[:, (mt % 2) * CH + 508:(mt % 2) * CH + 512]))
                    dgv = 0
                    for kk in range(4):
                        dgv = v_op(lambda et=et, kk=kk: V.tensor_scalar(
                            diag[:, (et % 2) * 4 * P + kk * P:(et % 2) * 4 * P + (kk + 1) * P],
                            eye_bf[:, :], convw[:, et * 4 + kk:et * 4 + kk + 1],
                            None, ALU.mult),
                            waits=([(sT, convgrp_done.get(et - 2, 0))] if kk == 0 else []))
                    cnv = 0
                    for kk in range(4):
                        wts = [(sV, dgv), (sS, silu_done.get(et - 2, 0))] if kk == 0 else []
                        cnv = t_op(lambda et=et, kk=kk, base=base: T.matmul(
                            conv_ps[:, (et % 2) * CH:(et % 2) * CH + CH],
                            diag[:, (et % 2) * 4 * P + kk * P:(et % 2) * 4 * P + (kk + 1) * P],
                            xcry[:, base + 5 + kk:base + 5 + kk + CH],
                            start=(kk == 0), stop=(kk == 3)), waits=wts)
                    convgrp_done[et] = cnv
                    silu_done[et] = s_op(lambda et=et: S.activation(
                        xc_bf[:, et * CH:(et + 1) * CH],
                        conv_ps[:, (et % 2) * CH:(et % 2) * CH + CH],
                        AF.Silu, bias=convb[:, et:et + 1]),
                        waits=[(sT, cnv),
                               (sV, prev["dxc_hist"].get(et, 0))])
                else:
                    et = mt - NET
                    z_done[et] = s_op(lambda et=et, mt=mt: S.activation(
                        z_all[:, et * CH:(et + 1) * CH],
                        mm_ps[:, (mt % 2) * CH:(mt % 2) * CH + CH],
                        AF.Silu),
                        waits=[(sT, mmv),
                               (sV, prev["gate_done"].get(et, 0))])

            mmfree = {0: ("S", z_done[14]), 1: ("S", z_done[15])}

            xpv = 0
            for kt in range(NET):
                wts = [(sS, silu_done[kt])]
                if kt == 0:
                    wts += [(sS, silu_done[NET - 1])]
                xpv = t_op(lambda kt=kt: T.matmul(
                    conv_ps[0:96, 0:CH],
                    xprojT_bf[:, kt * 96:(kt + 1) * 96],
                    xc_bf[:, kt * CH:(kt + 1) * CH],
                    start=(kt == 0), stop=(kt == NET - 1)), waits=wts)
            draw_done = s_op(lambda: S.activation(drawT[:, :], conv_ps[0:DTR, 0:CH],
                                                  AF.Copy),
                             waits=[(sT, xpv), (sT, prev["dtmm_last"])])
            bc_done = v_op(lambda: V.tensor_copy(bc_sb[DTR:96, :], conv_ps[DTR:96, 0:CH]),
                           waits=[(sT, xpv)])
            bc_w = sy_dma(scr_bc[:, :], bc_sb[DTR:96, :],
                          waits=[(sV, bc_done), (sL, prev["rep_sy"]),
                                 (sG, prev["rep_g"])])
            rep_sy = 0
            rep_g = 0
            for n in range(NST):
                wts = ([(sL, bc_w), w(prev["bcb_free"])] if n == 0 else [])
                rep_sy = sy_dma(bcb[:, n * CH:(n + 1) * CH],
                                scr_bc[n:n + 1, :].broadcast_to([P, CH]), waits=wts)
                rep_g = g_dma(bcb[:, (16 + n) * CH:(17 + n) * CH],
                              scr_bc[16 + n:17 + n, :].broadcast_to([P, CH]), waits=wts)

            dtmm_last = 0
            for et in range(NET):
                etg = k * NET + et
                par = et % 2
                dtmm = t_op(lambda et=et, par=par: T.matmul(
                    mm_ps[:, par * CH:par * CH + CH],
                    dt_wT_bf[:, et * P:(et + 1) * P],
                    drawT[:, :], start=True, stop=True),
                    waits=[w(mmfree[par]), (sS, draw_done)])
                dtmm_last = dtmm
                s_op(lambda et=et, par=par: S.activation(
                    stage[:, par * CH:par * CH + CH],
                    mm_ps[:, par * CH:par * CH + CH],
                    AF.Exp, bias=dtb[:, et:et + 1]),
                    waits=[(sT, dtmm)])
                sp = s_op(lambda par=par: S.activation(
                    delta2[:, par * CH:par * CH + CH],
                    stage[:, par * CH:par * CH + CH],
                    AF.Ln, bias=1.0),
                    waits=[(sV, v2_hist.get(etg - 2, 0))])
                mmfree[par] = ("S", sp)
                v2d = v_op(lambda par=par, et=et: V.tensor_tensor(
                    v2[:, par * CH:par * CH + CH],
                    delta2[:, par * CH:par * CH + CH],
                    xc_bf[:, et * CH:(et + 1) * CH], ALU.mult),
                    waits=[(sS, sp)])
                v2_hist[etg] = v2d

                for g in range(NG):
                    gg = etg * NG + g
                    gp = gg % 2
                    if g == 1 and pending_gate[0] is not None:
                        pending_gate[0]()
                        pending_gate[0] = None
                    uw = []
                    if et == 0 and g == 0:
                        uw = [(sL, rep_sy), (sG, rep_g)]
                    v_op(lambda par=par, g=g, gp=gp: V.tensor_tensor(
                        out=AP(u_roll, gp * GN * CH, [[2 * GN * CH, P], [CH, GN], [1, CH]]),
                        in0=AP(v2, par * CH, [[2 * CH, P], [0, GN], [1, CH]]),
                        in1=AP(bcb, g * GN * CH, [[32 * CH, P], [CH, GN], [1, CH]]),
                        op=ALU.mult), waits=uw)
                    for n4 in range(GN):
                        n = g * GN + n4
                        i = et * NST + n
                        gi = k * 256 + i
                        ew = [(sV, scan_hist.get(gi - 4, 0))]
                        if gi == 0:
                            ew += [(sV, nega_done)]
                        expd = s_op(lambda par=par, i=i: S.activation(
                            dA[:, (i % 4) * CH:(i % 4) * CH + CH],
                            delta2[:, par * CH:par * CH + CH],
                            AF.Exp, scale=negA[:, i:i + 1]), waits=ew)
                        init = 0.0 if k == 0 else states[:, i:i + 1]
                        sw = [(sS, expd)]
                        if n4 == 0:
                            sw += [(sT, stmm_hist.get(gg - 2, 0))]
                        scan_hist[gi] = v_op(lambda gp=gp, n4=n4, i=i, init=init:
                                             V.tensor_tensor_scan(
                            hs_roll[:, gp * GN * CH + n4 * CH:gp * GN * CH + (n4 + 1) * CH],
                            dA[:, (i % 4) * CH:(i % 4) * CH + CH],
                            u_roll[:, gp * GN * CH + n4 * CH:gp * GN * CH + (n4 + 1) * CH],
                            init, ALU.mult, ALU.add), waits=sw)
                    stw = [(sV, scan_hist[k * 256 + et * NST + g * GN + GN - 1])]
                    if et == 0 and g == 0:
                        stw += [(sV, prev["states_copy"])]
                    stmm_hist[gg] = t_op(lambda et=et, g=g, gp=gp: T.matmul(
                        st_ps[:, et * NST + g * GN:et * NST + (g + 1) * GN],
                        eye_bf[:, :],
                        AP(hs_roll, gp * GN * CH + CH - 1, [[2 * GN * CH, P], [CH, GN]]),
                        start=True, stop=True), waits=stw)
                    tw = [(sT, foldl_hist.get(gg - 2, 0))]
                    if et == 0 and g == 0:
                        tw += [(sG, rep_g)]
                    tmpd = v_op(lambda g=g, gp=gp: V.tensor_tensor(
                        out=AP(tmp_roll, gp * GN * CH, [[2 * GN * CH, P], [CH, GN], [1, CH]]),
                        in0=AP(hs_roll, gp * GN * CH, [[2 * GN * CH, P], [CH, GN], [1, CH]]),
                        in1=AP(bcb, (16 + g * GN) * CH, [[32 * CH, P], [CH, GN], [1, CH]]),
                        op=ALU.mult), waits=tw)
                    fold = 0
                    for j in range(GN):
                        fw = []
                        if j == 0:
                            fw += [(sV, tmpd)]
                            if g == 0:
                                fw += [(sV, gate_hist.get(etg - 2, 0))]
                        fold = t_op(lambda par=par, gp=gp, j=j, g=g: T.matmul(
                            y_ps[:, par * CH:par * CH + CH],
                            eye_bf[:, :],
                            tmp_roll[:, gp * GN * CH + j * CH:gp * GN * CH + (j + 1) * CH],
                            start=(g == 0 and j == 0), stop=False), waits=fw)
                    foldl_hist[gg] = fold

                dxcv = v_op(lambda et=et, par=par: V.tensor_scalar(
                    dxc[:, par * CH:par * CH + CH],
                    xc_bf[:, et * CH:(et + 1) * CH],
                    dparm[:, et:et + 1], None, ALU.mult),
                    waits=[(sT, f17_hist.get(etg - 2, 0))])
                dxc_hist[etg] = dxcv
                f17 = t_op(lambda par=par: T.matmul(
                    y_ps[:, par * CH:par * CH + CH],
                    eye_bf[:, :],
                    dxc[:, par * CH:par * CH + CH],
                    start=False, stop=True),
                    waits=[(sV, dxcv)])
                f17_hist[etg] = f17

                def emit_gate(et=et, par=par, etg=etg, f17=f17, zd=z_done[et]):
                    gate_hist[etg] = v_op(lambda et=et, par=par: V.tensor_tensor(
                        xcry[:, et * CH:(et + 1) * CH],
                        y_ps[:, par * CH:par * CH + CH],
                        z_all[:, et * CH:(et + 1) * CH], ALU.mult),
                        waits=[(sT, f17), (sS, zd)])
                pending_gate[0] = emit_gate
            if pending_gate[0] is not None:
                pending_gate[0]()
                pending_gate[0] = None
            states_copy = v_op(lambda: V.tensor_copy(states[:, :], st_ps[:, :]),
                               waits=[(sT, stmm_hist[(k * NET + NET - 1) * NG + NG - 1])])

            if debug:
                prog.emit("sync", lambda k=k: nc.sync.dma_start(out=dbg_xn[k], in_=xn_bf[:, :]),
                          waits=[(sV, xn_done)], incs=[(sO, 16)])
                prog.emit("sync", lambda k=k: nc.sync.dma_start(out=dbg_xc[k], in_=xc_bf[:, :]),
                          waits=[(sS, silu_done[NET - 1])], incs=[(sO, 16)])
                prog.emit("sync", lambda k=k: nc.sync.dma_start(out=dbg_bcb[k], in_=bcb[:, :]),
                          waits=[(sL, rep_sy), (sG, rep_g)], incs=[(sO, 16)])
                prog.emit("sync", lambda k=k: nc.sync.dma_start(out=dbg_states[k], in_=states[:, :]),
                          waits=[(sV, states_copy)], incs=[(sO, 16)])
                prog.emit("sync", lambda k=k: nc.sync.dma_start(out=dbg_ygate[k], in_=xcry[:, 0:NET * CH]),
                          waits=[(sV, gate_hist[k * NET + NET - 1])], incs=[(sO, 16)])
                prog.emit("sync", lambda k=k: nc.sync.dma_start(out=dbg_z[k], in_=z_all[:, 0:2 * CH]),
                          waits=[(sS, z_done[NET - 1])], incs=[(sO, 16)])

            ffn_add = {}
            ffn_th = {}
            for ct in range(NCT):
                mmv = 0
                for kt in range(NCT):
                    wts = []
                    if kt == 0:
                        wts += [w(mmfree[ct % 2])]
                        if ct == 0:
                            wts += [(sV, xn_done)]
                    mmv = t_op(lambda ct=ct, kt=kt: T.matmul(
                        mm_ps[:, (ct % 2) * CH:(ct % 2) * CH + CH],
                        wp_bf[:, kt * DM + ct * P:kt * DM + (ct + 1) * P],
                        xn_bf[:, kt * CH:(kt + 1) * CH],
                        start=(kt == 0), stop=(kt == NCT - 1)), waits=wts)
                ffn_th[ct] = s_op(lambda ct=ct: S.activation(
                    stage[:, (ct % 2) * CH:(ct % 2) * CH + CH],
                    mm_ps[:, (ct % 2) * CH:(ct % 2) * CH + CH],
                    AF.Tanh, bias=fbias[:, ct:ct + 1]),
                    waits=[(sT, mmv)] + ([(sV, ffn_add[ct - 2])] if ct >= 2 else []))
                mmfree[ct % 2] = ("S", ffn_th[ct])
                ffn_add[ct] = v_op(lambda ct=ct: V.tensor_tensor(
                    x_ch[:, ct * CH:(ct + 1) * CH],
                    x_ch[:, ct * CH:(ct + 1) * CH],
                    stage[:, (ct % 2) * CH:(ct % 2) * CH + CH], ALU.add),
                    waits=[(sS, ffn_th[ct])])

            outstage_done = 0
            opmm = {}
            st_vals = dict(prev.get("st_vals", {}))
            for mt in range(NCT):
                wts = [(sT, opmm.get(mt - 1, 0))]
                if mt == 0:
                    wts += [(sT, mmgrp_done[31])]
                    if k == 0:
                        wts += [(sL, last_w_outproj)]
                wload = g_dma(wstream[:, 0:NET * P],
                              scr_outproj[mt, :, :], waits=wts)
                mmv = 0
                for kt in range(NET):
                    wts = []
                    if kt == 0:
                        wts += [(sG, wload), (sV, gate_hist[k * NET + NET - 1]),
                                w(mmfree[mt % 2])]
                        if mt >= 2:
                            wts += [(sV, prog.count(sV))]
                    mmv = t_op(lambda mt=mt, kt=kt: T.matmul(
                        mm_ps[:, (mt % 2) * CH:(mt % 2) * CH + CH],
                        wstream[:, kt * P:(kt + 1) * P],
                        xcry[:, kt * CH:(kt + 1) * CH],
                        start=(kt == 0), stop=(kt == NET - 1)), waits=wts)
                opmm[mt] = mmv
                outstage_done = v_op(lambda mt=mt: V.tensor_tensor(
                    outstg[:, (mt % 2) * CH:(mt % 2) * CH + CH],
                    mm_ps[:, (mt % 2) * CH:(mt % 2) * CH + CH],
                    x_ch[:, mt * CH:(mt + 1) * CH], ALU.add),
                    waits=[(sT, mmv), (sV, ffn_add[mt]),
                           (sO, st_vals.get(mt - 2, st_vals.get(6 + mt, 0)))])
                mmfree[mt % 2] = ("V", outstage_done)
                st_vals[mt] = prog.emit("sync",
                          lambda mt=mt, c0=c0: nc.sync.dma_start(
                              out=outT[mt * P:(mt + 1) * P, c0:c0 + CH],
                              in_=outstg[:, (mt % 2) * CH:(mt % 2) * CH + CH]),
                          waits=[(sV, outstage_done)], incs=[(sO, 16)])

            prev = dict(outstage_done=outstage_done,
                        states_copy=states_copy, gate_done=gate_hist_slice(gate_hist, k),
                        opmm=opmm,
                        rep_sy=rep_sy, rep_g=rep_g,
                        stat_rep=stat_rep, st_vals=st_vals,
                        ffn_add_last=ffn_add[NCT - 1],
                        bcb_free=("V", prog.count(sV)),
                        dtmm_last=dtmm_last,
                        dxc_hist=dxc_hist_slice(dxc_hist, k))

        total_stores = prog.count(sO)
        if enddump:
            allv, alls, allt, allg = prog.count(sV), prog.count(sS), prog.count(sT), prog.count(sG)
            srcs = dict(xch=x_ch[:, :],
                        negA=negA[:, :], dtw=dt_wT_bf[:, :], xproj=xprojT_bf[:, :],
                        wp=wp_bf[:, :], bcb=bcb[:, :], xn=xn_bf[:, :], xc=xc_bf[:, :],
                        zall=z_all[:, :], ygate=xcry[:, 0:NET * CH], states=states[:, :],
                        convw=convw[:, :], dtb=dtb[:, :], aB=aB[:, :], bB=bB[:, :],
                        drawT=drawT[:, :], eye=eye_bf[:, :])
            first = True
            for nm, src in srcs.items():
                wts = [(sO, total_stores), (sV, allv), (sS, alls), (sT, allt), (sG, allg)] if first else []
                first = False
                prog.emit("sync", lambda d=ed[nm], s=src: nc.sync.dma_start(out=d, in_=s),
                          waits=wts, incs=[(sO, 16)])
            total_stores = prog.count(sO)
        prog.emit("sync", lambda: nc.sync.nop(), waits=[(sO, total_stores)])

        with nc.Block() as block:
            @block.sync
            def _(eng):
                prog.replay("sync", eng)

            @block.vector
            def _(eng):
                prog.replay("vector", eng)

            @block.scalar
            def _(eng):
                prog.replay("scalar", eng)

            @block.tensor
            def _(eng):
                prog.replay("tensor", eng)

            @block.gpsimd
            def _(eng):
                prog.replay("gpsimd", eng)

    return nc


def gate_hist_slice(gate_hist, k):
    return {et: gate_hist[k * NET + et] for et in range(NET)}


def dxc_hist_slice(dxc_hist, k):
    return {et: dxc_hist[k * NET + et] for et in range(NET)}


_NC_CACHE = None


def _get_nc():
    global _NC_CACHE
    if _NC_CACHE is None:
        _NC_CACHE = build_program()
    return _NC_CACHE


def kernel(L0_attn_x, L0_attn_in_proj_w, L0_attn_conv_w, L0_attn_conv_b,
           L0_attn_x_proj_w, L0_attn_dt_w, L0_attn_dt_b, L0_attn_A_log,
           L0_attn_D_param, L0_attn_out_proj_w, L0_ffn_weight,
           L0_ffn_aux_weight1, L0_ffn_aux_weight2, L0_ffn_bias):
    x = np.asarray(L0_attn_x, dtype=np.float32)
    B = x.shape[0]
    f32 = lambda a: np.ascontiguousarray(np.asarray(a, dtype=np.float32))

    inprojT = f32(np.asarray(L0_attn_in_proj_w).T)
    outprojT = f32(np.asarray(L0_attn_out_proj_w).T)
    xprojT = f32(np.asarray(L0_attn_x_proj_w).T)
    dt_wT = f32(np.asarray(L0_attn_dt_w).T)
    w1T = f32(np.asarray(L0_ffn_weight).T)
    wa1T = f32(np.asarray(L0_ffn_aux_weight1).T)
    wa2 = f32(L0_ffn_aux_weight2)

    def chan_r(vec, nt):
        return f32(np.asarray(vec).reshape(nt, P).T)

    convw_r = f32(np.asarray(L0_attn_conv_w).reshape(NET, P, 4)
                  .transpose(1, 0, 2).reshape(P, 4 * NET))
    convb_r = chan_r(L0_attn_conv_b, NET)
    dtb_r = chan_r(L0_attn_dt_b, NET)
    alog_r = f32(np.asarray(L0_attn_A_log).reshape(NET, P, NST)
                 .transpose(1, 0, 2).reshape(P, NST * NET))
    d_r = chan_r(L0_attn_D_param, NET)
    fbias_r = chan_r(L0_ffn_bias, NCT)
    eye = np.eye(P, dtype=BF)
    ones = np.ones((P, 1), dtype=np.float32)

    shared = dict(inprojT=inprojT, outprojT=outprojT, xprojT=xprojT,
                  dt_wT=dt_wT, w1T=w1T, wa1T=wa1T, wa2=wa2,
                  convw_r=convw_r, convb_r=convb_r, dtb_r=dtb_r,
                  alog_r=alog_r, d_r=d_r, fbias_r=fbias_r,
                  eye_in=eye, ones_in=ones)

    in_maps = []
    for b in range(B):
        m = dict(shared)
        m["xT"] = np.ascontiguousarray(x[b].T)
        in_maps.append(m)

    nc = _get_nc()
    res = run_bass_kernel_spmd(nc, in_maps, core_ids=list(range(B)))
    out = np.stack([np.asarray(r["outT"]).T for r in res.results], axis=0)
    return out.astype(np.float32)


# version 09
# speedup vs baseline: 1.1566x; 1.0876x over previous
import numpy as np
import ml_dtypes
from contextlib import ExitStack

import concourse.bass as bass
import concourse.mybir as mybir
from concourse.ap import AP
from concourse.bass_utils import run_bass_kernel_spmd

F32 = mybir.dt.float32
BF16 = mybir.dt.bfloat16
AF = mybir.ActivationFunctionType
ALU = mybir.AluOpType

P = 128
L = 2048
CH = 512
NCH = L // CH
NET = 16
NCT = 8
NST = 16
DTR = 64
DM = 1024
DI = 2048
NG = 4
GN = 4

BF = ml_dtypes.bfloat16


class Prog:

    ENGINES = ("sync", "vector", "scalar", "tensor", "gpsimd")

    def __init__(self):
        self.items = {e: [] for e in self.ENGINES}
        self.counts = {}

    def emit(self, eng, fn, waits=(), incs=()):
        waits = [(s, v) for (s, v) in waits if v > 0]
        self.items[eng].append((fn, waits, list(incs)))
        post = None
        for (sem, amt) in incs:
            self.counts[id(sem)] = self.counts.get(id(sem), 0) + amt
            post = self.counts[id(sem)]
        return post

    def count(self, sem):
        return self.counts.get(id(sem), 0)

    def replay(self, eng, handle):
        for fn, waits, incs in self.items[eng]:
            for sem, val in waits:
                handle.wait_ge(sem, val)
            inst = fn()
            for sem, amt in incs:
                inst = inst.then_inc(sem, amt)


def build_program(debug=False, enddump=False):
    nc = bass.Bass("TRN2", target_bir_lowering=False)

    xT = nc.dram_tensor("xT", [DM, L], F32, kind="ExternalInput").ap()
    inprojT = nc.dram_tensor("inprojT", [DM, 2 * DI], F32, kind="ExternalInput").ap()
    outprojT = nc.dram_tensor("outprojT", [DI, DM], F32, kind="ExternalInput").ap()
    xprojT = nc.dram_tensor("xprojT", [DI, 96], F32, kind="ExternalInput").ap()
    dt_wT = nc.dram_tensor("dt_wT", [DTR, DI], F32, kind="ExternalInput").ap()
    w1T = nc.dram_tensor("w1T", [DM, DM], F32, kind="ExternalInput").ap()
    wa1T = nc.dram_tensor("wa1T", [DM, DM], F32, kind="ExternalInput").ap()
    wa2 = nc.dram_tensor("wa2", [DM, DM], F32, kind="ExternalInput").ap()
    convw_r = nc.dram_tensor("convw_r", [P, 4 * NET], F32, kind="ExternalInput").ap()
    convb_r = nc.dram_tensor("convb_r", [P, NET], F32, kind="ExternalInput").ap()
    dtb_r = nc.dram_tensor("dtb_r", [P, NET], F32, kind="ExternalInput").ap()
    alog_r = nc.dram_tensor("alog_r", [P, NST * NET], F32, kind="ExternalInput").ap()
    d_r = nc.dram_tensor("d_r", [P, NET], F32, kind="ExternalInput").ap()
    fbias_r = nc.dram_tensor("fbias_r", [P, NCT], F32, kind="ExternalInput").ap()
    eye_in = nc.dram_tensor("eye_in", [P, P], BF16, kind="ExternalInput").ap()
    ones_in = nc.dram_tensor("ones_in", [P, 1], F32, kind="ExternalInput").ap()

    outT = nc.dram_tensor("outT", [DM, L], F32, kind="ExternalOutput").ap()
    if enddump:
        ed = {}
        for nm, shp, dt in (("xch", [P, NCT * CH], F32),
                            ("negA", [P, NST * NET], F32), ("dtw", [DTR, DI], BF16),
                            ("xproj", [P, NET * 96], BF16), ("wp", [P, NCT * DM], BF16),
                            ("bcb", [P, 32 * CH], BF16), ("xn", [P, NCT * CH], BF16),
                            ("xc", [P, NET * CH], BF16), ("zall", [P, NET * CH], BF16),
                            ("ygate", [P, NET * CH], BF16), ("states", [P, NET * NST], F32),
                            ("convw", [P, 4 * NET], F32), ("dtb", [P, NET], F32),
                            ("aB", [P, CH], BF16), ("bB", [P, CH], BF16),
                            ("drawT", [DTR, CH], BF16), ("eye", [P, P], BF16)):
            ed[nm] = nc.dram_tensor("ed_" + nm, shp, dt, kind="ExternalOutput").ap()
    if debug:
        dbg_negA = nc.dram_tensor("dbg_negA", [P, NST * NET], F32, kind="ExternalOutput").ap()
        dbg_wp = nc.dram_tensor("dbg_wp", [P, NCT * DM], BF16, kind="ExternalOutput").ap()
        dbg_dtw = nc.dram_tensor("dbg_dtw", [DTR, DI], BF16, kind="ExternalOutput").ap()
        dbg_xproj = nc.dram_tensor("dbg_xproj", [P, NET * 96], BF16, kind="ExternalOutput").ap()
        dbg_xn = nc.dram_tensor("dbg_xn", [NCH, P, NCT * CH], BF16, kind="ExternalOutput").ap()
        dbg_xc = nc.dram_tensor("dbg_xc", [NCH, P, NET * CH], BF16, kind="ExternalOutput").ap()
        dbg_bcb = nc.dram_tensor("dbg_bcb", [NCH, P, 32 * CH], BF16, kind="ExternalOutput").ap()
        dbg_states = nc.dram_tensor("dbg_states", [NCH, P, NET * NST], F32, kind="ExternalOutput").ap()
        dbg_ygate = nc.dram_tensor("dbg_ygate", [NCH, P, NET * CH], BF16, kind="ExternalOutput").ap()
        dbg_z = nc.dram_tensor("dbg_z", [NCH, P, 2 * CH], BF16, kind="ExternalOutput").ap()

    scr_inproj_h = nc.dram_tensor("scr_inproj", [32, P, NCT * P], BF16, kind="Internal")
    scr_outproj_h = nc.dram_tensor("scr_outproj", [NCT, P, NET * P], BF16, kind="Internal")
    scr_inproj = scr_inproj_h.ap()
    scr_outproj = scr_outproj_h.ap()
    scr_bc = nc.dram_tensor("scr_bc", [32, CH], BF16, kind="Internal").ap()
    scr_stat = nc.dram_tensor("scr_stat", [2, CH], BF16, kind="Internal").ap()
    scr_wa2 = nc.dram_tensor("scr_wa2", [NCT, P, DM], BF16, kind="Internal").ap()

    prog = Prog()

    with ExitStack() as ctx:
        sbt = lambda name, shape, dt: ctx.enter_context(nc.sbuf_tensor(name, shape, dt))
        psb = lambda name, shape, dt: ctx.enter_context(nc.psum_tensor(name, shape, dt))
        sem = lambda name: ctx.enter_context(nc.semaphore(name))

        x_ch = sbt("x_ch", [P, NCT * CH], F32)
        xn_bf = sbt("xn_bf", [P, NCT * CH], BF16)
        xcry = sbt("xcry", [P, NET * 520], BF16)
        xc_bf = sbt("xc_bf", [P, NET * CH], BF16)
        z_all = sbt("z_all", [P, NET * CH], BF16)
        bcb = sbt("bcb", [P, 32 * CH], BF16)
        wp_bf = sbt("wp_bf", [P, NCT * DM], BF16)
        wstream = sbt("wstream", [P, 32 * P], BF16)
        u_roll = sbt("u_roll", [P, 2 * GN * CH], BF16)
        hs_roll = sbt("hs_roll", [P, 2 * GN * CH], BF16)
        tmp_roll = sbt("tmp_roll", [P, 2 * GN * CH], BF16)
        delta2 = sbt("delta2", [P, 2 * CH], F32)
        v2 = sbt("v2", [P, 2 * CH], BF16)
        dA = sbt("dA", [P, 4 * CH], F32)
        dxc = sbt("dxc", [P, 2 * CH], BF16)
        stage = sbt("stage", [P, 2 * CH], F32)
        outstg = sbt("outstg", [P, 2 * CH], F32)
        states = sbt("states", [P, NET * NST], F32)
        aB = sbt("aB", [P, CH], BF16)
        bB = sbt("bB", [P, CH], BF16)
        diag = sbt("diag", [P, 2 * 4 * P], BF16)
        drawT = sbt("drawT", [DTR, CH], BF16)
        bc_sb = sbt("bc_sb", [96, CH], BF16)
        halo = sbt("halo", [P, NET * 4], BF16)
        negA = sbt("negA", [P, NST * NET], F32)
        convw = sbt("convw", [P, 4 * NET], F32)
        convb = sbt("convb", [P, NET], F32)
        dtb = sbt("dtb", [P, NET], F32)
        dparm = sbt("dparm", [P, NET], F32)
        fbias = sbt("fbias", [P, NCT], F32)
        eye_bf = sbt("eye_bf", [P, P], BF16)
        ones = sbt("ones", [P, 1], F32)
        xprojT_bf = sbt("xprojT_bf", [P, NET * 96], BF16)
        dt_wT_bf = sbt("dt_wT_bf", [DTR, DI], BF16)
        r_sx = sbt("r_sx", [1, CH], F32)
        r_sx2 = sbt("r_sx2", [1, CH], F32)
        r_a = sbt("r_a", [1, CH], BF16)
        r_b = sbt("r_b", [1, CH], BF16)
        r_t = sbt("r_t", [1, CH], F32)

        mm_ps = psb("mm_ps", [P, 2 * CH], F32)
        conv_ps = psb("conv_ps", [P, 2 * CH], F32)
        y_ps = psb("y_ps", [P, 2 * CH], F32)
        st_ps = psb("st_ps", [P, NET * NST], F32)
        stat_ps = psb("stat_ps", [1, CH], F32)

        sL = sem("sL")
        sG = sem("sG")
        sD = sem("sD")
        sV = sem("sV")
        sS = sem("sS")
        sT = sem("sT")
        sO = sem("sO")

        T = nc.tensor
        V = nc.vector
        S = nc.scalar

        def sy_dma(dst, src, waits=()):
            return prog.emit("sync", lambda d=dst, s=src: nc.sync.dma_start(out=d, in_=s),
                             waits=waits, incs=[(sL, 16)])

        def g_dma(dst, src, waits=()):
            return prog.emit("gpsimd", lambda d=dst, s=src: nc.gpsimd.dma_start(out=d, in_=s),
                             waits=waits, incs=[(sG, 16)])

        def v_op(fn, waits=()):
            return prog.emit("vector", fn, waits=waits, incs=[(sV, 1)])

        def s_op(fn, waits=()):
            return prog.emit("scalar", fn, waits=waits, incs=[(sS, 1)])

        def t_op(fn, waits=()):
            return prog.emit("tensor", fn, waits=waits, incs=[(sT, 1)])

        SEMS = {"S": sS, "V": sV, "T": sT}

        def w(tag):
            eng, cnt = tag
            return (SEMS[eng], cnt)

        sy_dma(eye_bf[:, :], eye_in)
        sy_dma(ones[:, :], ones_in)
        sy_dma(convw[:, :], convw_r)
        sy_dma(convb[:, :], convb_r)
        sy_dma(dtb[:, :], dtb_r)
        sy_dma(dparm[:, :], d_r)
        sy_dma(fbias[:, :], fbias_r)
        w_alog = sy_dma(x_ch[:, 0:NST * NET], alog_r)
        w_dtw = sy_dma(x_ch[0:DTR, 256:256 + DI], dt_wT)
        w_xp = 0
        for kt in range(NET):
            w_xp = sy_dma(x_ch[:, 2304 + kt * 96:2304 + (kt + 1) * 96],
                          xprojT[kt * P:(kt + 1) * P, :])

        e1 = s_op(lambda: S.activation(stage[:, 0:NST * NET], x_ch[:, 0:NST * NET], AF.Exp),
                  waits=[(sL, w_xp)])
        nega_done = v_op(lambda: V.tensor_scalar(negA[:, :], stage[:, 0:NST * NET],
                                                 -1.0, None, ALU.mult),
                         waits=[(sS, e1)])
        dtw_cast = v_op(lambda: V.tensor_copy(dt_wT_bf[:, :], x_ch[0:DTR, 256:256 + DI]),
                        waits=[(sL, w_xp)])
        xproj_cast = v_op(lambda: V.tensor_copy(xprojT_bf[:, :],
                                                x_ch[:, 2304:2304 + NET * 96]),
                          waits=[(sL, w_xp)])
        v_op(lambda: V.memset(halo[:, :], 0.0))

        direct_w = 0
        for mt in range(32):
            src = AP(inprojT.tensor, mt * P, [[2 * DI, P], [P * 2 * DI, NCT], [1, P]])
            direct_w = prog.emit("gpsimd", lambda mt=mt, src=src: nc.gpsimd.dma_start(
                out=scr_inproj[mt], in_=src), incs=[(sD, 16)])
        for mt in range(NCT):
            src = AP(outprojT.tensor, mt * P, [[DM, P], [P * DM, NET], [1, P]])
            direct_w = prog.emit("gpsimd", lambda mt=mt, src=src: nc.gpsimd.dma_start(
                out=scr_outproj[mt], in_=src), incs=[(sD, 16)])

        lastv = prog.count(sV)
        for i in range(NCT):
            wld = sy_dma(x_ch[:, 0:DM], wa1T[i * P:(i + 1) * P, :],
                         waits=([(sV, lastv)] if i else [(sS, e1), (sV, dtw_cast),
                                                         (sV, xproj_cast)]))
            lastv = v_op(lambda i=i: V.tensor_copy(xc_bf[:, i * DM:(i + 1) * DM],
                                                   x_ch[:, 0:DM]),
                         waits=[(sL, wld)])
        last_wa2_w = 0
        for i in range(NCT):
            wld = sy_dma(x_ch[:, 0:DM], wa2[i * P:(i + 1) * P, :], waits=[(sV, lastv)])
            lastv = v_op(lambda i=i: V.tensor_copy(z_all[:, (i % 2) * CH:(i % 2) * CH + DM // 2],
                                                   x_ch[:, 0:DM // 2]),
                         waits=[(sL, wld)])
            lastv = v_op(lambda i=i: V.tensor_copy(z_all[:, (2 + i % 2) * CH:(2 + i % 2) * CH + DM // 2],
                                                   x_ch[:, DM // 2:DM]))
            last_wa2_w = sy_dma(scr_wa2[i, :, 0:DM // 2], z_all[:, (i % 2) * CH:(i % 2) * CH + DM // 2],
                                waits=[(sV, lastv)])
            last_wa2_w = sy_dma(scr_wa2[i, :, DM // 2:DM], z_all[:, (2 + i % 2) * CH:(2 + i % 2) * CH + DM // 2])
        for i in range(NCT):
            wld = sy_dma(x_ch[:, 0:DM], w1T[i * P:(i + 1) * P, :], waits=[(sV, lastv)])
            lastv = v_op(lambda i=i: V.tensor_copy(xcry[:, i * DM:(i + 1) * DM],
                                                   x_ch[:, 0:DM]),
                         waits=[(sL, wld)])
        cast_all = lastv

        wa2_ld = {}
        for kt in range(NCT):
            wa2_ld[kt] = g_dma(bcb[:, kt * DM:(kt + 1) * DM], scr_wa2[kt, :, :],
                               waits=[(sL, last_wa2_w)] if kt == 0 else ())
        T2B = 8 * DM
        for (lhs_buf, rhs_buf, rhs_base, out_buf, out_base, first_wait) in (
                (xc_bf, bcb, 0, bcb, T2B, cast_all),
                (xcry, bcb, T2B, wp_bf, 0, None)):
            for mt in range(NCT):
                for half in range(2):
                    mmv = 0
                    for kt in range(NCT):
                        if kt == 0:
                            if first_wait is not None and mt == 0 and half == 0:
                                wts = [(sV, first_wait), (sG, wa2_ld[NCT - 1])]
                            else:
                                wts = [(sV, prog.count(sV))]
                        else:
                            wts = []
                        mmv = t_op(lambda lb=lhs_buf, rb=rhs_buf, rb0=rhs_base, mt=mt, half=half, kt=kt:
                                   T.matmul(
                                       mm_ps[:, (mt % 2) * CH:(mt % 2) * CH + CH],
                                       lb[:, kt * DM + mt * P:kt * DM + (mt + 1) * P],
                                       rb[:, rb0 + kt * DM + half * CH:rb0 + kt * DM + half * CH + CH],
                                       start=(kt == 0), stop=(kt == NCT - 1)), waits=wts)
                    v_op(lambda ob=out_buf, ob0=out_base, mt=mt, half=half: V.tensor_copy(
                        ob[:, ob0 + mt * DM + half * CH:ob0 + mt * DM + half * CH + CH],
                        mm_ps[:, (mt % 2) * CH:(mt % 2) * CH + CH]),
                        waits=[(sT, mmv)])
        prep_mm_last = prog.count(sT)
        if debug:
            prog.emit("sync", lambda: nc.sync.dma_start(out=dbg_negA, in_=negA[:, :]),
                      waits=[(sV, nega_done)], incs=[(sO, 16)])
            prog.emit("sync", lambda: nc.sync.dma_start(out=dbg_dtw, in_=dt_wT_bf[:, :]),
                      waits=[(sV, dtw_cast)], incs=[(sO, 16)])
            prog.emit("sync", lambda: nc.sync.dma_start(out=dbg_xproj, in_=xprojT_bf[:, :]),
                      waits=[(sV, xproj_cast)], incs=[(sO, 16)])
            prog.emit("sync", lambda: nc.sync.dma_start(out=dbg_wp, in_=wp_bf[:, :]),
                      waits=[(sT, prep_mm_last), (sV, prog.count(sV))], incs=[(sO, 16)])

        prev = dict(outstage_done=0, states_copy=0,
                    gate_done={}, opmm={},
                    rep_sy=0, rep_g=0, stat_rep=0, ffn_add_last=0,
                    bcb_free=("T", prep_mm_last), dtmm_last=0,
                    dxc_hist={})

        pending_gate = [None]
        scan_hist = {}
        stmm_hist = {}
        foldl_hist = {}
        gate_hist = {}
        f17_hist = {}
        v2_hist = {}
        dxc_hist = {}

        for k in range(NCH):
            c0 = k * CH

            xload = 0
            for ct in range(NCT):
                if ct == 0:
                    xw = [(sV, prev["outstage_done"] if k > 0 else cast_all)]
                else:
                    xw = []
                xload = sy_dma(x_ch[:, ct * CH:(ct + 1) * CH],
                               xT[ct * P:(ct + 1) * P, c0:c0 + CH],
                               waits=xw)
            sxmm = 0
            for ct in range(NCT):
                sxmm = t_op(lambda ct=ct: T.matmul(
                    stat_ps[0:1, :], ones[:, :], x_ch[:, ct * CH:(ct + 1) * CH],
                    start=(ct == 0), stop=(ct == NCT - 1)),
                    waits=([(sL, xload)] if ct == 0 else []))
            cp_sx = v_op(lambda: V.tensor_copy(r_sx[:, :], stat_ps[0:1, :]),
                         waits=[(sT, sxmm)])
            sx2mm = 0
            for ct in range(NCT):
                sq = s_op(lambda ct=ct: S.activation(
                    stage[:, (ct % 2) * CH:(ct % 2) * CH + CH],
                    x_ch[:, ct * CH:(ct + 1) * CH], AF.Square),
                    waits=([(sL, xload), (sV, prev["ffn_add_last"])] if ct == 0 else []) +
                          ([(sT, sx2mm)] if ct >= 2 else []))
                sx2mm = t_op(lambda ct=ct: T.matmul(
                    stat_ps[0:1, :], ones[:, :], stage[:, (ct % 2) * CH:(ct % 2) * CH + CH],
                    start=(ct == 0), stop=(ct == NCT - 1)),
                    waits=[(sS, sq)] + ([(sV, cp_sx)] if ct == 0 else []))
            cp_sx2 = v_op(lambda: V.tensor_copy(r_sx2[:, :], stat_ps[0:1, :]),
                          waits=[(sT, sx2mm)])
            v_op(lambda: V.tensor_scalar(r_sx[:, :], r_sx[:, :], 1.0 / DM, None, ALU.mult))
            v_op(lambda: V.tensor_scalar(r_sx2[:, :], r_sx2[:, :], 1.0 / DM, None, ALU.mult))
            v_op(lambda: V.scalar_tensor_tensor(r_t[:, :], r_sx[:, :], -1.0,
                                                r_sx[:, :], ALU.mult, ALU.mult))
            v_op(lambda: V.tensor_tensor(r_sx2[:, :], r_sx2[:, :], r_t[:, :], ALU.add))
            vvar = v_op(lambda: V.tensor_scalar(r_sx2[:, :], r_sx2[:, :], 1e-5, None, ALU.add))
            s_op(lambda: S.activation(r_t[:, :], r_sx2[:, :], AF.Ln),
                 waits=[(sV, vvar)])
            rstd = s_op(lambda: S.activation(r_a[:, :], r_t[:, :], AF.Exp, scale=-0.5))
            rb = v_op(lambda: V.scalar_tensor_tensor(r_b[:, :], r_sx[:, :], -1.0,
                                                     r_a[:, :], ALU.mult, ALU.mult),
                      waits=[(sS, rstd)])
            wst = sy_dma(scr_stat[0:1, :], r_a[:, :],
                         waits=[(sV, rb), (sL, prev["stat_rep"])])
            wst = sy_dma(scr_stat[1:2, :], r_b[:, :])
            sy_dma(aB[:, :], scr_stat[0:1, :].broadcast_to([P, CH]), waits=[(sL, wst)])
            stat_rep = sy_dma(bB[:, :], scr_stat[1:2, :].broadcast_to([P, CH]))
            xn_done = 0
            for ct in range(NCT):
                v_op(lambda ct=ct: V.tensor_tensor(
                    outstg[:, (ct % 2) * CH:(ct % 2) * CH + CH],
                    x_ch[:, ct * CH:(ct + 1) * CH], aB[:, :], ALU.mult),
                    waits=([(sL, stat_rep), (sO, prog.count(sO))] if ct == 0 else []))
                xn_done = v_op(lambda ct=ct: V.tensor_tensor(
                    xn_bf[:, ct * CH:(ct + 1) * CH],
                    outstg[:, (ct % 2) * CH:(ct % 2) * CH + CH], bB[:, :], ALU.add))

            silu_done = {}
            z_done = {}
            convgrp_done = {}
            mmgrp_done = {}
            for mt in range(32):
                wts = [(sT, mmgrp_done.get(mt - 2, 0))]
                if k == 0 and mt < 2:
                    wts += [(sD, direct_w)]
                if k > 0 and mt < 2:
                    wts += [(sT, prev["opmm"].get(NCT - 1, 0))]
                wload = g_dma(wstream[:, (mt % 2) * NET * P:(mt % 2) * NET * P + NCT * P],
                              scr_inproj[mt, :, :], waits=wts)
                mmv = 0
                for kt in range(NCT):
                    wts = []
                    if kt == 0:
                        wts += [(sG, wload)]
                        if mt == 0:
                            wts += [(sV, xn_done)]
                        if mt - 2 >= 0:
                            if mt - 2 < NET:
                                wts += [(sV, prog.count(sV))]
                            else:
                                wts += [(sS, z_done.get(mt - 2 - NET, 0))]
                    mmv = t_op(lambda mt=mt, kt=kt: T.matmul(
                        mm_ps[:, (mt % 2) * CH:(mt % 2) * CH + CH],
                        wstream[:, (mt % 2) * NET * P + kt * P:(mt % 2) * NET * P + (kt + 1) * P],
                        xn_bf[:, kt * CH:(kt + 1) * CH],
                        start=(kt == 0), stop=(kt == NCT - 1)), waits=wts)
                mmgrp_done[mt] = mmv
                if mt < NET:
                    et = mt
                    base = et * 520
                    v_op(lambda et=et, base=base: V.tensor_copy(
                        xcry[:, base + 4:base + 8], halo[:, et * 4:(et + 1) * 4]),
                        waits=[(sV, prev["gate_done"].get(et, 0))])
                    v_op(lambda base=base, mt=mt: V.tensor_copy(
                        xcry[:, base + 8:base + 520],
                        mm_ps[:, (mt % 2) * CH:(mt % 2) * CH + CH]),
                        waits=[(sT, mmv)])
                    v_op(lambda et=et, mt=mt: V.tensor_copy(
                        halo[:, et * 4:(et + 1) * 4],
                        mm_ps[:, (mt % 2) * CH + 508:(mt % 2) * CH + 512]))
                    dgv = 0
                    for kk in range(4):
                        dgv = v_op(lambda et=et, kk=kk: V.tensor_scalar(
                            diag[:, (et % 2) * 4 * P + kk * P:(et % 2) * 4 * P + (kk + 1) * P],
                            eye_bf[:, :], convw[:, et * 4 + kk:et * 4 + kk + 1],
                            None, ALU.mult),
                            waits=([(sT, convgrp_done.get(et - 2, 0))] if kk == 0 else []))
                    cnv = 0
                    for kk in range(4):
                        wts = [(sV, dgv), (sS, silu_done.get(et - 2, 0))] if kk == 0 else []
                        cnv = t_op(lambda et=et, kk=kk, base=base: T.matmul(
                            conv_ps[:, (et % 2) * CH:(et % 2) * CH + CH],
                            diag[:, (et % 2) * 4 * P + kk * P:(et % 2) * 4 * P + (kk + 1) * P],
                            xcry[:, base + 5 + kk:base + 5 + kk + CH],
                            start=(kk == 0), stop=(kk == 3)), waits=wts)
                    convgrp_done[et] = cnv
                    silu_done[et] = s_op(lambda et=et: S.activation(
                        xc_bf[:, et * CH:(et + 1) * CH],
                        conv_ps[:, (et % 2) * CH:(et % 2) * CH + CH],
                        AF.Silu, bias=convb[:, et:et + 1]),
                        waits=[(sT, cnv),
                               (sV, prev["dxc_hist"].get(et, 0))])
                else:
                    et = mt - NET
                    z_done[et] = s_op(lambda et=et, mt=mt: S.activation(
                        z_all[:, et * CH:(et + 1) * CH],
                        mm_ps[:, (mt % 2) * CH:(mt % 2) * CH + CH],
                        AF.Silu),
                        waits=[(sT, mmv),
                               (sV, prev["gate_done"].get(et, 0))])

            mmfree = {0: ("S", z_done[14]), 1: ("S", z_done[15])}

            xpv = 0
            for kt in range(NET):
                wts = [(sS, silu_done[kt])]
                if kt == 0:
                    wts += [(sS, silu_done[NET - 1])]
                xpv = t_op(lambda kt=kt: T.matmul(
                    conv_ps[0:96, 0:CH],
                    xprojT_bf[:, kt * 96:(kt + 1) * 96],
                    xc_bf[:, kt * CH:(kt + 1) * CH],
                    start=(kt == 0), stop=(kt == NET - 1)), waits=wts)
            draw_done = s_op(lambda: S.activation(drawT[:, :], conv_ps[0:DTR, 0:CH],
                                                  AF.Copy),
                             waits=[(sT, xpv), (sT, prev["dtmm_last"])])
            bc_done = v_op(lambda: V.tensor_copy(bc_sb[DTR:96, :], conv_ps[DTR:96, 0:CH]),
                           waits=[(sT, xpv)])
            bc_w = sy_dma(scr_bc[:, :], bc_sb[DTR:96, :],
                          waits=[(sV, bc_done), (sL, prev["rep_sy"]),
                                 (sG, prev["rep_g"])])
            rep_sy = 0
            rep_g = 0
            for n in range(NST):
                wts = ([(sL, bc_w), w(prev["bcb_free"])] if n == 0 else [])
                rep_sy = sy_dma(bcb[:, n * CH:(n + 1) * CH],
                                scr_bc[n:n + 1, :].broadcast_to([P, CH]), waits=wts)
                rep_g = g_dma(bcb[:, (16 + n) * CH:(17 + n) * CH],
                              scr_bc[16 + n:17 + n, :].broadcast_to([P, CH]), waits=wts)

            dtmm_last = 0
            for et in range(NET):
                etg = k * NET + et
                par = et % 2
                dtmm = t_op(lambda et=et, par=par: T.matmul(
                    mm_ps[:, par * CH:par * CH + CH],
                    dt_wT_bf[:, et * P:(et + 1) * P],
                    drawT[:, :], start=True, stop=True),
                    waits=[w(mmfree[par]), (sS, draw_done)])
                dtmm_last = dtmm
                s_op(lambda et=et, par=par: S.activation(
                    stage[:, par * CH:par * CH + CH],
                    mm_ps[:, par * CH:par * CH + CH],
                    AF.Exp, bias=dtb[:, et:et + 1]),
                    waits=[(sT, dtmm)])
                sp = s_op(lambda par=par: S.activation(
                    delta2[:, par * CH:par * CH + CH],
                    stage[:, par * CH:par * CH + CH],
                    AF.Ln, bias=1.0),
                    waits=[(sV, v2_hist.get(etg - 2, 0))])
                mmfree[par] = ("S", sp)
                v2d = v_op(lambda par=par, et=et: V.tensor_tensor(
                    v2[:, par * CH:par * CH + CH],
                    delta2[:, par * CH:par * CH + CH],
                    xc_bf[:, et * CH:(et + 1) * CH], ALU.mult),
                    waits=[(sS, sp)])
                v2_hist[etg] = v2d

                for g in range(NG):
                    gg = etg * NG + g
                    gp = gg % 2
                    if g == 1 and pending_gate[0] is not None:
                        pending_gate[0]()
                        pending_gate[0] = None
                    uw = []
                    if et == 0 and g == 0:
                        uw = [(sL, rep_sy), (sG, rep_g)]
                    v_op(lambda par=par, g=g, gp=gp: V.tensor_tensor(
                        out=AP(u_roll, gp * GN * CH, [[2 * GN * CH, P], [CH, GN], [1, CH]]),
                        in0=AP(v2, par * CH, [[2 * CH, P], [0, GN], [1, CH]]),
                        in1=AP(bcb, g * GN * CH, [[32 * CH, P], [CH, GN], [1, CH]]),
                        op=ALU.mult), waits=uw)
                    for n4 in range(GN):
                        n = g * GN + n4
                        i = et * NST + n
                        gi = k * 256 + i
                        ew = [(sV, scan_hist.get(gi - 4, 0))]
                        if gi == 0:
                            ew += [(sV, nega_done)]
                        expd = s_op(lambda par=par, i=i: S.activation(
                            dA[:, (i % 4) * CH:(i % 4) * CH + CH],
                            delta2[:, par * CH:par * CH + CH],
                            AF.Exp, scale=negA[:, i:i + 1]), waits=ew)
                        init = 0.0 if k == 0 else states[:, i:i + 1]
                        sw = [(sS, expd)]
                        if n4 == 0:
                            sw += [(sT, stmm_hist.get(gg - 2, 0))]
                        scan_hist[gi] = v_op(lambda gp=gp, n4=n4, i=i, init=init:
                                             V.tensor_tensor_scan(
                            hs_roll[:, gp * GN * CH + n4 * CH:gp * GN * CH + (n4 + 1) * CH],
                            dA[:, (i % 4) * CH:(i % 4) * CH + CH],
                            u_roll[:, gp * GN * CH + n4 * CH:gp * GN * CH + (n4 + 1) * CH],
                            init, ALU.mult, ALU.add), waits=sw)
                    stw = [(sV, scan_hist[k * 256 + et * NST + g * GN + GN - 1])]
                    if et == 0 and g == 0:
                        stw += [(sV, prev["states_copy"])]
                    stmm_hist[gg] = t_op(lambda et=et, g=g, gp=gp: T.matmul(
                        st_ps[:, et * NST + g * GN:et * NST + (g + 1) * GN],
                        eye_bf[:, :],
                        AP(hs_roll, gp * GN * CH + CH - 1, [[2 * GN * CH, P], [CH, GN]]),
                        start=True, stop=True), waits=stw)
                    tw = [(sT, foldl_hist.get(gg - 2, 0))]
                    if et == 0 and g == 0:
                        tw += [(sG, rep_g)]
                    tmpd = v_op(lambda g=g, gp=gp: V.tensor_tensor(
                        out=AP(tmp_roll, gp * GN * CH, [[2 * GN * CH, P], [CH, GN], [1, CH]]),
                        in0=AP(hs_roll, gp * GN * CH, [[2 * GN * CH, P], [CH, GN], [1, CH]]),
                        in1=AP(bcb, (16 + g * GN) * CH, [[32 * CH, P], [CH, GN], [1, CH]]),
                        op=ALU.mult), waits=tw)
                    fold = 0
                    for j in range(GN):
                        fw = []
                        if j == 0:
                            fw += [(sV, tmpd)]
                            if g == 0:
                                fw += [(sV, gate_hist.get(etg - 2, 0))]
                        fold = t_op(lambda par=par, gp=gp, j=j, g=g: T.matmul(
                            y_ps[:, par * CH:par * CH + CH],
                            eye_bf[:, :],
                            tmp_roll[:, gp * GN * CH + j * CH:gp * GN * CH + (j + 1) * CH],
                            start=(g == 0 and j == 0), stop=False), waits=fw)
                    foldl_hist[gg] = fold

                dxcv = v_op(lambda et=et, par=par: V.tensor_scalar(
                    dxc[:, par * CH:par * CH + CH],
                    xc_bf[:, et * CH:(et + 1) * CH],
                    dparm[:, et:et + 1], None, ALU.mult),
                    waits=[(sT, f17_hist.get(etg - 2, 0))])
                dxc_hist[etg] = dxcv
                f17 = t_op(lambda par=par: T.matmul(
                    y_ps[:, par * CH:par * CH + CH],
                    eye_bf[:, :],
                    dxc[:, par * CH:par * CH + CH],
                    start=False, stop=True),
                    waits=[(sV, dxcv)])
                f17_hist[etg] = f17

                def emit_gate(et=et, par=par, etg=etg, f17=f17, zd=z_done[et]):
                    gate_hist[etg] = v_op(lambda et=et, par=par: V.tensor_tensor(
                        xcry[:, et * CH:(et + 1) * CH],
                        y_ps[:, par * CH:par * CH + CH],
                        z_all[:, et * CH:(et + 1) * CH], ALU.mult),
                        waits=[(sT, f17), (sS, zd)])
                pending_gate[0] = emit_gate
            if pending_gate[0] is not None:
                pending_gate[0]()
                pending_gate[0] = None
            states_copy = v_op(lambda: V.tensor_copy(states[:, :], st_ps[:, :]),
                               waits=[(sT, stmm_hist[(k * NET + NET - 1) * NG + NG - 1])])

            if debug:
                prog.emit("sync", lambda k=k: nc.sync.dma_start(out=dbg_xn[k], in_=xn_bf[:, :]),
                          waits=[(sV, xn_done)], incs=[(sO, 16)])
                prog.emit("sync", lambda k=k: nc.sync.dma_start(out=dbg_xc[k], in_=xc_bf[:, :]),
                          waits=[(sS, silu_done[NET - 1])], incs=[(sO, 16)])
                prog.emit("sync", lambda k=k: nc.sync.dma_start(out=dbg_bcb[k], in_=bcb[:, :]),
                          waits=[(sL, rep_sy), (sG, rep_g)], incs=[(sO, 16)])
                prog.emit("sync", lambda k=k: nc.sync.dma_start(out=dbg_states[k], in_=states[:, :]),
                          waits=[(sV, states_copy)], incs=[(sO, 16)])
                prog.emit("sync", lambda k=k: nc.sync.dma_start(out=dbg_ygate[k], in_=xcry[:, 0:NET * CH]),
                          waits=[(sV, gate_hist[k * NET + NET - 1])], incs=[(sO, 16)])
                prog.emit("sync", lambda k=k: nc.sync.dma_start(out=dbg_z[k], in_=z_all[:, 0:2 * CH]),
                          waits=[(sS, z_done[NET - 1])], incs=[(sO, 16)])

            ffn_add = {}
            ffn_th = {}
            for ct in range(NCT):
                mmv = 0
                for kt in range(NCT):
                    wts = []
                    if kt == 0:
                        wts += [w(mmfree[ct % 2])]
                        if ct == 0:
                            wts += [(sV, xn_done)]
                    mmv = t_op(lambda ct=ct, kt=kt: T.matmul(
                        mm_ps[:, (ct % 2) * CH:(ct % 2) * CH + CH],
                        wp_bf[:, kt * DM + ct * P:kt * DM + (ct + 1) * P],
                        xn_bf[:, kt * CH:(kt + 1) * CH],
                        start=(kt == 0), stop=(kt == NCT - 1)), waits=wts)
                ffn_th[ct] = s_op(lambda ct=ct: S.activation(
                    stage[:, (ct % 2) * CH:(ct % 2) * CH + CH],
                    mm_ps[:, (ct % 2) * CH:(ct % 2) * CH + CH],
                    AF.Tanh, bias=fbias[:, ct:ct + 1]),
                    waits=[(sT, mmv)] + ([(sV, ffn_add[ct - 2])] if ct >= 2 else []))
                mmfree[ct % 2] = ("S", ffn_th[ct])
                ffn_add[ct] = v_op(lambda ct=ct: V.tensor_tensor(
                    x_ch[:, ct * CH:(ct + 1) * CH],
                    x_ch[:, ct * CH:(ct + 1) * CH],
                    stage[:, (ct % 2) * CH:(ct % 2) * CH + CH], ALU.add),
                    waits=[(sS, ffn_th[ct])])

            outstage_done = 0
            opmm = {}
            st_vals = dict(prev.get("st_vals", {}))
            for mt in range(NCT):
                wts = [(sT, opmm.get(mt - 1, 0))]
                if mt == 0:
                    wts += [(sT, mmgrp_done[31])]
                    if k == 0:
                        wts += [(sD, direct_w)]
                wload = g_dma(wstream[:, 0:NET * P],
                              scr_outproj[mt, :, :], waits=wts)
                mmv = 0
                for kt in range(NET):
                    wts = []
                    if kt == 0:
                        wts += [(sG, wload), (sV, gate_hist[k * NET + NET - 1]),
                                w(mmfree[mt % 2])]
                        if mt >= 2:
                            wts += [(sV, prog.count(sV))]
                    mmv = t_op(lambda mt=mt, kt=kt: T.matmul(
                        mm_ps[:, (mt % 2) * CH:(mt % 2) * CH + CH],
                        wstream[:, kt * P:(kt + 1) * P],
                        xcry[:, kt * CH:(kt + 1) * CH],
                        start=(kt == 0), stop=(kt == NET - 1)), waits=wts)
                opmm[mt] = mmv
                outstage_done = v_op(lambda mt=mt: V.tensor_tensor(
                    outstg[:, (mt % 2) * CH:(mt % 2) * CH + CH],
                    mm_ps[:, (mt % 2) * CH:(mt % 2) * CH + CH],
                    x_ch[:, mt * CH:(mt + 1) * CH], ALU.add),
                    waits=[(sT, mmv), (sV, ffn_add[mt]),
                           (sO, st_vals.get(mt - 2, st_vals.get(6 + mt, 0)))])
                mmfree[mt % 2] = ("V", outstage_done)
                st_vals[mt] = prog.emit("sync",
                          lambda mt=mt, c0=c0: nc.sync.dma_start(
                              out=outT[mt * P:(mt + 1) * P, c0:c0 + CH],
                              in_=outstg[:, (mt % 2) * CH:(mt % 2) * CH + CH]),
                          waits=[(sV, outstage_done)], incs=[(sO, 16)])

            prev = dict(outstage_done=outstage_done,
                        states_copy=states_copy, gate_done=gate_hist_slice(gate_hist, k),
                        opmm=opmm,
                        rep_sy=rep_sy, rep_g=rep_g,
                        stat_rep=stat_rep, st_vals=st_vals,
                        ffn_add_last=ffn_add[NCT - 1],
                        bcb_free=("V", prog.count(sV)),
                        dtmm_last=dtmm_last,
                        dxc_hist=dxc_hist_slice(dxc_hist, k))

        total_stores = prog.count(sO)
        if enddump:
            allv, alls, allt, allg = prog.count(sV), prog.count(sS), prog.count(sT), prog.count(sG)
            srcs = dict(xch=x_ch[:, :],
                        negA=negA[:, :], dtw=dt_wT_bf[:, :], xproj=xprojT_bf[:, :],
                        wp=wp_bf[:, :], bcb=bcb[:, :], xn=xn_bf[:, :], xc=xc_bf[:, :],
                        zall=z_all[:, :], ygate=xcry[:, 0:NET * CH], states=states[:, :],
                        convw=convw[:, :], dtb=dtb[:, :], aB=aB[:, :], bB=bB[:, :],
                        drawT=drawT[:, :], eye=eye_bf[:, :])
            first = True
            for nm, src in srcs.items():
                wts = [(sO, total_stores), (sV, allv), (sS, alls), (sT, allt), (sG, allg)] if first else []
                first = False
                prog.emit("sync", lambda d=ed[nm], s=src: nc.sync.dma_start(out=d, in_=s),
                          waits=wts, incs=[(sO, 16)])
            total_stores = prog.count(sO)
        prog.emit("sync", lambda: nc.sync.nop(), waits=[(sO, total_stores)])

        with nc.Block() as block:
            @block.sync
            def _(eng):
                prog.replay("sync", eng)

            @block.vector
            def _(eng):
                prog.replay("vector", eng)

            @block.scalar
            def _(eng):
                prog.replay("scalar", eng)

            @block.tensor
            def _(eng):
                prog.replay("tensor", eng)

            @block.gpsimd
            def _(eng):
                prog.replay("gpsimd", eng)

    return nc


def gate_hist_slice(gate_hist, k):
    return {et: gate_hist[k * NET + et] for et in range(NET)}


def dxc_hist_slice(dxc_hist, k):
    return {et: dxc_hist[k * NET + et] for et in range(NET)}


_NC_CACHE = None


def _get_nc():
    global _NC_CACHE
    if _NC_CACHE is None:
        _NC_CACHE = build_program()
    return _NC_CACHE


def kernel(L0_attn_x, L0_attn_in_proj_w, L0_attn_conv_w, L0_attn_conv_b,
           L0_attn_x_proj_w, L0_attn_dt_w, L0_attn_dt_b, L0_attn_A_log,
           L0_attn_D_param, L0_attn_out_proj_w, L0_ffn_weight,
           L0_ffn_aux_weight1, L0_ffn_aux_weight2, L0_ffn_bias):
    x = np.asarray(L0_attn_x, dtype=np.float32)
    B = x.shape[0]
    f32 = lambda a: np.ascontiguousarray(np.asarray(a, dtype=np.float32))

    inprojT = f32(np.asarray(L0_attn_in_proj_w).T)
    outprojT = f32(np.asarray(L0_attn_out_proj_w).T)
    xprojT = f32(np.asarray(L0_attn_x_proj_w).T)
    dt_wT = f32(np.asarray(L0_attn_dt_w).T)
    w1T = f32(np.asarray(L0_ffn_weight).T)
    wa1T = f32(np.asarray(L0_ffn_aux_weight1).T)
    wa2 = f32(L0_ffn_aux_weight2)

    def chan_r(vec, nt):
        return f32(np.asarray(vec).reshape(nt, P).T)

    convw_r = f32(np.asarray(L0_attn_conv_w).reshape(NET, P, 4)
                  .transpose(1, 0, 2).reshape(P, 4 * NET))
    convb_r = chan_r(L0_attn_conv_b, NET)
    dtb_r = chan_r(L0_attn_dt_b, NET)
    alog_r = f32(np.asarray(L0_attn_A_log).reshape(NET, P, NST)
                 .transpose(1, 0, 2).reshape(P, NST * NET))
    d_r = chan_r(L0_attn_D_param, NET)
    fbias_r = chan_r(L0_ffn_bias, NCT)
    eye = np.eye(P, dtype=BF)
    ones = np.ones((P, 1), dtype=np.float32)

    shared = dict(inprojT=inprojT, outprojT=outprojT, xprojT=xprojT,
                  dt_wT=dt_wT, w1T=w1T, wa1T=wa1T, wa2=wa2,
                  convw_r=convw_r, convb_r=convb_r, dtb_r=dtb_r,
                  alog_r=alog_r, d_r=d_r, fbias_r=fbias_r,
                  eye_in=eye, ones_in=ones)

    in_maps = []
    for b in range(B):
        m = dict(shared)
        m["xT"] = np.ascontiguousarray(x[b].T)
        in_maps.append(m)

    nc = _get_nc()
    res = run_bass_kernel_spmd(nc, in_maps, core_ids=list(range(B)))
    out = np.stack([np.asarray(r["outT"]).T for r in res.results], axis=0)
    return out.astype(np.float32)


# version 10
# speedup vs baseline: 1.1709x; 1.0124x over previous
import numpy as np
import ml_dtypes
from contextlib import ExitStack

import concourse.bass as bass
import concourse.mybir as mybir
from concourse.ap import AP
from concourse.bass_utils import run_bass_kernel_spmd

F32 = mybir.dt.float32
BF16 = mybir.dt.bfloat16
AF = mybir.ActivationFunctionType
ALU = mybir.AluOpType

P = 128
L = 2048
CH = 512
NCH = L // CH
NET = 16
NCT = 8
NST = 16
DTR = 64
DM = 1024
DI = 2048
NG = 4
GN = 4

BF = ml_dtypes.bfloat16


class Prog:

    ENGINES = ("sync", "vector", "scalar", "tensor", "gpsimd")

    def __init__(self):
        self.items = {e: [] for e in self.ENGINES}
        self.counts = {}

    def emit(self, eng, fn, waits=(), incs=()):
        waits = [(s, v) for (s, v) in waits if v > 0]
        self.items[eng].append((fn, waits, list(incs)))
        post = None
        for (sem, amt) in incs:
            self.counts[id(sem)] = self.counts.get(id(sem), 0) + amt
            post = self.counts[id(sem)]
        return post

    def count(self, sem):
        return self.counts.get(id(sem), 0)

    def replay(self, eng, handle):
        for fn, waits, incs in self.items[eng]:
            for sem, val in waits:
                handle.wait_ge(sem, val)
            inst = fn()
            for sem, amt in incs:
                inst = inst.then_inc(sem, amt)


def build_program(debug=False, enddump=False):
    nc = bass.Bass("TRN2", target_bir_lowering=False)

    xT = nc.dram_tensor("xT", [DM, L], F32, kind="ExternalInput").ap()
    inprojT = nc.dram_tensor("inprojT", [DM, 2 * DI], F32, kind="ExternalInput").ap()
    outprojT = nc.dram_tensor("outprojT", [DI, DM], F32, kind="ExternalInput").ap()
    xprojT = nc.dram_tensor("xprojT", [DI, 96], F32, kind="ExternalInput").ap()
    dt_wT = nc.dram_tensor("dt_wT", [DTR, DI], F32, kind="ExternalInput").ap()
    w1T = nc.dram_tensor("w1T", [DM, DM], F32, kind="ExternalInput").ap()
    wa1T = nc.dram_tensor("wa1T", [DM, DM], F32, kind="ExternalInput").ap()
    wa2 = nc.dram_tensor("wa2", [DM, DM], F32, kind="ExternalInput").ap()
    convw_r = nc.dram_tensor("convw_r", [P, 4 * NET], F32, kind="ExternalInput").ap()
    convb_r = nc.dram_tensor("convb_r", [P, NET], F32, kind="ExternalInput").ap()
    dtb_r = nc.dram_tensor("dtb_r", [P, NET], F32, kind="ExternalInput").ap()
    alog_r = nc.dram_tensor("alog_r", [P, NST * NET], F32, kind="ExternalInput").ap()
    d_r = nc.dram_tensor("d_r", [P, NET], F32, kind="ExternalInput").ap()
    fbias_r = nc.dram_tensor("fbias_r", [P, NCT], F32, kind="ExternalInput").ap()
    eye_in = nc.dram_tensor("eye_in", [P, P], BF16, kind="ExternalInput").ap()
    ones_in = nc.dram_tensor("ones_in", [P, 1], F32, kind="ExternalInput").ap()

    outT = nc.dram_tensor("outT", [DM, L], F32, kind="ExternalOutput").ap()
    if enddump:
        ed = {}
        for nm, shp, dt in (("xch", [P, NCT * CH], F32),
                            ("negA", [P, NST * NET], F32), ("dtw", [DTR, DI], BF16),
                            ("xproj", [P, NET * 96], BF16), ("wp", [P, NCT * DM], BF16),
                            ("bcb", [P, 32 * CH], BF16), ("xn", [P, NCT * CH], BF16),
                            ("xc", [P, NET * CH], BF16), ("zall", [P, NET * CH], BF16),
                            ("ygate", [P, NET * CH], BF16), ("states", [P, NET * NST], F32),
                            ("convw", [P, 4 * NET], F32), ("dtb", [P, NET], F32),
                            ("aB", [P, CH], BF16), ("bB", [P, CH], BF16),
                            ("drawT", [DTR, CH], BF16), ("eye", [P, P], BF16)):
            ed[nm] = nc.dram_tensor("ed_" + nm, shp, dt, kind="ExternalOutput").ap()
    if debug:
        dbg_negA = nc.dram_tensor("dbg_negA", [P, NST * NET], F32, kind="ExternalOutput").ap()
        dbg_wp = nc.dram_tensor("dbg_wp", [P, NCT * DM], BF16, kind="ExternalOutput").ap()
        dbg_dtw = nc.dram_tensor("dbg_dtw", [DTR, DI], BF16, kind="ExternalOutput").ap()
        dbg_xproj = nc.dram_tensor("dbg_xproj", [P, NET * 96], BF16, kind="ExternalOutput").ap()
        dbg_xn = nc.dram_tensor("dbg_xn", [NCH, P, NCT * CH], BF16, kind="ExternalOutput").ap()
        dbg_xc = nc.dram_tensor("dbg_xc", [NCH, P, NET * CH], BF16, kind="ExternalOutput").ap()
        dbg_bcb = nc.dram_tensor("dbg_bcb", [NCH, P, 32 * CH], BF16, kind="ExternalOutput").ap()
        dbg_states = nc.dram_tensor("dbg_states", [NCH, P, NET * NST], F32, kind="ExternalOutput").ap()
        dbg_ygate = nc.dram_tensor("dbg_ygate", [NCH, P, NET * CH], BF16, kind="ExternalOutput").ap()
        dbg_z = nc.dram_tensor("dbg_z", [NCH, P, 2 * CH], BF16, kind="ExternalOutput").ap()

    scr_inproj_h = nc.dram_tensor("scr_inproj", [32, P, NCT * P], BF16, kind="Internal")
    scr_outproj_h = nc.dram_tensor("scr_outproj", [NCT, P, NET * P], BF16, kind="Internal")
    scr_inproj = scr_inproj_h.ap()
    scr_outproj = scr_outproj_h.ap()
    scr_bc = nc.dram_tensor("scr_bc", [32, CH], BF16, kind="Internal").ap()
    scr_stat = nc.dram_tensor("scr_stat", [2, CH], BF16, kind="Internal").ap()

    prog = Prog()

    with ExitStack() as ctx:
        sbt = lambda name, shape, dt: ctx.enter_context(nc.sbuf_tensor(name, shape, dt))
        psb = lambda name, shape, dt: ctx.enter_context(nc.psum_tensor(name, shape, dt))
        sem = lambda name: ctx.enter_context(nc.semaphore(name))

        x_ch = sbt("x_ch", [P, NCT * CH], F32)
        xn_bf = sbt("xn_bf", [P, NCT * CH], BF16)
        xcry = sbt("xcry", [P, NET * 520], BF16)
        xc_bf = sbt("xc_bf", [P, NET * CH], BF16)
        z_all = sbt("z_all", [P, NET * CH], BF16)
        bcb = sbt("bcb", [P, 32 * CH], BF16)
        wp_bf = sbt("wp_bf", [P, NCT * DM], BF16)
        wstream = sbt("wstream", [P, 32 * P], BF16)
        u_roll = sbt("u_roll", [P, 2 * GN * CH], BF16)
        hs_roll = sbt("hs_roll", [P, 2 * GN * CH], BF16)
        tmp_roll = sbt("tmp_roll", [P, 2 * GN * CH], BF16)
        delta2 = sbt("delta2", [P, 2 * CH], F32)
        v2 = sbt("v2", [P, 2 * CH], BF16)
        dA = sbt("dA", [P, 4 * CH], F32)
        dxc = sbt("dxc", [P, 2 * CH], BF16)
        stage = sbt("stage", [P, 2 * CH], F32)
        outstg = sbt("outstg", [P, 2 * CH], F32)
        states = sbt("states", [P, NET * NST], F32)
        aB = sbt("aB", [P, CH], BF16)
        bB = sbt("bB", [P, CH], BF16)
        diag = sbt("diag", [P, 2 * 4 * P], BF16)
        drawT = sbt("drawT", [DTR, CH], BF16)
        bc_sb = sbt("bc_sb", [96, CH], BF16)
        halo = sbt("halo", [P, NET * 4], BF16)
        negA = sbt("negA", [P, NST * NET], F32)
        convw = sbt("convw", [P, 4 * NET], F32)
        convb = sbt("convb", [P, NET], F32)
        dtb = sbt("dtb", [P, NET], F32)
        dparm = sbt("dparm", [P, NET], F32)
        fbias = sbt("fbias", [P, NCT], F32)
        eye_bf = sbt("eye_bf", [P, P], BF16)
        ones = sbt("ones", [P, 1], F32)
        xprojT_bf = sbt("xprojT_bf", [P, NET * 96], BF16)
        dt_wT_bf = sbt("dt_wT_bf", [DTR, DI], BF16)
        r_sx = sbt("r_sx", [1, CH], F32)
        r_sx2 = sbt("r_sx2", [1, CH], F32)
        r_a = sbt("r_a", [1, CH], BF16)
        r_b = sbt("r_b", [1, CH], BF16)
        r_t = sbt("r_t", [1, CH], F32)

        mm_ps = psb("mm_ps", [P, 2 * CH], F32)
        conv_ps = psb("conv_ps", [P, 2 * CH], F32)
        y_ps = psb("y_ps", [P, 2 * CH], F32)
        st_ps = psb("st_ps", [P, NET * NST], F32)
        stat_ps = psb("stat_ps", [1, CH], F32)

        sL = sem("sL")
        sG = sem("sG")
        sD = sem("sD")
        sE = sem("sE")
        sV = sem("sV")
        sS = sem("sS")
        sT = sem("sT")
        sO = sem("sO")

        T = nc.tensor
        V = nc.vector
        S = nc.scalar

        def sy_dma(dst, src, waits=()):
            return prog.emit("sync", lambda d=dst, s=src: nc.sync.dma_start(out=d, in_=s),
                             waits=waits, incs=[(sL, 16)])

        def g_dma(dst, src, waits=()):
            return prog.emit("gpsimd", lambda d=dst, s=src: nc.gpsimd.dma_start(out=d, in_=s),
                             waits=waits, incs=[(sG, 16)])

        def v_op(fn, waits=()):
            return prog.emit("vector", fn, waits=waits, incs=[(sV, 1)])

        def s_op(fn, waits=()):
            return prog.emit("scalar", fn, waits=waits, incs=[(sS, 1)])

        def t_op(fn, waits=()):
            return prog.emit("tensor", fn, waits=waits, incs=[(sT, 1)])

        SEMS = {"S": sS, "V": sV, "T": sT}

        def w(tag):
            eng, cnt = tag
            return (SEMS[eng], cnt)

        sy_dma(eye_bf[:, :], eye_in)
        sy_dma(ones[:, :], ones_in)
        sy_dma(convw[:, :], convw_r)
        sy_dma(convb[:, :], convb_r)
        sy_dma(dtb[:, :], dtb_r)
        sy_dma(dparm[:, :], d_r)
        sy_dma(fbias[:, :], fbias_r)
        w_alog = sy_dma(x_ch[:, 0:NST * NET], alog_r)
        w_dtw = sy_dma(x_ch[0:DTR, 256:256 + DI], dt_wT)
        w_xp = 0
        for kt in range(NET):
            w_xp = sy_dma(x_ch[:, 2304 + kt * 96:2304 + (kt + 1) * 96],
                          xprojT[kt * P:(kt + 1) * P, :])

        e1 = s_op(lambda: S.activation(stage[:, 0:NST * NET], x_ch[:, 0:NST * NET], AF.Exp),
                  waits=[(sL, w_xp)])
        nega_done = v_op(lambda: V.tensor_scalar(negA[:, :], stage[:, 0:NST * NET],
                                                 -1.0, None, ALU.mult),
                         waits=[(sS, e1)])
        dtw_cast = v_op(lambda: V.tensor_copy(dt_wT_bf[:, :], x_ch[0:DTR, 256:256 + DI]),
                        waits=[(sL, w_xp)])
        xproj_cast = v_op(lambda: V.tensor_copy(xprojT_bf[:, :],
                                                x_ch[:, 2304:2304 + NET * 96]),
                          waits=[(sL, w_xp)])
        v_op(lambda: V.memset(halo[:, :], 0.0))

        direct_w = 0
        for mt in range(32):
            src = AP(inprojT.tensor, mt * P, [[2 * DI, P], [P * 2 * DI, NCT], [1, P]])
            direct_w = prog.emit("gpsimd", lambda mt=mt, src=src: nc.gpsimd.dma_start(
                out=scr_inproj[mt], in_=src), incs=[(sD, 16)])
        for mt in range(NCT):
            src = AP(outprojT.tensor, mt * P, [[DM, P], [P * DM, NET], [1, P]])
            direct_w = prog.emit("gpsimd", lambda mt=mt, src=src: nc.gpsimd.dma_start(
                out=scr_outproj[mt], in_=src), incs=[(sD, 16)])

        ecast = 0
        for i in range(NCT):
            ecast = prog.emit("gpsimd", lambda i=i: nc.gpsimd.dma_start(
                out=xc_bf[:, i * DM:(i + 1) * DM],
                in_=wa1T[i * P:(i + 1) * P, :]), incs=[(sE, 16)])
        for i in range(NCT):
            ecast = prog.emit("gpsimd", lambda i=i: nc.gpsimd.dma_start(
                out=bcb[:, i * DM:(i + 1) * DM],
                in_=wa2[i * P:(i + 1) * P, :]), incs=[(sE, 16)])
        for i in range(NCT):
            ecast = prog.emit("gpsimd", lambda i=i: nc.gpsimd.dma_start(
                out=xcry[:, i * DM:(i + 1) * DM],
                in_=w1T[i * P:(i + 1) * P, :]), incs=[(sE, 16)])
        T2B = 8 * DM
        for (lhs_buf, rhs_buf, rhs_base, out_buf, out_base, first_wait) in (
                (xc_bf, bcb, 0, bcb, T2B, ecast),
                (xcry, bcb, T2B, wp_bf, 0, None)):
            for mt in range(NCT):
                for half in range(2):
                    mmv = 0
                    for kt in range(NCT):
                        if kt == 0:
                            if first_wait is not None and mt == 0 and half == 0:
                                wts = [(sE, ecast)]
                            else:
                                wts = [(sV, prog.count(sV))]
                        else:
                            wts = []
                        mmv = t_op(lambda lb=lhs_buf, rb=rhs_buf, rb0=rhs_base, mt=mt, half=half, kt=kt:
                                   T.matmul(
                                       mm_ps[:, (mt % 2) * CH:(mt % 2) * CH + CH],
                                       lb[:, kt * DM + mt * P:kt * DM + (mt + 1) * P],
                                       rb[:, rb0 + kt * DM + half * CH:rb0 + kt * DM + half * CH + CH],
                                       start=(kt == 0), stop=(kt == NCT - 1)), waits=wts)
                    v_op(lambda ob=out_buf, ob0=out_base, mt=mt, half=half: V.tensor_copy(
                        ob[:, ob0 + mt * DM + half * CH:ob0 + mt * DM + half * CH + CH],
                        mm_ps[:, (mt % 2) * CH:(mt % 2) * CH + CH]),
                        waits=[(sT, mmv)])
        prep_mm_last = prog.count(sT)
        if debug:
            prog.emit("sync", lambda: nc.sync.dma_start(out=dbg_negA, in_=negA[:, :]),
                      waits=[(sV, nega_done)], incs=[(sO, 16)])
            prog.emit("sync", lambda: nc.sync.dma_start(out=dbg_dtw, in_=dt_wT_bf[:, :]),
                      waits=[(sV, dtw_cast)], incs=[(sO, 16)])
            prog.emit("sync", lambda: nc.sync.dma_start(out=dbg_xproj, in_=xprojT_bf[:, :]),
                      waits=[(sV, xproj_cast)], incs=[(sO, 16)])
            prog.emit("sync", lambda: nc.sync.dma_start(out=dbg_wp, in_=wp_bf[:, :]),
                      waits=[(sT, prep_mm_last), (sV, prog.count(sV))], incs=[(sO, 16)])

        prev = dict(outstage_done=0, states_copy=0,
                    gate_done={}, opmm={},
                    rep_sy=0, rep_g=0, stat_rep=0, ffn_add_last=0,
                    bcb_free=("T", prep_mm_last), dtmm_last=0,
                    dxc_hist={})

        pending_gate = [None]
        scan_hist = {}
        stmm_hist = {}
        foldl_hist = {}
        gate_hist = {}
        f17_hist = {}
        v2_hist = {}
        dxc_hist = {}

        for k in range(NCH):
            c0 = k * CH

            xload = 0
            for ct in range(NCT):
                if ct == 0:
                    if k > 0:
                        xw = [(sV, prev["outstage_done"])]
                    else:
                        xw = [(sS, e1), (sV, max(dtw_cast, xproj_cast))]
                else:
                    xw = []
                xload = sy_dma(x_ch[:, ct * CH:(ct + 1) * CH],
                               xT[ct * P:(ct + 1) * P, c0:c0 + CH],
                               waits=xw)
            sxmm = 0
            for ct in range(NCT):
                sxmm = t_op(lambda ct=ct: T.matmul(
                    stat_ps[0:1, :], ones[:, :], x_ch[:, ct * CH:(ct + 1) * CH],
                    start=(ct == 0), stop=(ct == NCT - 1)),
                    waits=([(sL, xload)] if ct == 0 else []))
            cp_sx = v_op(lambda: V.tensor_copy(r_sx[:, :], stat_ps[0:1, :]),
                         waits=[(sT, sxmm)])
            sx2mm = 0
            for ct in range(NCT):
                sq = s_op(lambda ct=ct: S.activation(
                    stage[:, (ct % 2) * CH:(ct % 2) * CH + CH],
                    x_ch[:, ct * CH:(ct + 1) * CH], AF.Square),
                    waits=([(sL, xload), (sV, prev["ffn_add_last"])] if ct == 0 else []) +
                          ([(sT, sx2mm)] if ct >= 2 else []))
                sx2mm = t_op(lambda ct=ct: T.matmul(
                    stat_ps[0:1, :], ones[:, :], stage[:, (ct % 2) * CH:(ct % 2) * CH + CH],
                    start=(ct == 0), stop=(ct == NCT - 1)),
                    waits=[(sS, sq)] + ([(sV, cp_sx)] if ct == 0 else []))
            cp_sx2 = v_op(lambda: V.tensor_copy(r_sx2[:, :], stat_ps[0:1, :]),
                          waits=[(sT, sx2mm)])
            v_op(lambda: V.tensor_scalar(r_sx[:, :], r_sx[:, :], 1.0 / DM, None, ALU.mult))
            v_op(lambda: V.tensor_scalar(r_sx2[:, :], r_sx2[:, :], 1.0 / DM, None, ALU.mult))
            v_op(lambda: V.scalar_tensor_tensor(r_t[:, :], r_sx[:, :], -1.0,
                                                r_sx[:, :], ALU.mult, ALU.mult))
            v_op(lambda: V.tensor_tensor(r_sx2[:, :], r_sx2[:, :], r_t[:, :], ALU.add))
            vvar = v_op(lambda: V.tensor_scalar(r_sx2[:, :], r_sx2[:, :], 1e-5, None, ALU.add))
            s_op(lambda: S.activation(r_t[:, :], r_sx2[:, :], AF.Ln),
                 waits=[(sV, vvar)])
            rstd = s_op(lambda: S.activation(r_a[:, :], r_t[:, :], AF.Exp, scale=-0.5))
            rb = v_op(lambda: V.scalar_tensor_tensor(r_b[:, :], r_sx[:, :], -1.0,
                                                     r_a[:, :], ALU.mult, ALU.mult),
                      waits=[(sS, rstd)])
            wst = sy_dma(scr_stat[0:1, :], r_a[:, :],
                         waits=[(sV, rb), (sL, prev["stat_rep"])])
            wst = sy_dma(scr_stat[1:2, :], r_b[:, :])
            sy_dma(aB[:, :], scr_stat[0:1, :].broadcast_to([P, CH]), waits=[(sL, wst)])
            stat_rep = sy_dma(bB[:, :], scr_stat[1:2, :].broadcast_to([P, CH]))
            xn_done = 0
            for ct in range(NCT):
                v_op(lambda ct=ct: V.tensor_tensor(
                    outstg[:, (ct % 2) * CH:(ct % 2) * CH + CH],
                    x_ch[:, ct * CH:(ct + 1) * CH], aB[:, :], ALU.mult),
                    waits=([(sL, stat_rep), (sO, prog.count(sO))] if ct == 0 else []))
                xn_done = v_op(lambda ct=ct: V.tensor_tensor(
                    xn_bf[:, ct * CH:(ct + 1) * CH],
                    outstg[:, (ct % 2) * CH:(ct % 2) * CH + CH], bB[:, :], ALU.add))

            silu_done = {}
            z_done = {}
            convgrp_done = {}
            mmgrp_done = {}
            for mt in range(32):
                wts = [(sT, mmgrp_done.get(mt - 2, 0))]
                if k == 0 and mt < 2:
                    wts += [(sD, direct_w)]
                if k > 0 and mt < 2:
                    wts += [(sT, prev["opmm"].get(NCT - 1, 0))]
                wload = g_dma(wstream[:, (mt % 2) * NET * P:(mt % 2) * NET * P + NCT * P],
                              scr_inproj[mt, :, :], waits=wts)
                mmv = 0
                for kt in range(NCT):
                    wts = []
                    if kt == 0:
                        wts += [(sG, wload)]
                        if mt == 0:
                            wts += [(sV, xn_done)]
                        if mt - 2 >= 0:
                            if mt - 2 < NET:
                                wts += [(sV, prog.count(sV))]
                            else:
                                wts += [(sS, z_done.get(mt - 2 - NET, 0))]
                    mmv = t_op(lambda mt=mt, kt=kt: T.matmul(
                        mm_ps[:, (mt % 2) * CH:(mt % 2) * CH + CH],
                        wstream[:, (mt % 2) * NET * P + kt * P:(mt % 2) * NET * P + (kt + 1) * P],
                        xn_bf[:, kt * CH:(kt + 1) * CH],
                        start=(kt == 0), stop=(kt == NCT - 1)), waits=wts)
                mmgrp_done[mt] = mmv
                if mt < NET:
                    et = mt
                    base = et * 520
                    v_op(lambda et=et, base=base: V.tensor_copy(
                        xcry[:, base + 4:base + 8], halo[:, et * 4:(et + 1) * 4]),
                        waits=[(sV, prev["gate_done"].get(et, 0))])
                    v_op(lambda base=base, mt=mt: V.tensor_copy(
                        xcry[:, base + 8:base + 520],
                        mm_ps[:, (mt % 2) * CH:(mt % 2) * CH + CH]),
                        waits=[(sT, mmv)])
                    v_op(lambda et=et, mt=mt: V.tensor_copy(
                        halo[:, et * 4:(et + 1) * 4],
                        mm_ps[:, (mt % 2) * CH + 508:(mt % 2) * CH + 512]))
                    dgv = 0
                    for kk in range(4):
                        dgv = v_op(lambda et=et, kk=kk: V.tensor_scalar(
                            diag[:, (et % 2) * 4 * P + kk * P:(et % 2) * 4 * P + (kk + 1) * P],
                            eye_bf[:, :], convw[:, et * 4 + kk:et * 4 + kk + 1],
                            None, ALU.mult),
                            waits=([(sT, convgrp_done.get(et - 2, 0))] if kk == 0 else []))
                    cnv = 0
                    for kk in range(4):
                        wts = [(sV, dgv), (sS, silu_done.get(et - 2, 0))] if kk == 0 else []
                        cnv = t_op(lambda et=et, kk=kk, base=base: T.matmul(
                            conv_ps[:, (et % 2) * CH:(et % 2) * CH + CH],
                            diag[:, (et % 2) * 4 * P + kk * P:(et % 2) * 4 * P + (kk + 1) * P],
                            xcry[:, base + 5 + kk:base + 5 + kk + CH],
                            start=(kk == 0), stop=(kk == 3)), waits=wts)
                    convgrp_done[et] = cnv
                    silu_done[et] = s_op(lambda et=et: S.activation(
                        xc_bf[:, et * CH:(et + 1) * CH],
                        conv_ps[:, (et % 2) * CH:(et % 2) * CH + CH],
                        AF.Silu, bias=convb[:, et:et + 1]),
                        waits=[(sT, cnv),
                               (sV, prev["dxc_hist"].get(et, 0))])
                else:
                    et = mt - NET
                    z_done[et] = s_op(lambda et=et, mt=mt: S.activation(
                        z_all[:, et * CH:(et + 1) * CH],
                        mm_ps[:, (mt % 2) * CH:(mt % 2) * CH + CH],
                        AF.Silu),
                        waits=[(sT, mmv),
                               (sV, prev["gate_done"].get(et, 0))])

            mmfree = {0: ("S", z_done[14]), 1: ("S", z_done[15])}

            xpv = 0
            for kt in range(NET):
                wts = [(sS, silu_done[kt])]
                if kt == 0:
                    wts += [(sS, silu_done[NET - 1])]
                xpv = t_op(lambda kt=kt: T.matmul(
                    conv_ps[0:96, 0:CH],
                    xprojT_bf[:, kt * 96:(kt + 1) * 96],
                    xc_bf[:, kt * CH:(kt + 1) * CH],
                    start=(kt == 0), stop=(kt == NET - 1)), waits=wts)
            draw_done = s_op(lambda: S.activation(drawT[:, :], conv_ps[0:DTR, 0:CH],
                                                  AF.Copy),
                             waits=[(sT, xpv), (sT, prev["dtmm_last"])])
            bc_done = v_op(lambda: V.tensor_copy(bc_sb[DTR:96, :], conv_ps[DTR:96, 0:CH]),
                           waits=[(sT, xpv)])
            bc_w = sy_dma(scr_bc[:, :], bc_sb[DTR:96, :],
                          waits=[(sV, bc_done), (sL, prev["rep_sy"]),
                                 (sG, prev["rep_g"])])
            rep_sy = 0
            rep_g = 0
            for n in range(NST):
                wts = ([(sL, bc_w), w(prev["bcb_free"])] if n == 0 else [])
                rep_sy = sy_dma(bcb[:, n * CH:(n + 1) * CH],
                                scr_bc[n:n + 1, :].broadcast_to([P, CH]), waits=wts)
                rep_g = g_dma(bcb[:, (16 + n) * CH:(17 + n) * CH],
                              scr_bc[16 + n:17 + n, :].broadcast_to([P, CH]), waits=wts)

            dtmm_last = 0
            for et in range(NET):
                etg = k * NET + et
                par = et % 2
                dtmm = t_op(lambda et=et, par=par: T.matmul(
                    mm_ps[:, par * CH:par * CH + CH],
                    dt_wT_bf[:, et * P:(et + 1) * P],
                    drawT[:, :], start=True, stop=True),
                    waits=[w(mmfree[par]), (sS, draw_done)])
                dtmm_last = dtmm
                s_op(lambda et=et, par=par: S.activation(
                    stage[:, par * CH:par * CH + CH],
                    mm_ps[:, par * CH:par * CH + CH],
                    AF.Exp, bias=dtb[:, et:et + 1]),
                    waits=[(sT, dtmm)])
                sp = s_op(lambda par=par: S.activation(
                    delta2[:, par * CH:par * CH + CH],
                    stage[:, par * CH:par * CH + CH],
                    AF.Ln, bias=1.0),
                    waits=[(sV, v2_hist.get(etg - 2, 0))])
                mmfree[par] = ("S", sp)
                v2d = v_op(lambda par=par, et=et: V.tensor_tensor(
                    v2[:, par * CH:par * CH + CH],
                    delta2[:, par * CH:par * CH + CH],
                    xc_bf[:, et * CH:(et + 1) * CH], ALU.mult),
                    waits=[(sS, sp)])
                v2_hist[etg] = v2d

                for g in range(NG):
                    gg = etg * NG + g
                    gp = gg % 2
                    if g == 1 and pending_gate[0] is not None:
                        pending_gate[0]()
                        pending_gate[0] = None
                    uw = []
                    if et == 0 and g == 0:
                        uw = [(sL, rep_sy), (sG, rep_g)]
                    v_op(lambda par=par, g=g, gp=gp: V.tensor_tensor(
                        out=AP(u_roll, gp * GN * CH, [[2 * GN * CH, P], [CH, GN], [1, CH]]),
                        in0=AP(v2, par * CH, [[2 * CH, P], [0, GN], [1, CH]]),
                        in1=AP(bcb, g * GN * CH, [[32 * CH, P], [CH, GN], [1, CH]]),
                        op=ALU.mult), waits=uw)
                    for n4 in range(GN):
                        n = g * GN + n4
                        i = et * NST + n
                        gi = k * 256 + i
                        ew = [(sV, scan_hist.get(gi - 4, 0))]
                        if gi == 0:
                            ew += [(sV, nega_done)]
                        expd = s_op(lambda par=par, i=i: S.activation(
                            dA[:, (i % 4) * CH:(i % 4) * CH + CH],
                            delta2[:, par * CH:par * CH + CH],
                            AF.Exp, scale=negA[:, i:i + 1]), waits=ew)
                        init = 0.0 if k == 0 else states[:, i:i + 1]
                        sw = [(sS, expd)]
                        if n4 == 0:
                            sw += [(sT, stmm_hist.get(gg - 2, 0))]
                        scan_hist[gi] = v_op(lambda gp=gp, n4=n4, i=i, init=init:
                                             V.tensor_tensor_scan(
                            hs_roll[:, gp * GN * CH + n4 * CH:gp * GN * CH + (n4 + 1) * CH],
                            dA[:, (i % 4) * CH:(i % 4) * CH + CH],
                            u_roll[:, gp * GN * CH + n4 * CH:gp * GN * CH + (n4 + 1) * CH],
                            init, ALU.mult, ALU.add), waits=sw)
                    stw = [(sV, scan_hist[k * 256 + et * NST + g * GN + GN - 1])]
                    if et == 0 and g == 0:
                        stw += [(sV, prev["states_copy"])]
                    stmm_hist[gg] = t_op(lambda et=et, g=g, gp=gp: T.matmul(
                        st_ps[:, et * NST + g * GN:et * NST + (g + 1) * GN],
                        eye_bf[:, :],
                        AP(hs_roll, gp * GN * CH + CH - 1, [[2 * GN * CH, P], [CH, GN]]),
                        start=True, stop=True), waits=stw)
                    tw = [(sT, foldl_hist.get(gg - 2, 0))]
                    if et == 0 and g == 0:
                        tw += [(sG, rep_g)]
                    tmpd = v_op(lambda g=g, gp=gp: V.tensor_tensor(
                        out=AP(tmp_roll, gp * GN * CH, [[2 * GN * CH, P], [CH, GN], [1, CH]]),
                        in0=AP(hs_roll, gp * GN * CH, [[2 * GN * CH, P], [CH, GN], [1, CH]]),
                        in1=AP(bcb, (16 + g * GN) * CH, [[32 * CH, P], [CH, GN], [1, CH]]),
                        op=ALU.mult), waits=tw)
                    fold = 0
                    for j in range(GN):
                        fw = []
                        if j == 0:
                            fw += [(sV, tmpd)]
                            if g == 0:
                                fw += [(sV, gate_hist.get(etg - 2, 0))]
                        fold = t_op(lambda par=par, gp=gp, j=j, g=g: T.matmul(
                            y_ps[:, par * CH:par * CH + CH],
                            eye_bf[:, :],
                            tmp_roll[:, gp * GN * CH + j * CH:gp * GN * CH + (j + 1) * CH],
                            start=(g == 0 and j == 0), stop=False), waits=fw)
                    foldl_hist[gg] = fold

                dxcv = v_op(lambda et=et, par=par: V.tensor_scalar(
                    dxc[:, par * CH:par * CH + CH],
                    xc_bf[:, et * CH:(et + 1) * CH],
                    dparm[:, et:et + 1], None, ALU.mult),
                    waits=[(sT, f17_hist.get(etg - 2, 0))])
                dxc_hist[etg] = dxcv
                f17 = t_op(lambda par=par: T.matmul(
                    y_ps[:, par * CH:par * CH + CH],
                    eye_bf[:, :],
                    dxc[:, par * CH:par * CH + CH],
                    start=False, stop=True),
                    waits=[(sV, dxcv)])
                f17_hist[etg] = f17

                def emit_gate(et=et, par=par, etg=etg, f17=f17, zd=z_done[et]):
                    gate_hist[etg] = v_op(lambda et=et, par=par: V.tensor_tensor(
                        xcry[:, et * CH:(et + 1) * CH],
                        y_ps[:, par * CH:par * CH + CH],
                        z_all[:, et * CH:(et + 1) * CH], ALU.mult),
                        waits=[(sT, f17), (sS, zd)])
                pending_gate[0] = emit_gate
            if pending_gate[0] is not None:
                pending_gate[0]()
                pending_gate[0] = None
            states_copy = v_op(lambda: V.tensor_copy(states[:, :], st_ps[:, :]),
                               waits=[(sT, stmm_hist[(k * NET + NET - 1) * NG + NG - 1])])

            if debug:
                prog.emit("sync", lambda k=k: nc.sync.dma_start(out=dbg_xn[k], in_=xn_bf[:, :]),
                          waits=[(sV, xn_done)], incs=[(sO, 16)])
                prog.emit("sync", lambda k=k: nc.sync.dma_start(out=dbg_xc[k], in_=xc_bf[:, :]),
                          waits=[(sS, silu_done[NET - 1])], incs=[(sO, 16)])
                prog.emit("sync", lambda k=k: nc.sync.dma_start(out=dbg_bcb[k], in_=bcb[:, :]),
                          waits=[(sL, rep_sy), (sG, rep_g)], incs=[(sO, 16)])
                prog.emit("sync", lambda k=k: nc.sync.dma_start(out=dbg_states[k], in_=states[:, :]),
                          waits=[(sV, states_copy)], incs=[(sO, 16)])
                prog.emit("sync", lambda k=k: nc.sync.dma_start(out=dbg_ygate[k], in_=xcry[:, 0:NET * CH]),
                          waits=[(sV, gate_hist[k * NET + NET - 1])], incs=[(sO, 16)])
                prog.emit("sync", lambda k=k: nc.sync.dma_start(out=dbg_z[k], in_=z_all[:, 0:2 * CH]),
                          waits=[(sS, z_done[NET - 1])], incs=[(sO, 16)])

            ffn_add = {}
            ffn_th = {}
            for ct in range(NCT):
                mmv = 0
                for kt in range(NCT):
                    wts = []
                    if kt == 0:
                        wts += [w(mmfree[ct % 2])]
                        if ct == 0:
                            wts += [(sV, xn_done)]
                    mmv = t_op(lambda ct=ct, kt=kt: T.matmul(
                        mm_ps[:, (ct % 2) * CH:(ct % 2) * CH + CH],
                        wp_bf[:, kt * DM + ct * P:kt * DM + (ct + 1) * P],
                        xn_bf[:, kt * CH:(kt + 1) * CH],
                        start=(kt == 0), stop=(kt == NCT - 1)), waits=wts)
                ffn_th[ct] = s_op(lambda ct=ct: S.activation(
                    stage[:, (ct % 2) * CH:(ct % 2) * CH + CH],
                    mm_ps[:, (ct % 2) * CH:(ct % 2) * CH + CH],
                    AF.Tanh, bias=fbias[:, ct:ct + 1]),
                    waits=[(sT, mmv)] + ([(sV, ffn_add[ct - 2])] if ct >= 2 else []))
                mmfree[ct % 2] = ("S", ffn_th[ct])
                ffn_add[ct] = v_op(lambda ct=ct: V.tensor_tensor(
                    x_ch[:, ct * CH:(ct + 1) * CH],
                    x_ch[:, ct * CH:(ct + 1) * CH],
                    stage[:, (ct % 2) * CH:(ct % 2) * CH + CH], ALU.add),
                    waits=[(sS, ffn_th[ct])])

            outstage_done = 0
            opmm = {}
            st_vals = dict(prev.get("st_vals", {}))
            for mt in range(NCT):
                wts = [(sT, opmm.get(mt - 1, 0))]
                if mt == 0:
                    wts += [(sT, mmgrp_done[31])]
                    if k == 0:
                        wts += [(sD, direct_w)]
                wload = g_dma(wstream[:, 0:NET * P],
                              scr_outproj[mt, :, :], waits=wts)
                mmv = 0
                for kt in range(NET):
                    wts = []
                    if kt == 0:
                        wts += [(sG, wload), (sV, gate_hist[k * NET + NET - 1]),
                                w(mmfree[mt % 2])]
                        if mt >= 2:
                            wts += [(sV, prog.count(sV))]
                    mmv = t_op(lambda mt=mt, kt=kt: T.matmul(
                        mm_ps[:, (mt % 2) * CH:(mt % 2) * CH + CH],
                        wstream[:, kt * P:(kt + 1) * P],
                        xcry[:, kt * CH:(kt + 1) * CH],
                        start=(kt == 0), stop=(kt == NET - 1)), waits=wts)
                opmm[mt] = mmv
                outstage_done = v_op(lambda mt=mt: V.tensor_tensor(
                    outstg[:, (mt % 2) * CH:(mt % 2) * CH + CH],
                    mm_ps[:, (mt % 2) * CH:(mt % 2) * CH + CH],
                    x_ch[:, mt * CH:(mt + 1) * CH], ALU.add),
                    waits=[(sT, mmv), (sV, ffn_add[mt]),
                           (sO, st_vals.get(mt - 2, st_vals.get(6 + mt, 0)))])
                mmfree[mt % 2] = ("V", outstage_done)
                st_vals[mt] = prog.emit("sync",
                          lambda mt=mt, c0=c0: nc.sync.dma_start(
                              out=outT[mt * P:(mt + 1) * P, c0:c0 + CH],
                              in_=outstg[:, (mt % 2) * CH:(mt % 2) * CH + CH]),
                          waits=[(sV, outstage_done)], incs=[(sO, 16)])

            prev = dict(outstage_done=outstage_done,
                        states_copy=states_copy, gate_done=gate_hist_slice(gate_hist, k),
                        opmm=opmm,
                        rep_sy=rep_sy, rep_g=rep_g,
                        stat_rep=stat_rep, st_vals=st_vals,
                        ffn_add_last=ffn_add[NCT - 1],
                        bcb_free=("V", prog.count(sV)),
                        dtmm_last=dtmm_last,
                        dxc_hist=dxc_hist_slice(dxc_hist, k))

        total_stores = prog.count(sO)
        if enddump:
            allv, alls, allt, allg = prog.count(sV), prog.count(sS), prog.count(sT), prog.count(sG)
            srcs = dict(xch=x_ch[:, :],
                        negA=negA[:, :], dtw=dt_wT_bf[:, :], xproj=xprojT_bf[:, :],
                        wp=wp_bf[:, :], bcb=bcb[:, :], xn=xn_bf[:, :], xc=xc_bf[:, :],
                        zall=z_all[:, :], ygate=xcry[:, 0:NET * CH], states=states[:, :],
                        convw=convw[:, :], dtb=dtb[:, :], aB=aB[:, :], bB=bB[:, :],
                        drawT=drawT[:, :], eye=eye_bf[:, :])
            first = True
            for nm, src in srcs.items():
                wts = [(sO, total_stores), (sV, allv), (sS, alls), (sT, allt), (sG, allg)] if first else []
                first = False
                prog.emit("sync", lambda d=ed[nm], s=src: nc.sync.dma_start(out=d, in_=s),
                          waits=wts, incs=[(sO, 16)])
            total_stores = prog.count(sO)
        prog.emit("sync", lambda: nc.sync.nop(), waits=[(sO, total_stores)])

        with nc.Block() as block:
            @block.sync
            def _(eng):
                prog.replay("sync", eng)

            @block.vector
            def _(eng):
                prog.replay("vector", eng)

            @block.scalar
            def _(eng):
                prog.replay("scalar", eng)

            @block.tensor
            def _(eng):
                prog.replay("tensor", eng)

            @block.gpsimd
            def _(eng):
                prog.replay("gpsimd", eng)

    return nc


def gate_hist_slice(gate_hist, k):
    return {et: gate_hist[k * NET + et] for et in range(NET)}


def dxc_hist_slice(dxc_hist, k):
    return {et: dxc_hist[k * NET + et] for et in range(NET)}


_NC_CACHE = None


def _get_nc():
    global _NC_CACHE
    if _NC_CACHE is None:
        _NC_CACHE = build_program()
    return _NC_CACHE


def kernel(L0_attn_x, L0_attn_in_proj_w, L0_attn_conv_w, L0_attn_conv_b,
           L0_attn_x_proj_w, L0_attn_dt_w, L0_attn_dt_b, L0_attn_A_log,
           L0_attn_D_param, L0_attn_out_proj_w, L0_ffn_weight,
           L0_ffn_aux_weight1, L0_ffn_aux_weight2, L0_ffn_bias):
    x = np.asarray(L0_attn_x, dtype=np.float32)
    B = x.shape[0]
    f32 = lambda a: np.ascontiguousarray(np.asarray(a, dtype=np.float32))

    inprojT = f32(np.asarray(L0_attn_in_proj_w).T)
    outprojT = f32(np.asarray(L0_attn_out_proj_w).T)
    xprojT = f32(np.asarray(L0_attn_x_proj_w).T)
    dt_wT = f32(np.asarray(L0_attn_dt_w).T)
    w1T = f32(np.asarray(L0_ffn_weight).T)
    wa1T = f32(np.asarray(L0_ffn_aux_weight1).T)
    wa2 = f32(L0_ffn_aux_weight2)

    def chan_r(vec, nt):
        return f32(np.asarray(vec).reshape(nt, P).T)

    convw_r = f32(np.asarray(L0_attn_conv_w).reshape(NET, P, 4)
                  .transpose(1, 0, 2).reshape(P, 4 * NET))
    convb_r = chan_r(L0_attn_conv_b, NET)
    dtb_r = chan_r(L0_attn_dt_b, NET)
    alog_r = f32(np.asarray(L0_attn_A_log).reshape(NET, P, NST)
                 .transpose(1, 0, 2).reshape(P, NST * NET))
    d_r = chan_r(L0_attn_D_param, NET)
    fbias_r = chan_r(L0_ffn_bias, NCT)
    eye = np.eye(P, dtype=BF)
    ones = np.ones((P, 1), dtype=np.float32)

    shared = dict(inprojT=inprojT, outprojT=outprojT, xprojT=xprojT,
                  dt_wT=dt_wT, w1T=w1T, wa1T=wa1T, wa2=wa2,
                  convw_r=convw_r, convb_r=convb_r, dtb_r=dtb_r,
                  alog_r=alog_r, d_r=d_r, fbias_r=fbias_r,
                  eye_in=eye, ones_in=ones)

    in_maps = []
    for b in range(B):
        m = dict(shared)
        m["xT"] = np.ascontiguousarray(x[b].T)
        in_maps.append(m)

    nc = _get_nc()
    res = run_bass_kernel_spmd(nc, in_maps, core_ids=list(range(B)))
    out = np.stack([np.asarray(r["outT"]).T for r in res.results], axis=0)
    return out.astype(np.float32)


# version 11 (final)
# speedup vs baseline: 1.1887x; 1.0152x over previous
import numpy as np
import ml_dtypes
from contextlib import ExitStack

import concourse.bass as bass
import concourse.mybir as mybir
from concourse.ap import AP
from concourse.bass_utils import run_bass_kernel_spmd

F32 = mybir.dt.float32
BF16 = mybir.dt.bfloat16
AF = mybir.ActivationFunctionType
ALU = mybir.AluOpType

P = 128
L = 2048
CH = 512
NCH = L // CH
NET = 16
NCT = 8
NST = 16
DTR = 64
DM = 1024
DI = 2048
NG = 4
GN = 4

BF = ml_dtypes.bfloat16


class Prog:

    ENGINES = ("sync", "vector", "scalar", "tensor", "gpsimd")

    def __init__(self):
        self.items = {e: [] for e in self.ENGINES}
        self.counts = {}

    def emit(self, eng, fn, waits=(), incs=()):
        waits = [(s, v) for (s, v) in waits if v > 0]
        self.items[eng].append((fn, waits, list(incs)))
        post = None
        for (sem, amt) in incs:
            self.counts[id(sem)] = self.counts.get(id(sem), 0) + amt
            post = self.counts[id(sem)]
        return post

    def count(self, sem):
        return self.counts.get(id(sem), 0)

    def replay(self, eng, handle):
        for fn, waits, incs in self.items[eng]:
            for sem, val in waits:
                handle.wait_ge(sem, val)
            inst = fn()
            for sem, amt in incs:
                inst = inst.then_inc(sem, amt)


def build_program(debug=False, enddump=False):
    nc = bass.Bass("TRN2", target_bir_lowering=False)

    xT = nc.dram_tensor("xT", [DM, L], F32, kind="ExternalInput").ap()
    inprojT = nc.dram_tensor("inprojT", [DM, 2 * DI], F32, kind="ExternalInput").ap()
    outprojT = nc.dram_tensor("outprojT", [DI, DM], F32, kind="ExternalInput").ap()
    xprojT = nc.dram_tensor("xprojT", [DI, 96], F32, kind="ExternalInput").ap()
    dt_wT = nc.dram_tensor("dt_wT", [DTR, DI], F32, kind="ExternalInput").ap()
    w1T = nc.dram_tensor("w1T", [DM, DM], F32, kind="ExternalInput").ap()
    wa1T = nc.dram_tensor("wa1T", [DM, DM], F32, kind="ExternalInput").ap()
    wa2 = nc.dram_tensor("wa2", [DM, DM], F32, kind="ExternalInput").ap()
    convw_r = nc.dram_tensor("convw_r", [P, 4 * NET], F32, kind="ExternalInput").ap()
    convb_r = nc.dram_tensor("convb_r", [P, NET], F32, kind="ExternalInput").ap()
    dtb_r = nc.dram_tensor("dtb_r", [P, NET], F32, kind="ExternalInput").ap()
    alog_r = nc.dram_tensor("alog_r", [P, NST * NET], F32, kind="ExternalInput").ap()
    d_r = nc.dram_tensor("d_r", [P, NET], F32, kind="ExternalInput").ap()
    fbias_r = nc.dram_tensor("fbias_r", [P, NCT], F32, kind="ExternalInput").ap()
    eye_in = nc.dram_tensor("eye_in", [P, P], BF16, kind="ExternalInput").ap()
    ones_in = nc.dram_tensor("ones_in", [P, 1], F32, kind="ExternalInput").ap()

    outT = nc.dram_tensor("outT", [DM, L], F32, kind="ExternalOutput").ap()
    if enddump:
        ed = {}
        for nm, shp, dt in (("xch", [P, NCT * CH], F32),
                            ("negA", [P, NST * NET], F32), ("dtw", [DTR, DI], BF16),
                            ("xproj", [P, NET * 96], BF16), ("wp", [P, NCT * DM], BF16),
                            ("bcb", [P, 32 * CH], BF16), ("xn", [P, NCT * CH], BF16),
                            ("xc", [P, NET * CH], BF16), ("zall", [P, NET * CH], BF16),
                            ("ygate", [P, NET * CH], BF16), ("states", [P, NET * NST], F32),
                            ("convw", [P, 4 * NET], F32), ("dtb", [P, NET], F32),
                            ("aB", [P, CH], BF16), ("bB", [P, CH], BF16),
                            ("drawT", [DTR, CH], BF16), ("eye", [P, P], BF16)):
            ed[nm] = nc.dram_tensor("ed_" + nm, shp, dt, kind="ExternalOutput").ap()
    if debug:
        dbg_negA = nc.dram_tensor("dbg_negA", [P, NST * NET], F32, kind="ExternalOutput").ap()
        dbg_wp = nc.dram_tensor("dbg_wp", [P, NCT * DM], BF16, kind="ExternalOutput").ap()
        dbg_dtw = nc.dram_tensor("dbg_dtw", [DTR, DI], BF16, kind="ExternalOutput").ap()
        dbg_xproj = nc.dram_tensor("dbg_xproj", [P, NET * 96], BF16, kind="ExternalOutput").ap()
        dbg_xn = nc.dram_tensor("dbg_xn", [NCH, P, NCT * CH], BF16, kind="ExternalOutput").ap()
        dbg_xc = nc.dram_tensor("dbg_xc", [NCH, P, NET * CH], BF16, kind="ExternalOutput").ap()
        dbg_bcb = nc.dram_tensor("dbg_bcb", [NCH, P, 32 * CH], BF16, kind="ExternalOutput").ap()
        dbg_states = nc.dram_tensor("dbg_states", [NCH, P, NET * NST], F32, kind="ExternalOutput").ap()
        dbg_ygate = nc.dram_tensor("dbg_ygate", [NCH, P, NET * CH], BF16, kind="ExternalOutput").ap()
        dbg_z = nc.dram_tensor("dbg_z", [NCH, P, 2 * CH], BF16, kind="ExternalOutput").ap()

    scr_inproj_h = nc.dram_tensor("scr_inproj", [32, P, NCT * P], BF16, kind="Internal")
    scr_outproj_h = nc.dram_tensor("scr_outproj", [NCT, P, NET * P], BF16, kind="Internal")
    scr_inproj = scr_inproj_h.ap()
    scr_outproj = scr_outproj_h.ap()
    scr_bc = nc.dram_tensor("scr_bc", [32, CH], BF16, kind="Internal").ap()
    scr_stat = nc.dram_tensor("scr_stat", [2, CH], BF16, kind="Internal").ap()

    prog = Prog()

    with ExitStack() as ctx:
        sbt = lambda name, shape, dt: ctx.enter_context(nc.sbuf_tensor(name, shape, dt))
        psb = lambda name, shape, dt: ctx.enter_context(nc.psum_tensor(name, shape, dt))
        sem = lambda name: ctx.enter_context(nc.semaphore(name))

        x_ch = sbt("x_ch", [P, NCT * CH], F32)
        xn_bf = sbt("xn_bf", [P, NCT * CH], BF16)
        xcry = sbt("xcry", [P, NET * 520], BF16)
        xc_bf = sbt("xc_bf", [P, NET * CH], BF16)
        z_all = sbt("z_all", [P, NET * CH], BF16)
        bcb = sbt("bcb", [P, 32 * CH], BF16)
        wp_bf = sbt("wp_bf", [P, NCT * DM], BF16)
        wstream = sbt("wstream", [P, 32 * P], BF16)
        u_roll = sbt("u_roll", [P, 2 * GN * CH], BF16)
        hs_roll = sbt("hs_roll", [P, 2 * GN * CH], BF16)
        tmp_roll = sbt("tmp_roll", [P, 2 * GN * CH], BF16)
        delta2 = sbt("delta2", [P, 2 * CH], F32)
        v2 = sbt("v2", [P, 2 * CH], BF16)
        dA = sbt("dA", [P, 2 * CH], F32)
        dxc = sbt("dxc", [P, 2 * CH], BF16)
        stage = sbt("stage", [P, 2 * CH], F32)
        outstg = sbt("outstg", [P, 2 * CH], F32)
        states = sbt("states", [P, NET * NST], F32)
        aB = sbt("aB", [P, CH], BF16)
        bB = sbt("bB", [P, CH], BF16)
        diag = sbt("diag", [P, 2 * 4 * P], BF16)
        drawT = sbt("drawT", [DTR, CH], BF16)
        bc_sb = sbt("bc_sb", [96, CH], BF16)
        halo = sbt("halo", [P, NET * 4], BF16)
        negA = sbt("negA", [P, NST * NET], F32)
        convw = sbt("convw", [P, 4 * NET], F32)
        convb = sbt("convb", [P, NET], F32)
        dtb = sbt("dtb", [P, NET], F32)
        dparm = sbt("dparm", [P, NET], F32)
        fbias = sbt("fbias", [P, NCT], F32)
        eye_bf = sbt("eye_bf", [P, P], BF16)
        ones = sbt("ones", [P, 1], F32)
        xprojT_bf = sbt("xprojT_bf", [P, NET * 96], BF16)
        dt_wT_bf = sbt("dt_wT_bf", [DTR, DI], BF16)
        r_sx = sbt("r_sx", [1, CH], F32)
        r_sx2 = sbt("r_sx2", [1, CH], F32)
        r_a = sbt("r_a", [1, CH], BF16)
        r_b = sbt("r_b", [1, CH], BF16)
        r_t = sbt("r_t", [1, CH], F32)

        mm_ps = psb("mm_ps", [P, 2 * CH], F32)
        conv_ps = psb("conv_ps", [P, 2 * CH], F32)
        y_ps = psb("y_ps", [P, 2 * CH], F32)
        st_ps = psb("st_ps", [P, NET * NST], F32)
        stat_ps = psb("stat_ps", [1, CH], F32)

        sL = sem("sL")
        sG = sem("sG")
        sD = sem("sD")
        sE = sem("sE")
        sV = sem("sV")
        sS = sem("sS")
        sT = sem("sT")
        sO = sem("sO")

        T = nc.tensor
        V = nc.vector
        S = nc.scalar

        def sy_dma(dst, src, waits=()):
            return prog.emit("sync", lambda d=dst, s=src: nc.sync.dma_start(out=d, in_=s),
                             waits=waits, incs=[(sL, 16)])

        def g_dma(dst, src, waits=()):
            return prog.emit("gpsimd", lambda d=dst, s=src: nc.gpsimd.dma_start(out=d, in_=s),
                             waits=waits, incs=[(sG, 16)])

        def v_op(fn, waits=()):
            return prog.emit("vector", fn, waits=waits, incs=[(sV, 1)])

        def s_op(fn, waits=()):
            return prog.emit("scalar", fn, waits=waits, incs=[(sS, 1)])

        def t_op(fn, waits=()):
            return prog.emit("tensor", fn, waits=waits, incs=[(sT, 1)])

        SEMS = {"S": sS, "V": sV, "T": sT}

        def w(tag):
            eng, cnt = tag
            return (SEMS[eng], cnt)

        sy_dma(eye_bf[:, :], eye_in)
        sy_dma(ones[:, :], ones_in)
        sy_dma(convw[:, :], convw_r)
        sy_dma(convb[:, :], convb_r)
        sy_dma(dtb[:, :], dtb_r)
        sy_dma(dparm[:, :], d_r)
        sy_dma(fbias[:, :], fbias_r)
        w_alog = sy_dma(x_ch[:, 0:NST * NET], alog_r)
        w_dtw = sy_dma(x_ch[0:DTR, 256:256 + DI], dt_wT)
        w_xp = 0
        for kt in range(NET):
            w_xp = sy_dma(x_ch[:, 2304 + kt * 96:2304 + (kt + 1) * 96],
                          xprojT[kt * P:(kt + 1) * P, :])

        e1 = s_op(lambda: S.activation(stage[:, 0:NST * NET], x_ch[:, 0:NST * NET], AF.Exp),
                  waits=[(sL, w_xp)])
        nega_done = v_op(lambda: V.tensor_scalar(negA[:, :], stage[:, 0:NST * NET],
                                                 -1.0, None, ALU.mult),
                         waits=[(sS, e1)])
        dtw_cast = v_op(lambda: V.tensor_copy(dt_wT_bf[:, :], x_ch[0:DTR, 256:256 + DI]),
                        waits=[(sL, w_xp)])
        xproj_cast = v_op(lambda: V.tensor_copy(xprojT_bf[:, :],
                                                x_ch[:, 2304:2304 + NET * 96]),
                          waits=[(sL, w_xp)])
        v_op(lambda: V.memset(halo[:, :], 0.0))

        direct_w = 0
        for mt in range(32):
            src = AP(inprojT.tensor, mt * P, [[2 * DI, P], [P * 2 * DI, NCT], [1, P]])
            direct_w = prog.emit("gpsimd", lambda mt=mt, src=src: nc.gpsimd.dma_start(
                out=scr_inproj[mt], in_=src), incs=[(sD, 16)])
        for mt in range(NCT):
            src = AP(outprojT.tensor, mt * P, [[DM, P], [P * DM, NET], [1, P]])
            direct_w = prog.emit("gpsimd", lambda mt=mt, src=src: nc.gpsimd.dma_start(
                out=scr_outproj[mt], in_=src), incs=[(sD, 16)])

        ecast = 0
        for i in range(NCT):
            ecast = prog.emit("gpsimd", lambda i=i: nc.gpsimd.dma_start(
                out=xc_bf[:, i * DM:(i + 1) * DM],
                in_=wa1T[i * P:(i + 1) * P, :]), incs=[(sE, 16)])
        for i in range(NCT):
            ecast = prog.emit("gpsimd", lambda i=i: nc.gpsimd.dma_start(
                out=bcb[:, i * DM:(i + 1) * DM],
                in_=wa2[i * P:(i + 1) * P, :]), incs=[(sE, 16)])
        for i in range(NCT):
            ecast = prog.emit("gpsimd", lambda i=i: nc.gpsimd.dma_start(
                out=xcry[:, i * DM:(i + 1) * DM],
                in_=w1T[i * P:(i + 1) * P, :]), incs=[(sE, 16)])
        T2B = 8 * DM
        for (lhs_buf, rhs_buf, rhs_base, out_buf, out_base, first_wait) in (
                (xc_bf, bcb, 0, bcb, T2B, ecast),
                (xcry, bcb, T2B, wp_bf, 0, None)):
            for mt in range(NCT):
                for half in range(2):
                    mmv = 0
                    for kt in range(NCT):
                        if kt == 0:
                            if first_wait is not None and mt == 0 and half == 0:
                                wts = [(sE, ecast)]
                            else:
                                wts = [(sV, prog.count(sV))]
                        else:
                            wts = []
                        mmv = t_op(lambda lb=lhs_buf, rb=rhs_buf, rb0=rhs_base, mt=mt, half=half, kt=kt:
                                   T.matmul(
                                       mm_ps[:, (mt % 2) * CH:(mt % 2) * CH + CH],
                                       lb[:, kt * DM + mt * P:kt * DM + (mt + 1) * P],
                                       rb[:, rb0 + kt * DM + half * CH:rb0 + kt * DM + half * CH + CH],
                                       start=(kt == 0), stop=(kt == NCT - 1)), waits=wts)
                    v_op(lambda ob=out_buf, ob0=out_base, mt=mt, half=half: V.tensor_copy(
                        ob[:, ob0 + mt * DM + half * CH:ob0 + mt * DM + half * CH + CH],
                        mm_ps[:, (mt % 2) * CH:(mt % 2) * CH + CH]),
                        waits=[(sT, mmv)])
        prep_mm_last = prog.count(sT)
        if debug:
            prog.emit("sync", lambda: nc.sync.dma_start(out=dbg_negA, in_=negA[:, :]),
                      waits=[(sV, nega_done)], incs=[(sO, 16)])
            prog.emit("sync", lambda: nc.sync.dma_start(out=dbg_dtw, in_=dt_wT_bf[:, :]),
                      waits=[(sV, dtw_cast)], incs=[(sO, 16)])
            prog.emit("sync", lambda: nc.sync.dma_start(out=dbg_xproj, in_=xprojT_bf[:, :]),
                      waits=[(sV, xproj_cast)], incs=[(sO, 16)])
            prog.emit("sync", lambda: nc.sync.dma_start(out=dbg_wp, in_=wp_bf[:, :]),
                      waits=[(sT, prep_mm_last), (sV, prog.count(sV))], incs=[(sO, 16)])

        prev = dict(outstage_done=0, states_copy=0,
                    gate_done={}, opmm={},
                    rep_sy=0, rep_g=0, stat_rep=0, ffn_add_last=0,
                    bcb_free=("T", prep_mm_last), dtmm_last=0,
                    dxc_hist={})

        pending_gate = [None]
        scan_hist = {}
        stmm_hist = {}
        foldl_hist = {}
        gate_hist = {}
        f17_hist = {}
        v2_hist = {}
        dxc_hist = {}

        for k in range(NCH):
            c0 = k * CH

            xload = 0
            for ct in range(NCT):
                if ct == 0:
                    if k > 0:
                        xw = [(sV, prev["outstage_done"])]
                    else:
                        xw = [(sS, e1), (sV, max(dtw_cast, xproj_cast))]
                else:
                    xw = []
                xload = sy_dma(x_ch[:, ct * CH:(ct + 1) * CH],
                               xT[ct * P:(ct + 1) * P, c0:c0 + CH],
                               waits=xw)
            sxmm = 0
            for ct in range(NCT):
                sxmm = t_op(lambda ct=ct: T.matmul(
                    stat_ps[0:1, :], ones[:, :], x_ch[:, ct * CH:(ct + 1) * CH],
                    start=(ct == 0), stop=(ct == NCT - 1)),
                    waits=([(sL, xload)] if ct == 0 else []))
            cp_sx = v_op(lambda: V.tensor_copy(r_sx[:, :], stat_ps[0:1, :]),
                         waits=[(sT, sxmm)])
            sx2mm = 0
            for ct in range(NCT):
                sq = s_op(lambda ct=ct: S.activation(
                    stage[:, (ct % 2) * CH:(ct % 2) * CH + CH],
                    x_ch[:, ct * CH:(ct + 1) * CH], AF.Square),
                    waits=([(sL, xload), (sV, prev["ffn_add_last"])] if ct == 0 else []) +
                          ([(sT, sx2mm)] if ct >= 2 else []))
                sx2mm = t_op(lambda ct=ct: T.matmul(
                    stat_ps[0:1, :], ones[:, :], stage[:, (ct % 2) * CH:(ct % 2) * CH + CH],
                    start=(ct == 0), stop=(ct == NCT - 1)),
                    waits=[(sS, sq)] + ([(sV, cp_sx)] if ct == 0 else []))
            cp_sx2 = v_op(lambda: V.tensor_copy(r_sx2[:, :], stat_ps[0:1, :]),
                          waits=[(sT, sx2mm)])
            v_op(lambda: V.tensor_scalar(r_sx[:, :], r_sx[:, :], 1.0 / DM, None, ALU.mult))
            v_op(lambda: V.tensor_scalar(r_sx2[:, :], r_sx2[:, :], 1.0 / DM, None, ALU.mult))
            v_op(lambda: V.scalar_tensor_tensor(r_t[:, :], r_sx[:, :], -1.0,
                                                r_sx[:, :], ALU.mult, ALU.mult))
            v_op(lambda: V.tensor_tensor(r_sx2[:, :], r_sx2[:, :], r_t[:, :], ALU.add))
            vvar = v_op(lambda: V.tensor_scalar(r_sx2[:, :], r_sx2[:, :], 1e-5, None, ALU.add))
            s_op(lambda: S.activation(r_t[:, :], r_sx2[:, :], AF.Ln),
                 waits=[(sV, vvar)])
            rstd = s_op(lambda: S.activation(r_a[:, :], r_t[:, :], AF.Exp, scale=-0.5))
            rb = v_op(lambda: V.scalar_tensor_tensor(r_b[:, :], r_sx[:, :], -1.0,
                                                     r_a[:, :], ALU.mult, ALU.mult),
                      waits=[(sS, rstd)])
            wst = sy_dma(scr_stat[0:1, :], r_a[:, :],
                         waits=[(sV, rb), (sL, prev["stat_rep"])])
            wst = sy_dma(scr_stat[1:2, :], r_b[:, :])
            sy_dma(aB[:, :], scr_stat[0:1, :].broadcast_to([P, CH]), waits=[(sL, wst)])
            stat_rep = sy_dma(bB[:, :], scr_stat[1:2, :].broadcast_to([P, CH]))
            xn_done = 0
            for ct in range(NCT):
                v_op(lambda ct=ct: V.tensor_tensor(
                    outstg[:, (ct % 2) * CH:(ct % 2) * CH + CH],
                    x_ch[:, ct * CH:(ct + 1) * CH], aB[:, :], ALU.mult),
                    waits=([(sL, stat_rep), (sO, prog.count(sO))] if ct == 0 else []))
                xn_done = v_op(lambda ct=ct: V.tensor_tensor(
                    xn_bf[:, ct * CH:(ct + 1) * CH],
                    outstg[:, (ct % 2) * CH:(ct % 2) * CH + CH], bB[:, :], ALU.add))

            silu_done = {}
            z_done = {}
            convgrp_done = {}
            mmgrp_done = {}
            for mt in range(32):
                wts = [(sT, mmgrp_done.get(mt - 2, 0))]
                if k == 0 and mt < 2:
                    wts += [(sD, direct_w)]
                if k > 0 and mt < 2:
                    wts += [(sT, prev["opmm"].get(NCT - 1, 0))]
                wload = g_dma(wstream[:, (mt % 2) * NET * P:(mt % 2) * NET * P + NCT * P],
                              scr_inproj[mt, :, :], waits=wts)
                mmv = 0
                for kt in range(NCT):
                    wts = []
                    if kt == 0:
                        wts += [(sG, wload)]
                        if mt == 0:
                            wts += [(sV, xn_done)]
                        if mt - 2 >= 0:
                            if mt - 2 < NET:
                                wts += [(sV, prog.count(sV))]
                            else:
                                wts += [(sS, z_done.get(mt - 2 - NET, 0))]
                    mmv = t_op(lambda mt=mt, kt=kt: T.matmul(
                        mm_ps[:, (mt % 2) * CH:(mt % 2) * CH + CH],
                        wstream[:, (mt % 2) * NET * P + kt * P:(mt % 2) * NET * P + (kt + 1) * P],
                        xn_bf[:, kt * CH:(kt + 1) * CH],
                        start=(kt == 0), stop=(kt == NCT - 1)), waits=wts)
                mmgrp_done[mt] = mmv
                if mt < NET:
                    et = mt
                    base = et * 520
                    v_op(lambda et=et, base=base: V.tensor_copy(
                        xcry[:, base + 4:base + 8], halo[:, et * 4:(et + 1) * 4]),
                        waits=[(sV, prev["gate_done"].get(et, 0))])
                    v_op(lambda base=base, mt=mt: V.tensor_copy(
                        xcry[:, base + 8:base + 520],
                        mm_ps[:, (mt % 2) * CH:(mt % 2) * CH + CH]),
                        waits=[(sT, mmv)])
                    v_op(lambda et=et, mt=mt: V.tensor_copy(
                        halo[:, et * 4:(et + 1) * 4],
                        mm_ps[:, (mt % 2) * CH + 508:(mt % 2) * CH + 512]))
                    dgv = 0
                    for kk in range(4):
                        dgv = v_op(lambda et=et, kk=kk: V.tensor_scalar(
                            diag[:, (et % 2) * 4 * P + kk * P:(et % 2) * 4 * P + (kk + 1) * P],
                            eye_bf[:, :], convw[:, et * 4 + kk:et * 4 + kk + 1],
                            None, ALU.mult),
                            waits=([(sT, convgrp_done.get(et - 2, 0))] if kk == 0 else []))
                    cnv = 0
                    for kk in range(4):
                        wts = [(sV, dgv), (sS, silu_done.get(et - 2, 0))] if kk == 0 else []
                        cnv = t_op(lambda et=et, kk=kk, base=base: T.matmul(
                            conv_ps[:, (et % 2) * CH:(et % 2) * CH + CH],
                            diag[:, (et % 2) * 4 * P + kk * P:(et % 2) * 4 * P + (kk + 1) * P],
                            xcry[:, base + 5 + kk:base + 5 + kk + CH],
                            start=(kk == 0), stop=(kk == 3)), waits=wts)
                    convgrp_done[et] = cnv
                    silu_done[et] = s_op(lambda et=et: S.activation(
                        xc_bf[:, et * CH:(et + 1) * CH],
                        conv_ps[:, (et % 2) * CH:(et % 2) * CH + CH],
                        AF.Silu, bias=convb[:, et:et + 1]),
                        waits=[(sT, cnv),
                               (sV, prev["dxc_hist"].get(et, 0))])
                else:
                    et = mt - NET
                    z_done[et] = s_op(lambda et=et, mt=mt: S.activation(
                        z_all[:, et * CH:(et + 1) * CH],
                        mm_ps[:, (mt % 2) * CH:(mt % 2) * CH + CH],
                        AF.Silu),
                        waits=[(sT, mmv),
                               (sV, prev["gate_done"].get(et, 0))])

            mmfree = {0: ("S", z_done[14]), 1: ("S", z_done[15])}

            xpv = 0
            for kt in range(NET):
                wts = [(sS, silu_done[kt])]
                if kt == 0:
                    wts += [(sS, silu_done[NET - 1])]
                xpv = t_op(lambda kt=kt: T.matmul(
                    conv_ps[0:96, 0:CH],
                    xprojT_bf[:, kt * 96:(kt + 1) * 96],
                    xc_bf[:, kt * CH:(kt + 1) * CH],
                    start=(kt == 0), stop=(kt == NET - 1)), waits=wts)
            draw_done = s_op(lambda: S.activation(drawT[:, :], conv_ps[0:DTR, 0:CH],
                                                  AF.Copy),
                             waits=[(sT, xpv), (sT, prev["dtmm_last"])])
            bc_done = v_op(lambda: V.tensor_copy(bc_sb[DTR:96, :], conv_ps[DTR:96, 0:CH]),
                           waits=[(sT, xpv)])
            bc_w = sy_dma(scr_bc[:, :], bc_sb[DTR:96, :],
                          waits=[(sV, bc_done), (sL, prev["rep_sy"]),
                                 (sG, prev["rep_g"])])
            rep_sy = 0
            rep_g = 0
            for n in range(NST):
                wts = ([(sL, bc_w), w(prev["bcb_free"])] if n == 0 else [])
                rep_sy = sy_dma(bcb[:, n * CH:(n + 1) * CH],
                                scr_bc[n:n + 1, :].broadcast_to([P, CH]), waits=wts)
                rep_g = g_dma(bcb[:, (16 + n) * CH:(17 + n) * CH],
                              scr_bc[16 + n:17 + n, :].broadcast_to([P, CH]), waits=wts)

            dtmm_last = 0
            for et in range(NET):
                etg = k * NET + et
                par = et % 2
                dtmm = t_op(lambda et=et, par=par: T.matmul(
                    mm_ps[:, par * CH:par * CH + CH],
                    dt_wT_bf[:, et * P:(et + 1) * P],
                    drawT[:, :], start=True, stop=True),
                    waits=[w(mmfree[par]), (sS, draw_done)])
                dtmm_last = dtmm
                s_op(lambda et=et, par=par: S.activation(
                    stage[:, par * CH:par * CH + CH],
                    mm_ps[:, par * CH:par * CH + CH],
                    AF.Exp, bias=dtb[:, et:et + 1]),
                    waits=[(sT, dtmm)])
                sp = s_op(lambda par=par: S.activation(
                    delta2[:, par * CH:par * CH + CH],
                    stage[:, par * CH:par * CH + CH],
                    AF.Ln, bias=1.0),
                    waits=[(sV, v2_hist.get(etg - 2, 0))])
                mmfree[par] = ("S", sp)
                v2d = v_op(lambda par=par, et=et: V.tensor_tensor(
                    v2[:, par * CH:par * CH + CH],
                    delta2[:, par * CH:par * CH + CH],
                    xc_bf[:, et * CH:(et + 1) * CH], ALU.mult),
                    waits=[(sS, sp)])
                v2_hist[etg] = v2d

                for g in range(NG):
                    gg = etg * NG + g
                    gp = gg % 2
                    uw = []
                    if et == 0 and g == 0:
                        uw = [(sL, rep_sy), (sG, rep_g)]
                    v_op(lambda par=par, g=g, gp=gp: V.tensor_tensor(
                        out=AP(u_roll, gp * GN * CH, [[2 * GN * CH, P], [CH, GN], [1, CH]]),
                        in0=AP(v2, par * CH, [[2 * CH, P], [0, GN], [1, CH]]),
                        in1=AP(bcb, g * GN * CH, [[32 * CH, P], [CH, GN], [1, CH]]),
                        op=ALU.mult), waits=uw)
                    for n4 in range(GN):
                        n = g * GN + n4
                        i = et * NST + n
                        gi = k * 256 + i
                        ew = [(sV, scan_hist.get(gi - 2, 0))]
                        if gi == 0:
                            ew += [(sV, nega_done)]
                        expd = s_op(lambda par=par, i=i: S.activation(
                            dA[:, (i % 2) * CH:(i % 2) * CH + CH],
                            delta2[:, par * CH:par * CH + CH],
                            AF.Exp, scale=negA[:, i:i + 1]), waits=ew)
                        init = 0.0 if k == 0 else states[:, i:i + 1]
                        sw = [(sS, expd)]
                        if n4 == 0:
                            sw += [(sT, stmm_hist.get(gg - 2, 0))]
                        scan_hist[gi] = v_op(lambda gp=gp, n4=n4, i=i, init=init:
                                             V.tensor_tensor_scan(
                            hs_roll[:, gp * GN * CH + n4 * CH:gp * GN * CH + (n4 + 1) * CH],
                            dA[:, (i % 2) * CH:(i % 2) * CH + CH],
                            u_roll[:, gp * GN * CH + n4 * CH:gp * GN * CH + (n4 + 1) * CH],
                            init, ALU.mult, ALU.add), waits=sw)
                    stw = [(sV, scan_hist[k * 256 + et * NST + g * GN + GN - 1])]
                    if et == 0 and g == 0:
                        stw += [(sV, prev["states_copy"])]
                    stmm_hist[gg] = t_op(lambda et=et, g=g, gp=gp: T.matmul(
                        st_ps[:, et * NST + g * GN:et * NST + (g + 1) * GN],
                        eye_bf[:, :],
                        AP(hs_roll, gp * GN * CH + CH - 1, [[2 * GN * CH, P], [CH, GN]]),
                        start=True, stop=True), waits=stw)
                    tw = [(sT, foldl_hist.get(gg - 2, 0))]
                    if et == 0 and g == 0:
                        tw += [(sG, rep_g)]
                    tmpd = v_op(lambda g=g, gp=gp: V.tensor_tensor(
                        out=AP(tmp_roll, gp * GN * CH, [[2 * GN * CH, P], [CH, GN], [1, CH]]),
                        in0=AP(hs_roll, gp * GN * CH, [[2 * GN * CH, P], [CH, GN], [1, CH]]),
                        in1=AP(bcb, (16 + g * GN) * CH, [[32 * CH, P], [CH, GN], [1, CH]]),
                        op=ALU.mult), waits=tw)
                    fold = 0
                    for j in range(GN):
                        fw = []
                        if j == 0:
                            fw += [(sV, tmpd)]
                            if g == 0:
                                fw += [(sV, gate_hist.get(etg - 2, 0))]
                        fold = t_op(lambda par=par, gp=gp, j=j, g=g: T.matmul(
                            y_ps[:, par * CH:par * CH + CH],
                            eye_bf[:, :],
                            tmp_roll[:, gp * GN * CH + j * CH:gp * GN * CH + (j + 1) * CH],
                            start=(g == 0 and j == 0), stop=False), waits=fw)
                    foldl_hist[gg] = fold

                dxcv = v_op(lambda et=et, par=par: V.tensor_scalar(
                    dxc[:, par * CH:par * CH + CH],
                    xc_bf[:, et * CH:(et + 1) * CH],
                    dparm[:, et:et + 1], None, ALU.mult),
                    waits=[(sT, f17_hist.get(etg - 2, 0))])
                dxc_hist[etg] = dxcv
                f17 = t_op(lambda par=par: T.matmul(
                    y_ps[:, par * CH:par * CH + CH],
                    eye_bf[:, :],
                    dxc[:, par * CH:par * CH + CH],
                    start=False, stop=True),
                    waits=[(sV, dxcv)])
                f17_hist[etg] = f17
                gate_hist[etg] = v_op(lambda et=et, par=par: V.tensor_tensor(
                    xcry[:, et * CH:(et + 1) * CH],
                    y_ps[:, par * CH:par * CH + CH],
                    z_all[:, et * CH:(et + 1) * CH], ALU.mult),
                    waits=[(sT, f17), (sS, z_done[et])])
            states_copy = v_op(lambda: V.tensor_copy(states[:, :], st_ps[:, :]),
                               waits=[(sT, stmm_hist[(k * NET + NET - 1) * NG + NG - 1])])

            if debug:
                prog.emit("sync", lambda k=k: nc.sync.dma_start(out=dbg_xn[k], in_=xn_bf[:, :]),
                          waits=[(sV, xn_done)], incs=[(sO, 16)])
                prog.emit("sync", lambda k=k: nc.sync.dma_start(out=dbg_xc[k], in_=xc_bf[:, :]),
                          waits=[(sS, silu_done[NET - 1])], incs=[(sO, 16)])
                prog.emit("sync", lambda k=k: nc.sync.dma_start(out=dbg_bcb[k], in_=bcb[:, :]),
                          waits=[(sL, rep_sy), (sG, rep_g)], incs=[(sO, 16)])
                prog.emit("sync", lambda k=k: nc.sync.dma_start(out=dbg_states[k], in_=states[:, :]),
                          waits=[(sV, states_copy)], incs=[(sO, 16)])
                prog.emit("sync", lambda k=k: nc.sync.dma_start(out=dbg_ygate[k], in_=xcry[:, 0:NET * CH]),
                          waits=[(sV, gate_hist[k * NET + NET - 1])], incs=[(sO, 16)])
                prog.emit("sync", lambda k=k: nc.sync.dma_start(out=dbg_z[k], in_=z_all[:, 0:2 * CH]),
                          waits=[(sS, z_done[NET - 1])], incs=[(sO, 16)])

            ffn_add = {}
            ffn_th = {}
            for ct in range(NCT):
                mmv = 0
                for kt in range(NCT):
                    wts = []
                    if kt == 0:
                        wts += [w(mmfree[ct % 2])]
                        if ct == 0:
                            wts += [(sV, xn_done)]
                    mmv = t_op(lambda ct=ct, kt=kt: T.matmul(
                        mm_ps[:, (ct % 2) * CH:(ct % 2) * CH + CH],
                        wp_bf[:, kt * DM + ct * P:kt * DM + (ct + 1) * P],
                        xn_bf[:, kt * CH:(kt + 1) * CH],
                        start=(kt == 0), stop=(kt == NCT - 1)), waits=wts)
                ffn_th[ct] = s_op(lambda ct=ct: S.activation(
                    stage[:, (ct % 2) * CH:(ct % 2) * CH + CH],
                    mm_ps[:, (ct % 2) * CH:(ct % 2) * CH + CH],
                    AF.Tanh, bias=fbias[:, ct:ct + 1]),
                    waits=[(sT, mmv)] + ([(sV, ffn_add[ct - 2])] if ct >= 2 else []))
                mmfree[ct % 2] = ("S", ffn_th[ct])
                ffn_add[ct] = v_op(lambda ct=ct: V.tensor_tensor(
                    x_ch[:, ct * CH:(ct + 1) * CH],
                    x_ch[:, ct * CH:(ct + 1) * CH],
                    stage[:, (ct % 2) * CH:(ct % 2) * CH + CH], ALU.add),
                    waits=[(sS, ffn_th[ct])])

            outstage_done = 0
            opmm = {}
            st_vals = dict(prev.get("st_vals", {}))
            for mt in range(NCT):
                wts = [(sT, opmm.get(mt - 1, 0))]
                if mt == 0:
                    wts += [(sT, mmgrp_done[31])]
                    if k == 0:
                        wts += [(sD, direct_w)]
                wload = g_dma(wstream[:, 0:NET * P],
                              scr_outproj[mt, :, :], waits=wts)
                mmv = 0
                for kt in range(NET):
                    wts = []
                    if kt == 0:
                        wts += [(sG, wload), (sV, gate_hist[k * NET + NET - 1]),
                                w(mmfree[mt % 2])]
                        if mt >= 2:
                            wts += [(sV, prog.count(sV))]
                    mmv = t_op(lambda mt=mt, kt=kt: T.matmul(
                        mm_ps[:, (mt % 2) * CH:(mt % 2) * CH + CH],
                        wstream[:, kt * P:(kt + 1) * P],
                        xcry[:, kt * CH:(kt + 1) * CH],
                        start=(kt == 0), stop=(kt == NET - 1)), waits=wts)
                opmm[mt] = mmv
                outstage_done = v_op(lambda mt=mt: V.tensor_tensor(
                    outstg[:, (mt % 2) * CH:(mt % 2) * CH + CH],
                    mm_ps[:, (mt % 2) * CH:(mt % 2) * CH + CH],
                    x_ch[:, mt * CH:(mt + 1) * CH], ALU.add),
                    waits=[(sT, mmv), (sV, ffn_add[mt]),
                           (sO, st_vals.get(mt - 2, st_vals.get(6 + mt, 0)))])
                mmfree[mt % 2] = ("V", outstage_done)
                st_vals[mt] = prog.emit("sync",
                          lambda mt=mt, c0=c0: nc.sync.dma_start(
                              out=outT[mt * P:(mt + 1) * P, c0:c0 + CH],
                              in_=outstg[:, (mt % 2) * CH:(mt % 2) * CH + CH]),
                          waits=[(sV, outstage_done)], incs=[(sO, 16)])

            prev = dict(outstage_done=outstage_done,
                        states_copy=states_copy, gate_done=gate_hist_slice(gate_hist, k),
                        opmm=opmm,
                        rep_sy=rep_sy, rep_g=rep_g,
                        stat_rep=stat_rep, st_vals=st_vals,
                        ffn_add_last=ffn_add[NCT - 1],
                        bcb_free=("V", prog.count(sV)),
                        dtmm_last=dtmm_last,
                        dxc_hist=dxc_hist_slice(dxc_hist, k))

        total_stores = prog.count(sO)
        if enddump:
            allv, alls, allt, allg = prog.count(sV), prog.count(sS), prog.count(sT), prog.count(sG)
            srcs = dict(xch=x_ch[:, :],
                        negA=negA[:, :], dtw=dt_wT_bf[:, :], xproj=xprojT_bf[:, :],
                        wp=wp_bf[:, :], bcb=bcb[:, :], xn=xn_bf[:, :], xc=xc_bf[:, :],
                        zall=z_all[:, :], ygate=xcry[:, 0:NET * CH], states=states[:, :],
                        convw=convw[:, :], dtb=dtb[:, :], aB=aB[:, :], bB=bB[:, :],
                        drawT=drawT[:, :], eye=eye_bf[:, :])
            first = True
            for nm, src in srcs.items():
                wts = [(sO, total_stores), (sV, allv), (sS, alls), (sT, allt), (sG, allg)] if first else []
                first = False
                prog.emit("sync", lambda d=ed[nm], s=src: nc.sync.dma_start(out=d, in_=s),
                          waits=wts, incs=[(sO, 16)])
            total_stores = prog.count(sO)
        prog.emit("sync", lambda: nc.sync.nop(), waits=[(sO, total_stores)])

        with nc.Block() as block:
            @block.sync
            def _(eng):
                prog.replay("sync", eng)

            @block.vector
            def _(eng):
                prog.replay("vector", eng)

            @block.scalar
            def _(eng):
                prog.replay("scalar", eng)

            @block.tensor
            def _(eng):
                prog.replay("tensor", eng)

            @block.gpsimd
            def _(eng):
                prog.replay("gpsimd", eng)

    return nc


def gate_hist_slice(gate_hist, k):
    return {et: gate_hist[k * NET + et] for et in range(NET)}


def dxc_hist_slice(dxc_hist, k):
    return {et: dxc_hist[k * NET + et] for et in range(NET)}


_NC_CACHE = None


def _get_nc():
    global _NC_CACHE
    if _NC_CACHE is None:
        _NC_CACHE = build_program()
    return _NC_CACHE


def kernel(L0_attn_x, L0_attn_in_proj_w, L0_attn_conv_w, L0_attn_conv_b,
           L0_attn_x_proj_w, L0_attn_dt_w, L0_attn_dt_b, L0_attn_A_log,
           L0_attn_D_param, L0_attn_out_proj_w, L0_ffn_weight,
           L0_ffn_aux_weight1, L0_ffn_aux_weight2, L0_ffn_bias):
    x = np.asarray(L0_attn_x, dtype=np.float32)
    B = x.shape[0]
    f32 = lambda a: np.ascontiguousarray(np.asarray(a, dtype=np.float32))

    inprojT = f32(np.asarray(L0_attn_in_proj_w).T)
    outprojT = f32(np.asarray(L0_attn_out_proj_w).T)
    xprojT = f32(np.asarray(L0_attn_x_proj_w).T)
    dt_wT = f32(np.asarray(L0_attn_dt_w).T)
    w1T = f32(np.asarray(L0_ffn_weight).T)
    wa1T = f32(np.asarray(L0_ffn_aux_weight1).T)
    wa2 = f32(L0_ffn_aux_weight2)

    def chan_r(vec, nt):
        return f32(np.asarray(vec).reshape(nt, P).T)

    convw_r = f32(np.asarray(L0_attn_conv_w).reshape(NET, P, 4)
                  .transpose(1, 0, 2).reshape(P, 4 * NET))
    convb_r = chan_r(L0_attn_conv_b, NET)
    dtb_r = chan_r(L0_attn_dt_b, NET)
    alog_r = f32(np.asarray(L0_attn_A_log).reshape(NET, P, NST)
                 .transpose(1, 0, 2).reshape(P, NST * NET))
    d_r = chan_r(L0_attn_D_param, NET)
    fbias_r = chan_r(L0_ffn_bias, NCT)
    eye = np.eye(P, dtype=BF)
    ones = np.ones((P, 1), dtype=np.float32)

    shared = dict(inprojT=inprojT, outprojT=outprojT, xprojT=xprojT,
                  dt_wT=dt_wT, w1T=w1T, wa1T=wa1T, wa2=wa2,
                  convw_r=convw_r, convb_r=convb_r, dtb_r=dtb_r,
                  alog_r=alog_r, d_r=d_r, fbias_r=fbias_r,
                  eye_in=eye, ones_in=ones)

    in_maps = []
    for b in range(B):
        m = dict(shared)
        m["xT"] = np.ascontiguousarray(x[b].T)
        in_maps.append(m)

    nc = _get_nc()
    res = run_bass_kernel_spmd(nc, in_maps, core_ids=list(range(B)))
    out = np.stack([np.asarray(r["outT"]).T for r in res.results], axis=0)
    return out.astype(np.float32)
